# version 23
# baseline (speedup 1.0000x reference)
import math
from contextlib import ExitStack

import numpy as np
import ml_dtypes

import concourse.bass as bass
import concourse.mybir as mybir
from concourse.bass_utils import run_bass_kernel_spmd

F32 = mybir.dt.float32
BF16 = mybir.dt.bfloat16
AF = mybir.ActivationFunctionType
ALU = mybir.AluOpType
AX = mybir.AxisListType

f32 = np.float32
bf16 = ml_dtypes.bfloat16

FULL = dict(S=2048, D=2048, H=16, INTER=2048)


class Sem:
    def __init__(self, nc, es, name):
        self.h = es.enter_context(nc.semaphore(name))
        self.n = 0


def build_program(S, D, H, INTER):
    HPC = H // 2
    NC = D // 128
    NQ = max(1, S // 512)
    QW = S // NQ
    NKT = S // 128
    KPR = NKT // NQ
    S2 = S // 2
    NF = max(1, S2 // 512)
    FW = S2 // NF
    TT2 = S2 // 128
    NIT = INTER // 128
    NCI = INTER // 128
    VG = min(2, HPC)
    NB = HPC // VG
    D2 = D // 2
    NER = max(1, D // 512)
    EW = D // NER
    EWH = min(256, D)
    NERH = D // EWH
    CB = min(1024, S)
    XH = S // CB
    CB2 = min(1024, S2)
    XH2 = S2 // CB2
    NST = max(1, S // 512)
    SW = S // NST
    NST2 = max(1, S2 // 512)
    SW2 = S2 // NST2
    QT = S // 128
    SCALE = float(1.0 / math.sqrt(128.0))
    CSHIFT = -16.0
    EPS = 1e-5
    recipD = 1.0 / D

    nc = bass.Bass("TRN2", target_bir_lowering=False)

    dI = lambda n, sh, dt=F32: nc.dram_tensor(n, sh, dt, kind="ExternalInput").ap()
    xT = dI("xT", [D, S])
    xf = dI("xf", [D, S2])
    xr = dI("xr", [S, D2])
    qw = dI("qw", [HPC, 128, NC * 128])
    kw = dI("kw", [HPC, 128, NC * 128])
    vw = dI("vw", [NB, 128, NC, VG * 128])
    gw = dI("gw", [128, 128])
    gb = dI("gb", [128, 1])
    ow = dI("ow", [HPC * 128, D])
    w1 = dI("w1", [NIT, 128, NC * 128])
    w3 = dI("w3", [NIT, 128, NCI * 128])
    w2 = dI("w2", [NERH, 128, NCI, EWH])
    wr = dI("wr", [128, NC * 8])
    mk = dI("mk", [128, 4 * 512], BF16)
    sel_a = dI("sel_a", [8, 1])
    sel_b = dI("sel_b", [8, 1])

    out = nc.dram_tensor("out", [S, D2], F32, kind="ExternalOutput").ap()

    owbf = nc.dram_tensor("owbf", [NER, 128, HPC * EW], BF16)
    w2bf = nc.dram_tensor("w2bf", [NERH, 128, NCI * EWH], BF16)
    op_dram = nc.dram_tensor("op_dram", [2 * S, D2], BF16)
    rs_dram = nc.dram_tensor("rs_dram", [S, D2], BF16)
    o8_dram = nc.dram_tensor("o8_dram", [8, D], F32)
    o8r_dram = nc.dram_tensor("o8r_dram", [8, D], F32)
    o8g_dram = nc.dram_tensor("o8g_dram", [64, D], F32)

    with ExitStack() as es:
        sbuf = lambda n, sh, dt: es.enter_context(nc.sbuf_tensor(n, sh, dt))

        RA = sbuf("RA", [128, NC * S], BF16)
        xnT = lambda c: RA[:, c * S:(c + 1) * S]
        HALF = NC * S // 2
        xnF = lambda c: RA[:, c * S2:(c + 1) * S2]
        h1sb = lambda i: RA[:, HALF + i * S2: HALF + (i + 1) * S2]
        h3sb = lambda t: RA[:, HALF + t * D: HALF + (t + 1) * D]

        RB = sbuf("RB", [128, HPC * S], BF16)
        hob = lambda hh: RB[:, hh * S:(hh + 1) * S]
        h2sb = lambda i: RB[:, i * S2:(i + 1) * S2]

        NSLOT = 4
        RC = sbuf("RC", [128, NSLOT * 2 * QW], BF16)
        probsT = RC

        RD = sbuf("RD", [128, NKT * VG * 128], BF16)
        vsb = lambda tt: RD[:, tt * VG * 128:(tt + 1) * VG * 128]

        assert HPC >= 2, "need >=2 heads per core for buffer aliasing"
        vwb = sbuf("vwb", [128, NC * VG * 128], BF16)
        qwb = vwb[:, 0:NC * 128]
        kwb = vwb[:, NC * 128:2 * NC * 128]
        w13b = [sbuf("w13b0", [128, NC * 128], BF16),
                sbuf("w13b1", [128, NC * 128], BF16)]
        RDW = NKT * VG * 128

        def w2b(ci):
            off = ci * EW
            if off < RDW:
                return RD[:, off:off + EW]
            return vwb[:, off - RDW:off - RDW + EW]
        gwb = sbuf("gwb", [128, 128], BF16)
        owx = [sbuf("owx0", [128, 512], BF16), sbuf("owx1", [128, 512], BF16)]
        wrb = sbuf("wrb", [128, NC * 8], BF16)

        WS = [sbuf("WS0", [128, 1024], F32), sbuf("WS1", [128, 1024], F32)]
        XS = [sbuf("XS0", [128, CB], F32), sbuf("XS1", [128, CB], F32)]

        qsb = sbuf("qsb", [128, S], BF16)
        ksb = sbuf("ksb", [128, S], BF16)
        tgb = sbuf("tgb", [128, S], BF16)

        scrD = sbuf("scrD", [128, 2 * QW], F32)
        Dsb = scrD[:, 0:QW]
        Dinv = scrD[:, QW:2 * QW]
        nmr2b2 = sbuf("nmr2b2", [128, 2 * S2], F32)
        nm2b_sb = nmr2b2[:, 0:S2]
        r2b_sb = nmr2b2[:, S2:2 * S2]
        outs2 = sbuf("outs2", [128, 2 * D2], F32)
        outsb = None
        OPW = min(4, QT) * EW
        opst = [sbuf("opst0", [128, OPW], BF16),
                sbuf("opst1", [128, OPW], BF16)]
        rssb = [sbuf("rssb0", [128, D2], BF16), sbuf("rssb1", [128, D2], BF16)]

        assert 2 * QW >= D2
        outsb = [outs2[:, 0:D2], outs2[:, D2:2 * D2], scrD[:, 0:D2]]
        m_bf = opst[0][0:1, 0:S]
        stmp = opst[1][0:1, 0:S]
        m2_bf = rssb[0][0:1, 0:S2]
        e2f = outs2[0:1, 0:S]
        e2f2 = outs2[0:1, 0:S2]
        rho_sb = sbuf("rho_sb", [1, QW], F32)

        rt_sb = sbuf("rt_sb", [128, TT2 * 8], F32)
        g_sb = sbuf("g_sb", [128, TT2 * 8], BF16)
        r_m1 = sbuf("r_m1", [128, TT2], F32)
        r_m2 = sbuf("r_m2", [128, TT2], F32)
        r_lo = sbuf("r_lo", [128, TT2 * 8], F32)
        r_kp = sbuf("r_kp", [128, TT2 * 8], F32)
        r_dn = sbuf("r_dn", [128, TT2], F32)

        onesD = sbuf("onesD", [128, 1], BF16)
        ones1 = sbuf("ones1", [1, 128], F32)
        onesC = sbuf("onesC", [128, 1], BF16)
        ones1b = sbuf("ones1b", [1, 128], BF16)
        mask_sb = sbuf("mask_sb", [128, 4 * 512], BF16)
        gbn = sbuf("gbn", [128, 1], F32)
        gbs = sbuf("gbs", [128, 1], F32)
        sela = sbuf("sela", [8, 1], F32)
        selb = sbuf("selb", [8, 1], F32)
        o8sb = nmr2b2[0:8, 0:D]
        fad = scrD[0:8, 0:D2]

        psA = es.enter_context(nc.psum_tensor("psA", [128, 2048], F32))
        psB = es.enter_context(nc.psum_tensor("psB", [128, 2048], F32))

        SM = lambda name: Sem(nc, es, name)
        sX = [SM("sX0"), SM("sX1")]
        sXFr = [SM("sXF0"), SM("sXF1")]
        sW = [SM("sW0"), SM("sW1")]
        sWFr = [SM("sWF0"), SM("sWF1")]
        sInit = SM("sInit")
        sSq = SM("sSq")
        sCv = SM("sCv")
        sStP = SM("sStP")
        sStE = SM("sStE")
        sMath = SM("sMath")
        sBc = SM("sBc")
        sXn = SM("sXn")
        sXn2 = SM("sXn2")
        sM2c = SM("sM2c")
        sWc = SM("sWc")
        sQm = SM("sQm")
        sQe = SM("sQe")
        sKm = SM("sKm")
        sKe = SM("sKe")
        sVm = SM("sVm")
        sVe = SM("sVe")
        sGm = SM("sGm")
        sTg = SM("sTg")
        sSc = SM("sSc")
        sEx = SM("sEx")
        sMk = SM("sMk")
        sAv = SM("sAv")
        sRh = SM("sRh")
        sRhE = SM("sRhE")
        sRhB = SM("sRhB")
        sDm = SM("sDm")
        sDi = SM("sDi")
        sHo = SM("sHo")
        sOp = SM("sOp")
        sOpE = SM("sOpE")
        sOpS = SM("sOpS")
        sRS = SM("sRS")
        sRt = SM("sRt")
        sRtE = SM("sRtE")
        sTp = SM("sTp")
        sH1 = SM("sH1")
        sH1E = SM("sH1E")
        sH2 = SM("sH2")
        sH2E = SM("sH2E")
        sH3 = SM("sH3")
        sH3E = SM("sH3E")
        sO8 = SM("sO8")
        sO8E = SM("sO8E")
        sO8S = SM("sO8S")
        sAR = SM("sAR")
        sFin = SM("sFin")
        sFlb = SM("sFlb")
        sSt = [SM("sSt0"), SM("sSt1"), SM("sSt2")]
        sCst = SM("sCst")
        sWcst = SM("sWcst")
        sRsL = [SM("sRsL0"), SM("sRsL1")]
        sArL = SM("sArL")
        sOpSp = [SM("sOpS0"), SM("sOpS1")]
        sMs = SM("sMs")
        sVch = SM("sVch")
        sPgr = SM("sPgr")
        sPcv = SM("sPcv")
        sPcS = [SM("sPcS0"), SM("sPcS1"), SM("sPcS2"), SM("sPcS3")]
        sOwL = [SM("sOwL0"), SM("sOwL1")]
        sWL = [SM("sWL0"), SM("sWL1")]
        sW2L = [SM("sW2L0"), SM("sW2L1")]
        sAch = SM("sAch")

        prog = {k: [] for k in ("sync", "gpsimd", "tensor", "vector", "scalar")}

        def em(eng, fn):
            prog[eng].append(fn)

        def wt(eng, sem, v):
            if v > 0:
                em(eng, lambda e, s=sem.h, v=v: e.wait_ge(s, v))

        def mm(dst, lhsT, rhs, start, stop, sem=None, waits=(), sem2=None):
            for (s, v) in waits:
                wt("tensor", s, v)

            def f(e, dst=dst, lhsT=lhsT, rhs=rhs, start=start, stop=stop,
                  sem=sem, sem2=sem2):
                i = e.matmul(dst, lhsT, rhs, start=start, stop=stop,
                             skip_group_check=True)
                if sem is not None:
                    i = i.then_inc(sem.h, 1)
                if sem2 is not None:
                    i.then_inc(sem2, 1)
            em("tensor", f)
            if sem is not None:
                sem.n += 1

        def act(out_, in_, func, sem=None, waits=(), bias=None, scl=None,
                sem2=None):
            for (s, v) in waits:
                wt("scalar", s, v)

            def f(e, out_=out_, in_=in_, func=func, sem=sem, bias=bias,
                  scl=scl, sem2=sem2):
                kw_ = {}
                if bias is not None:
                    kw_["bias"] = f32(bias) if isinstance(bias, float) else bias
                if scl is not None:
                    kw_["scale"] = f32(scl)
                i = e.activation(out_, in_, func, **kw_)
                if sem is not None:
                    i = i.then_inc(sem.h, 1)
                if sem2 is not None:
                    i.then_inc(sem2, 1)
            em("scalar", f)
            if sem is not None:
                sem.n += 1

        def vop(name, *args, sem=None, waits=(), sem2=None, **kwargs):
            for (s, v) in waits:
                wt("vector", s, v)

            def f(e, name=name, args=args, sem=sem, kwargs=kwargs, sem2=sem2):
                i = getattr(e, name)(*args, **kwargs)
                if sem is not None:
                    i = i.then_inc(sem.h, 1)
                if sem2 is not None:
                    i.then_inc(sem2, 1)
            em("vector", f)
            if sem is not None:
                sem.n += 1

        def gop(name, *args, sem=None, waits=(), **kwargs):
            for (s, v) in waits:
                wt("gpsimd", s, v)

            def f(e, name=name, args=args, sem=sem, kwargs=kwargs):
                i = getattr(e, name)(*args, **kwargs)
                if sem is not None:
                    i.then_inc(sem.h, 1)
            em("gpsimd", f)
            if sem is not None:
                sem.n += 1

        def dma(eng, out_, in_, sem, waits=(), accum=False):
            for (s, v) in waits:
                wt(eng, s, v)

            def f(e, out_=out_, in_=in_, sem=sem, accum=accum):
                kw_ = dict(accum_op=ALU.add) if accum else {}
                e.dma_start(out=out_, in_=in_, **kw_).then_inc(sem.h, 16)
            em(eng, f)
            sem.n += 16

        def collective(kind, op, groups, in_ap, out_ap, sem, waits=()):
            for (s, v) in waits:
                wt("gpsimd", s, v)

            def f(e, kind=kind, op=op, groups=groups, in_ap=in_ap,
                  out_ap=out_ap, sem=sem):
                e.collective_compute(kind, op, replica_groups=groups,
                                     ins=[in_ap], outs=[out_ap]).then_inc(sem.h)
            em("gpsimd", f)
            sem.n += 1

        def pe_milestone():
            em("tensor", lambda e, s=sMs.h: e.nop().then_inc(s, 1))
            sMs.n += 1
            return sMs.n

        class Stream:
            def __init__(self, queue, bufs, sLoad, sFree):
                self.q, self.bufs = queue, bufs
                self.sL = sLoad
                self.cnt = [0, 0]
                self.pending = [[], []]

            def load(self, src, cols, rows=128):
                p = (self.cnt[0] + self.cnt[1]) % 2
                w = self.pending[p]
                self.pending[p] = []
                dma(self.q, self.bufs[p][0:rows, 0:cols], src, self.sL[p], waits=w)
                self.cnt[p] += 1
                return self.bufs[p], p, self.sL[p].n

            def set_free(self, p, gates):
                self.pending[p] = list(gates)

        xs = Stream("sync", XS, sX, sXFr)
        ws = Stream("gpsimd", WS, sW, sWFr)
        WSC = 1024

        pconv_units = []
        for hhh in range(HPC):
            for er_ in range(NER):
                pconv_units.append(
                    (ow[hhh * 128:(hhh + 1) * 128, er_ * EW:(er_ + 1) * EW],
                     owbf[er_][:, hhh * EW:(hhh + 1) * EW], EW))
        for er in range(NERH):
            tot = NCI * EWH
            for off in range(0, tot, WSC):
                wdt = min(WSC, tot - off)
                pconv_units.append(
                    (w2[er].rearrange("p c e -> p (c e)")[:, off:off + wdt],
                     w2bf[er][:, off:off + wdt], wdt))
        pconv_state = [0]

        def emit_pconv(n):
            for _ in range(n):
                k = pconv_state[0]
                if k >= len(pconv_units):
                    return
                pconv_state[0] = k + 1
                srcap, dstap, wdt = pconv_units[k]
                buf, p, lv = xs.load(srcap, wdt)
                pp = k % 4
                slot = opst[pp // 2][:, (pp % 2) * 1024:(pp % 2) * 1024 + wdt]
                wv = [(sX[p], lv)]
                if k >= 4:
                    wv.append((sPcS[pp], 16 * (k // 4)))
                else:
                    wv.append((sMath, p2_state['nmr2']))
                gop("tensor_copy", slot, buf[:, 0:wdt], sem=sPcv, waits=wv)
                xs.set_free(p, [(sPcv, sPcv.n)])
                dma("sync", dstap, slot, sPcS[pp],
                    waits=[(sPcv, sPcv.n)])

        def pconv_done_gate():
            assert pconv_state[0] == len(pconv_units), \
                f"pconv incomplete {pconv_state[0]}/{len(pconv_units)}"
            return [(s, s.n) for s in sPcS]

        def load_conv(dst_fn, src_fn, total, first_waits):
            for off in range(0, total, WSC):
                wdt = min(WSC, total - off)
                buf, p, lv = ws.load(src_fn(off, wdt), wdt)
                vop("tensor_copy", dst_fn(off, wdt), buf[:, 0:wdt], sem=sWc,
                    waits=[(sW[p], lv)] +
                          (list(first_waits) if off == 0 else []))
                ws.set_free(p, [(sWc, sWc.n)])
            return sWc.n

        XCACHE_OK = NC * CB <= HPC * S
        xcache = lambda c: RB[:, c * CB:(c + 1) * CB]
        xcb_scratch = [qsb[:, 0:CB], ksb[:, 0:CB]]
        xsq_buf = [tgb[:, 0:CB], RD[:, 0:CB]]

        dma("sync", mask_sb[:, :], mk, sCst)
        dma("sync", gbs[:, :], gb, sCst)
        dma("sync", sela[:, :], sel_a, sCst)
        dma("sync", selb[:, :], sel_b, sCst)
        cl0 = sCst.n
        dma("gpsimd", WS[0][:, 0:128], gw, sWcst)
        dma("gpsimd", WS[0][:, 128:128 + NC * 8], wr, sWcst)
        cl1 = sWcst.n
        ws.cnt[0] = 1

        vop("memset", onesD[:, :], recipD)
        vop("memset", ones1[:, :], 1.0)
        vop("memset", onesC[:, :], 1.0)
        vop("memset", ones1b[:, :], 1.0)
        vop("tensor_scalar", gbn[:, :], gbs[:, :], -1.0, None, op0=ALU.mult,
            waits=[(sCst, cl0)])
        vop("tensor_copy", gwb[:, :], WS[0][:, 0:128], waits=[(sWcst, cl1)])
        vop("tensor_copy", wrb[:, :], WS[0][:, 128:128 + NC * 8], sem=sInit)
        ws.set_free(0, [(sInit, sInit.n)])

        vw_flat0 = vw[0].rearrange("p c e -> p (c e)")
        hoisted_vconv = load_conv(
            lambda off, wdt: vwb[:, off:off + wdt],
            lambda off, wdt: vw_flat0[:, off:off + wdt],
            VG * NC * 128, [])

        p1_last = {}

        def pass1_chunk(src_ap, c, bcols, base, CBx, SWx, gate,
                        mt=None, et=None, ebase=None, cache=None,
                        cache_gate=(), sq=None):
            mt = psA if mt is None else mt
            et = psB if et is None else et
            ebase = base if ebase is None else ebase
            buf, p, lv = xs.load(src_ap[c * 128:(c + 1) * 128, bcols], CBx)
            xcb = cache if cache is not None else xcb_scratch[p]
            xsq = sq if sq is not None else xsq_buf[p]
            w_prev = ([(sStP, p1_last[p])] if p in p1_last else [])
            vop("tensor_copy", xcb[:, 0:CBx], buf[:, 0:CBx], sem=sCv,
                waits=[(sX[p], lv)] + w_prev + list(cache_gate))
            act(xsq[:, 0:CBx], buf[:, 0:CBx], AF.Square, sem=sSq,
                waits=[(sX[p], lv)] + w_prev)
            xs.set_free(p, [(sCv, sCv.n), (sSq, sSq.n)])
            cv_v, sq_v = sCv.n, sSq.n
            nrr = CBx // SWx
            for rr in range(nrr):
                first = (c == 0)
                last = (c == NC - 1)
                mm(mt[0:1, base + rr * SWx: base + (rr + 1) * SWx],
                   onesD[:, :], xcb[:, rr * SWx:(rr + 1) * SWx], first, last,
                   waits=(([(sCv, cv_v), (sInit, 1)] + (gate if c == 0 else []))
                          if rr == 0 else []))
                mm(et[0:1, ebase + rr * SWx: ebase + (rr + 1) * SWx],
                   onesD[:, :], xsq[:, rr * SWx:(rr + 1) * SWx], first, last,
                   sem=(sStP if rr == nrr - 1 else None),
                   waits=([(sSq, sq_v)] if rr == 0 else []))
            p1_last[p] = sStP.n

        def math_block(m_ap, e2_ap, st_ap, base, width, evac_gate,
                       mt=None, et=None, ebase=None):
            mt = psA if mt is None else mt
            et = psB if et is None else et
            ebase = base if ebase is None else ebase
            act(m_ap, mt[0:1, base:base + width], AF.Copy, sem=sStE,
                waits=evac_gate)
            act(e2_ap, et[0:1, ebase:ebase + width], AF.Copy, sem=sStE)
            stv = sStE.n
            act(st_ap, m_ap, AF.Square, sem=sSq, waits=[(sStE, stv)])
            sqv = sSq.n
            vop("scalar_tensor_tensor", e2_ap, e2_ap, EPS, st_ap,
                op0=ALU.add, op1=ALU.subtract, sem=sMath,
                waits=[(sSq, sqv), (sStE, stv)])
            mtv = sMath.n
            act(e2_ap, e2_ap, AF.Ln, waits=[(sMath, mtv)], sem=sAch)
            act(e2_ap, e2_ap, AF.Exp, scl=-0.5, sem=sStE,
                waits=[(sAch, sAch.n)])
            rv = sStE.n
            vop("tensor_tensor", m_ap, m_ap, e2_ap, op=ALU.mult,
                waits=[(sStE, rv)], sem=sVch)
            vop("tensor_scalar", m_ap, m_ap, -1.0, None, op0=ALU.mult,
                sem=sMath, waits=[(sVch, sVch.n)])
            return sMath.n

        def bcast_block(m_ap, e2_ap, base, width, nmr_v):
            for r in range(width // 512):
                mm(psB[:, base + r * 512: base + (r + 1) * 512], ones1[0:1, :],
                   e2_ap[0:1, r * 512:(r + 1) * 512], True, True,
                   waits=([(sMath, nmr_v)] if r == 0 else []))
                mm(psA[:, base + r * 512: base + (r + 1) * 512],
                   ones1b[0:1, :],
                   m_ap[0:1, r * 512:(r + 1) * 512], True, True, sem=sBc)
            return sBc.n

        def pass2_chunk(c, bcols, base, CBx, dst_ap, bc_v, tmp):
            if XCACHE_OK:
                vop("tensor_tensor", tmp[:, 0:CBx], xcache(c)[:, 0:CBx],
                    psB[:, base:base + CBx], op=ALU.mult,
                    waits=[(sBc, bc_v)], sem=sVch)
                vop("tensor_tensor", dst_ap, tmp[:, 0:CBx],
                    psA[:, base:base + CBx], op=ALU.add, sem=sXn,
                    waits=[(sVch, sVch.n)])
            else:
                buf, p, lv = xs.load(xT[c * 128:(c + 1) * 128, bcols], CBx)
                vop("tensor_tensor", buf[:, 0:CBx], buf[:, 0:CBx],
                    psB[:, base:base + CBx], op=ALU.mult,
                    waits=[(sX[p], lv), (sBc, bc_v)], sem=sVch)
                vop("tensor_tensor", dst_ap, buf[:, 0:CBx],
                    psA[:, base:base + CBx], op=ALU.add, sem=sXn,
                    waits=[(sVch, sVch.n)])
                xs.set_free(p, [(sXn, sXn.n)])

        xn_after_block = []
        xn_chunk = {}
        for c in range(NC):
            pass1_chunk(xT, c, slice(0, CB), 0, CB, SW, [],
                        cache=(xcache(c) if XCACHE_OK else None))
        for h in range(XH):
            base = (h % 2) * 1024
            bc = slice(h * CB, (h + 1) * CB)
            mb = m_bf[0:1, h * CB:(h + 1) * CB]
            eb = e2f[0:1, h * CB:(h + 1) * CB]
            stb = stmp[0:1, h * CB:(h + 1) * CB]
            nmr_v = math_block(mb, eb, stb, base, CB, [(sStP, sStP.n)])
            bc_v = bcast_block(mb, eb, base, CB, nmr_v)
            nb = (h + 1) % 2 * 1024
            for c in range(NC):
                pass2_chunk(c, bc, base, CB, xnT(c)[:, bc], bc_v,
                            xcb_scratch[c % 2])
                xn_chunk[(h, c)] = sXn.n
                if h + 1 < XH:
                    g = ([(sXn, xn_after_block[h - 1])] if h >= 1 else [])
                    pass1_chunk(xT, c, slice((h + 1) * CB, (h + 2) * CB),
                                nb, CB, SW, g,
                                cache=(xcache(c) if XCACHE_OK else None),
                                cache_gate=[(sXn, sXn.n)])
            xn_after_block.append(sXn.n)
        xn_all = sXn.n
        assert XH2 == 1 and S2 <= 1024
        p2_state = {}

        def emit_p2():
            for c in range(NC):
                pass1_chunk(xf, c, slice(0, CB2), 0, CB2, SW2,
                            [(sXn, xn_all)] if c == 0 else [],
                            mt=psB, et=psB, ebase=1024,
                            sq=tgb[:, (c % 2) * CB2:(c % 2 + 1) * CB2])
            p2_state['nmr2'] = math_block(
                m2_bf, e2f2, stmp[0:1, 0:S2], 0, S2,
                [(sStP, sStP.n)], mt=psB, et=psB, ebase=1024)
            p2_state['st2'] = sStE.n

        expv = {}
        gctr = [0]
        grpG = {}
        gmark = {}
        avbank_gate = {0: 0, 1: 0}
        tail_pending = []

        def flush_tail():
            while tail_pending:
                tail_pending.pop(0)()
        prev_head_psA_free = None
        VGW = VG * 128
        per_ps = max(1, 2048 // VGW)

        vconv_next = [None] * (NB + 1)
        vconv_next[0] = hoisted_vconv
        qkconv_next = [None, -1]
        for vb in range(NB):
            if vconv_next[vb] is None:
                vw_flat = vw[vb].rearrange("p c e -> p (c e)")
                vconv = load_conv(
                    lambda off, wdt: vwb[:, off:off + wdt],
                    lambda off, wdt: vw_flat[:, off:off + wdt],
                    NC * VGW,
                    ([(sVm, vb * NKT), (sQm, sQm.n), (sKm, sKm.n)]
                     if vb > 0 else []))
            else:
                vconv = vconv_next[vb]
            sve0 = sVe.n
            TBK = CB // 128
            for tt in range(NKT):
                col = (tt % 2) * 512
                w = []
                if tt == 0:
                    w += [(sWc, vconv)]
                if vb == 0 and tt % TBK == 0:
                    w.append((sXn, xn_after_block[min(tt // TBK, XH - 1)]))
                elif vb > 0 and tt == 0:
                    w.append((sXn, xn_all))
                if tt >= 2:
                    w.append((sVe, sve0 + tt - 1))
                for c in range(NC):
                    mm(psA[:, col:col + VGW],
                       xnT(c)[:, tt * 128:(tt + 1) * 128],
                       vwb[:, c * VGW:(c + 1) * VGW],
                       c == 0, c == NC - 1,
                       sem=(sVm if c == NC - 1 else None),
                       waits=(w if c == 0 else []))
                wv = [(sVm, sVm.n)]
                if vb > 0 and tt == 0:
                    wv.append((sAv, sAv.n))
                act(vsb(tt), psA[:, col:col + VGW], AF.Copy, sem=sVe, waits=wv)

            prev_head_psA_free = (sVe, sVe.n)
            if vb == 0:
                emit_p2()

            for j_in in range(VG):
                hh = vb * VG + j_in
                if qkconv_next[0] is not None and qkconv_next[1] == hh:
                    qconv, kconv = qkconv_next[0]
                    qkconv_next[0] = None
                else:
                    qconv = load_conv(
                        lambda off, wdt: qwb[:, off:off + wdt],
                        lambda off, wdt, hh=hh: qw[hh][:, off:off + wdt],
                        NC * 128,
                        [(sQm, sQm.n)] + [(sVm, (vb + 1) * NKT)])
                    kconv = load_conv(
                        lambda off, wdt: kwb[:, off:off + wdt],
                        lambda off, wdt, hh=hh: kw[hh][:, off:off + wdt],
                        NC * 128,
                        [(sKm, sKm.n)] + [(sVm, (vb + 1) * NKT)])

                for n in range(NQ):
                    w0 = []
                    if n == 0:
                        w0 += [(sWc, qconv), prev_head_psA_free]
                    if hh == 0:
                        w0.append((sXn,
                                   xn_after_block[min(n * QW // CB, XH - 1)]))
                    for c in range(NC):
                        mm(psA[:, n * QW:(n + 1) * QW],
                           qwb[:, c * 128:(c + 1) * 128],
                           xnT(c)[:, n * QW:(n + 1) * QW],
                           c == 0, c == NC - 1,
                           sem=(sQm if c == NC - 1 else None),
                           waits=(w0 if c == 0 else []))
                    act(qsb[:, n * QW:(n + 1) * QW],
                        psA[:, n * QW:(n + 1) * QW], AF.Copy, sem=sQe,
                        waits=[(sQm, sQm.n)])
                qm = sQm.n
                qe = sQe.n

                for n in range(NQ):
                    mm(psA[:, n * QW:(n + 1) * QW],
                       gwb[:, :],
                       qsb[:, n * QW:(n + 1) * QW], True, True,
                       sem=(sGm if n == NQ - 1 else None),
                       waits=([(sQe, qe), (sInit, 1)] if n == 0 else []))
                gm = sGm.n
                flush_tail()
                act(tgb[:, 0:S], psA[:, 0:S], AF.Exp, sem=sTg,
                    waits=[(sGm, gm), (sDm, sDm.n)], bias=gbn[:, :], scl=-1.0)
                tg = sTg.n

                for n in range(NQ):
                    w0 = []
                    if n == 0:
                        w0 += [(sWc, kconv), (sHo, sHo.n), (sVe, sVe.n),
                               (sStE, p2_state['st2'])]
                    if hh == 0:
                        w0.append((sXn,
                                   xn_after_block[min(n * QW // CB, XH - 1)]))
                    for c in range(NC):
                        mm(psB[:, n * QW:(n + 1) * QW],
                           kwb[:, c * 128:(c + 1) * 128],
                           xnT(c)[:, n * QW:(n + 1) * QW],
                           c == 0, c == NC - 1,
                           sem=(sKm if (n == NQ - 1 and c == NC - 1) else None),
                           waits=(w0 if c == 0 else []))
                km = sKm.n
                act(ksb[:, 0:S], psB[:, 0:S], AF.Copy, sem=sKe,
                    waits=[(sKm, km)])
                ke = sKe.n

                spread = []
                if j_in == VG - 1 and vb + 1 < NB:
                    vwf2 = vw[vb + 1].rearrange("p c e -> p (c e)")

                    def vdone(v, vb=vb):
                        vconv_next[vb + 1] = v
                    spread.append((
                        lambda off, wdt: vwb[:, off:off + wdt],
                        lambda off, wdt, vwf2=vwf2: vwf2[:, off:off + wdt],
                        NC * VGW,
                        [(sVm, (vb + 1) * NKT), (sQm, qm), (sKm, km)],
                        vdone))
                elif j_in + 1 < VG:
                    hn = hh + 1
                    qk_acc = {}

                    def qdone(v, qk_acc=qk_acc, hn=hn):
                        qk_acc['q'] = v
                        if 'k' in qk_acc:
                            qkconv_next[0] = (qk_acc['q'], qk_acc['k'])
                            qkconv_next[1] = hn

                    def kdone(v, qk_acc=qk_acc, hn=hn):
                        qk_acc['k'] = v
                        if 'q' in qk_acc:
                            qkconv_next[0] = (qk_acc['q'], qk_acc['k'])
                            qkconv_next[1] = hn
                    spread.append((
                        lambda off, wdt: qwb[:, off:off + wdt],
                        lambda off, wdt, hn=hn: qw[hn][:, off:off + wdt],
                        NC * 128, [(sQm, qm)], qdone))
                    spread.append((
                        lambda off, wdt: kwb[:, off:off + wdt],
                        lambda off, wdt, hn=hn: kw[hn][:, off:off + wdt],
                        NC * 128, [(sKm, km)], kdone))
                spread_pieces = []
                for dstf, srcf, tot, gates, cb_ in spread:
                    for off in range(0, tot, WSC):
                        wdt = min(WSC, tot - off)
                        last = off + wdt >= tot
                        spread_pieces.append(
                            (dstf, srcf, off, wdt, gates if off == 0 else [],
                             cb_ if last else None))

                spread_loaded = []

                def emit_spread_load(n):
                    for _ in range(n):
                        if not spread_pieces or len(spread_loaded) >= 2:
                            return
                        dstf, srcf, off, wdt, gates, cb_ = \
                            spread_pieces.pop(0)
                        buf, p, lv = ws.load(srcf(off, wdt), wdt)
                        spread_loaded.append(
                            (dstf, off, wdt, gates, cb_, buf, p, lv))

                def spread_drain():
                    while spread_pieces or spread_loaded:
                        emit_spread_conv(1)
                        emit_spread_load(1)

                def emit_spread_conv(n):
                    for _ in range(n):
                        if not spread_loaded:
                            return
                        dstf, off, wdt, gates, cb_, buf, p, lv = \
                            spread_loaded.pop(0)
                        vop("tensor_copy", dstf(off, wdt), buf[:, 0:wdt],
                            sem=sWc, waits=[(sW[p], lv)] + list(gates))
                        ws.set_free(p, [(sWc, sWc.n)])
                        if cb_ is not None:
                            cb_(sWc.n)

                npc_tot = len(spread_pieces)
                emit_spread_load((npc_tot + NQ - 1) // NQ)

                for j in range(NQ):
                    nkt_j = KPR * (j + 1)
                    ng = nkt_j // 2
                    av_col = 0 if (j % 2 == 0) else 1536
                    avps = psB[:, av_col:av_col + QW]
                    rhops = psB[0:1, 512:512 + QW]
                    rhobps = psB[:, 1024:1024 + QW]
                    ho_gate = avbank_gate[j % 2]

                    def emit_avrho(g, j=j, j_in=j_in, nkt_j=nkt_j, avps=avps,
                                   rhops=rhops, hh=hh, ho_gate=ho_gate):
                        G = grpG[(hh, j, g)]
                        base = (G % NSLOT) * 2 * QW
                        for kk in range(2):
                            kt = 2 * g + kk
                            diag = kt >= KPR * j
                            if diag:
                                gate, gv = sMk, expv[(hh, j, g, "mk", kk)]
                            else:
                                gate, gv = sEx, expv[(hh, j, g, "ex")]
                            w = [(gate, gv)]
                            if kt == 0 and ho_gate > 0:
                                w.append((sHo, ho_gate))
                            mm(avps,
                               vsb(kt)[:, j_in * 128:(j_in + 1) * 128],
                               probsT[:, base + kk * QW: base + (kk + 1) * QW],
                               kt == 0, kt == nkt_j - 1,
                               sem=(sAv if kt == nkt_j - 1 else None),
                               waits=w)
                            w2_ = [(sRhE, sRhE.n)] if kt == 0 else []
                            last = (kt == nkt_j - 1)
                            mm(rhops, onesC[:, :],
                               probsT[:, base + kk * QW: base + (kk + 1) * QW],
                               kt == 0, last,
                               sem=(sRh if last else
                                    (sPgr if kk == 1 else None)),
                               waits=w2_)
                            if kk == 1:
                                gmark[G] = ((sRh, sRh.n) if last
                                            else (sPgr, sPgr.n))

                    for g in range(ng):
                        gp = g % 2
                        G = gctr[0]
                        gctr[0] += 1
                        grpG[(hh, j, g)] = G
                        base = (G % NSLOT) * 2 * QW
                        w = []
                        if g == 0 and j == 0:
                            w += [(sTg, tg)]
                        if g >= 2:
                            w.append((sEx, expv[(hh, j, g - 2, "ex")]))
                        elif j > 0 and g < 2:
                            w.append((sEx, expv[(hh, j - 1, "last_ex")]))
                        if g == 0 and j == 0:
                            w.append((sKe, ke))
                        for kk in range(2):
                            kt = 2 * g + kk
                            mm(psA[:, gp * 1024 + kk * 512:
                                   gp * 1024 + kk * 512 + QW],
                               ksb[:, kt * 128:(kt + 1) * 128],
                               qsb[:, j * QW:(j + 1) * QW],
                               True, True,
                               sem=(sSc if kk == 1 else None),
                               waits=(w if kk == 0 else []))
                        scv = sSc.n
                        if g == 1:
                            flush_tail()
                        we = [(sSc, scv)]
                        if G >= NSLOT:
                            we.append(gmark[G - NSLOT])
                        act(probsT[:, base: base + 2 * QW],
                            psA[:, gp * 1024: gp * 1024 + 2 * QW],
                            AF.Exp, sem=sEx, waits=we,
                            bias=CSHIFT, scl=SCALE)
                        expv[(hh, j, g, "ex")] = sEx.n
                        if 2 * g + 1 >= KPR * j:
                            d0 = 2 * g - KPR * j
                            for kk2 in range(2):
                                vop("tensor_tensor",
                                    probsT[:, base + kk2 * QW:
                                           base + (kk2 + 1) * QW],
                                    probsT[:, base + kk2 * QW:
                                           base + (kk2 + 1) * QW],
                                    mask_sb[:, (d0 + kk2) * 512:
                                            (d0 + kk2 + 1) * 512],
                                    op=ALU.mult, sem=sMk,
                                    waits=[(sEx, sEx.n)] if kk2 == 0 else [])
                                expv[(hh, j, g, "mk", kk2)] = sMk.n
                        if g >= 1:
                            emit_avrho(g - 1)
                    expv[(hh, j, "last_ex")] = sEx.n
                    emit_avrho(ng - 1)
                    avv = sAv.n
                    rhv = sRh.n
                    act(rho_sb[0:1, 0:QW], rhops, AF.Copy, sem=sRhE,
                        waits=[(sRh, rhv)])
                    rhev = sRhE.n

                    def emit_tail(j=j, hh=hh, rhev=rhev, avv=avv, tg=tg,
                                  avps=avps, rhobps=rhobps):
                        mm(rhobps, ones1[0:1, :], rho_sb[0:1, 0:QW],
                           True, True, sem=sRhB,
                           waits=[(sRhE, rhev), (sDm, sDm.n)])
                        rhbv = sRhB.n
                        vop("scalar_tensor_tensor", Dsb[:, 0:QW],
                            tgb[:, j * QW:(j + 1) * QW], 1.0, rhobps,
                            op0=ALU.add, op1=ALU.mult, sem=sDm,
                            waits=[(sRhB, rhbv), (sTg, tg), (sDi, sDi.n)])
                        dmv = sDm.n
                        act(Dsb[:, 0:QW], Dsb[:, 0:QW], AF.Ln,
                            waits=[(sDm, dmv), (sHo, sHo.n)], sem=sAch)
                        act(Dinv[:, 0:QW], Dsb[:, 0:QW], AF.Exp, scl=-1.0,
                            sem=sDi, waits=[(sAch, sAch.n)])
                        div = sDi.n
                        vop("tensor_tensor", hob(hh)[:, j * QW:(j + 1) * QW],
                            avps, Dinv[:, 0:QW], op=ALU.mult, sem=sHo,
                            waits=[(sDi, div), (sAv, avv)])
                        avbank_gate[j % 2] = sHo.n
                    tail_pending.append(emit_tail)
                    nu = len(pconv_units)
                    rdone = hh * NQ + j + 1
                    emit_pconv((nu * rdone) // (HPC * NQ)
                               - (nu * (rdone - 1)) // (HPC * NQ))
                    if j < NQ - 1:
                        cv_t = (npc_tot * (j + 1) + NQ - 1) // NQ
                        ld_t = min(npc_tot,
                                   (npc_tot * (j + 2) + NQ - 1) // NQ)
                        done_cv = npc_tot - len(spread_pieces) \
                            - len(spread_loaded)
                        emit_spread_conv(cv_t - done_cv)
                        done_ld = npc_tot - len(spread_pieces)
                        emit_spread_load(ld_t - done_ld)
                    else:
                        spread_drain()
                spread_drain()
                prev_head_psA_free = (sEx, expv[(hh, NQ - 1, "last_ex")])
        flush_tail()
        ho_all = sHo.n
        qm_all, km_all, vm_all = sQm.n, sKm.n, sVm.n

        for n in range(NST2):
            mm(psB[:, n * SW2:(n + 1) * SW2], ones1[0:1, :],
               e2f2[0:1, n * SW2:(n + 1) * SW2], True, True,
               waits=([(sMath, p2_state['nmr2']), (sHo, ho_all), (sDm, sDm.n)]
                      if n == 0 else []))
            mm(psA[:, n * SW2:(n + 1) * SW2], ones1b[0:1, :],
               m2_bf[0:1, n * SW2:(n + 1) * SW2], True, True, sem=sBc,
               waits=([(sEx, expv[(HPC - 1, NQ - 1, "last_ex")])]
                      if n == 0 else []))
        bc2 = sBc.n
        act(r2b_sb[:, 0:S2], psB[:, 0:S2], AF.Copy, sem=sM2c,
            waits=[(sBc, bc2)])
        act(nm2b_sb[:, 0:S2], psA[:, 0:S2], AF.Copy, sem=sM2c)
        m2c = sM2c.n
        for c in range(NC):
            for h in range(XH2):
                buf, p, lv = xs.load(
                    xf[c * 128:(c + 1) * 128, h * CB2:(h + 1) * CB2], CB2)
                cols = slice(h * CB2, (h + 1) * CB2)
                vop("tensor_tensor", buf[:, 0:CB2], buf[:, 0:CB2],
                    r2b_sb[:, cols], op=ALU.mult,
                    waits=[(sX[p], lv), (sM2c, m2c)], sem=sVch)
                vop("tensor_tensor", xnF(c)[:, cols], buf[:, 0:CB2],
                    nm2b_sb[:, cols], op=ALU.add, sem=sXn2,
                    waits=[(sVch, sVch.n)] +
                          ([(sQm, qm_all), (sKm, km_all), (sVm, vm_all)]
                           if (c == 0 and h == 0) else []))
                xs.set_free(p, [(sXn2, sXn2.n)])
        xn2_all = sXn2.n

        G8 = min(4, QT)
        QH = QT // G8
        opev_done = {}
        store_done = {}
        oper_done = {}

        ow_segs = {0: [(qsb, S), (ksb, S)],
                   1: [(tgb, S), (rssb[0], D2), (rssb[1], D2)]}

        def seg_place(segs, off):
            for t, w_ in segs:
                if off < w_:
                    return t, off
                off -= w_
            raise AssertionError("owq overflow")

        def segs_ok(segs):
            off = 0
            for hh_ in range(HPC):
                try:
                    t, o = seg_place(segs, hh_ * EW)
                except AssertionError:
                    return False
                if o + EW > dict((id(a), b) for a, b in segs)[id(t)]:
                    return False
            return True

        ow_dbuf = segs_ok(ow_segs[1]) and NER >= 2

        def owq(par, hh):
            t, o = seg_place(ow_segs[par], hh * EW)
            return t[:, o:o + EW]

        op_store_q = []

        def flush_op_stores():
            while op_store_q:
                op_store_q.pop(0)()

        def grp_nstores(er):
            n = 0
            e0 = er * EW
            for r in range(2):
                lo = max(e0, r * D2)
                hi = min(e0 + EW, (r + 1) * D2)
                if lo < hi:
                    n += 1
            return n * G8

        store_plan = {}
        _cnt = [0, 0]
        for _er in range(NER):
            for _qh in range(QH):
                _g = _er * QH + _qh
                _cnt[_g % 2] += 16 * grp_nstores(_er)
                store_plan[_g] = _cnt[_g % 2]

        for er in range(NER):
            par = (er % 2) if ow_dbuf else 0
            wv = []
            if er < (2 if ow_dbuf else 1):
                wv += [(sSc, sSc.n)] + pconv_done_gate()
                if par == 1:
                    wv += [(sDm, sDm.n), (sBc, sBc.n)]
            else:
                wv.append((sOp, oper_done[er - (2 if ow_dbuf else 1)]))
            off = 0
            first_dma = True
            while off < HPC * EW:
                t, o = seg_place(ow_segs[par], off)
                run = EW
                while (off + run < HPC * EW):
                    t2, o2 = seg_place(ow_segs[par], off + run)
                    if t2 is t and o2 == o + run:
                        run += EW
                    else:
                        break
                dma("gpsimd", t[:, o:o + run],
                    owbf[er][:, off:off + run],
                    sOwL[par], waits=(wv if first_dma else []))
                first_dma = False
                off += run
            owlv = sOwL[par].n
            flush_op_stores()
            for qh in range(QH):
                grp = er * QH + qh
                for hhh in range(HPC):
                    for q8 in range(G8):
                        qt = qh * G8 + q8
                        ps = psA if grp % 2 == 0 else psB
                        col = q8 * 512
                        w = []
                        if q8 == 0 and hhh == 0:
                            w.append((sOwL[par], owlv))
                            if grp == 0:
                                w += [(sHo, ho_all), (sM2c, m2c)]
                            elif grp == 1:
                                w.append((sHo, ho_all))
                            else:
                                w.append((sOpE, opev_done[grp - 2]))
                        mm(ps[:, col:col + EW],
                           hob(hhh)[:, qt * 128:(qt + 1) * 128],
                           owq(par, hhh),
                           hhh == 0, hhh == HPC - 1,
                           sem=(sOp if (hhh == HPC - 1 and q8 == G8 - 1)
                                else None),
                           waits=w)
                opv = sOp.n
                ps = psA if grp % 2 == 0 else psB
                wv2 = [(sOp, opv)]
                if grp >= 2:
                    wv2.append((sOpSp[grp % 2], store_plan[grp - 2]))
                act(opst[grp % 2][:, 0:G8 * EW], ps[:, 0:G8 * EW], AF.Copy,
                    sem=sOpE, waits=wv2)
                opev = sOpE.n

                def emit_stores(qh=qh, er=er, grp=grp, opev=opev):
                    first_store = True
                    for q8 in range(G8):
                        qt = qh * G8 + q8
                        e0 = er * EW
                        for r in range(2):
                            lo = max(e0, r * D2)
                            hi = min(e0 + EW, (r + 1) * D2)
                            if lo >= hi:
                                continue
                            dma("gpsimd",
                                op_dram[r * S + qt * 128:
                                        r * S + (qt + 1) * 128,
                                        lo - r * D2:hi - r * D2],
                                opst[grp % 2][:, q8 * EW + lo - e0:
                                              q8 * EW + hi - e0],
                                sOpSp[grp % 2],
                                waits=([(sOpE, opev)] if first_store else []))
                            first_store = False
                    store_done[grp] = sOpSp[grp % 2].n
                op_store_q.append(emit_stores)
                opev_done[grp] = sOpE.n
            oper_done[er] = sOp.n
        flush_op_stores()
        collective("ReduceScatter", ALU.add,
                   [[0, 1], [2, 3], [4, 5], [6, 7]],
                   op_dram.ap().opt(), rs_dram.ap().opt(), sRS,
                   waits=[(sOpSp[0], sOpSp[0].n), (sOpSp[1], sOpSp[1].n)])

        opev_all = sOpE.n
        for tt in range(TT2):
            bank = tt % 8
            ps = psA if bank < 4 else psB
            col = (bank % 4) * 512
            w = [(sXn2, xn2_all)]
            if tt == 0:
                w.append((sOpE, opev_all))
            if tt >= 8:
                w.append((sRtE, sRtE.n - 7))
            for c in range(NC):
                mm(ps[:, col:col + 8],
                   xnF(c)[:, tt * 128:(tt + 1) * 128],
                   wrb[:, c * 8:(c + 1) * 8],
                   c == 0, c == NC - 1,
                   sem=(sRt if c == NC - 1 else None),
                   waits=(w if c == 0 else []))
            rtv = sRt.n
            act(rt_sb[:, tt * 8:(tt + 1) * 8], ps[:, col:col + 8],
                AF.Copy, sem=sRtE, waits=[(sRt, rtv)])
        rte = sRtE.n
        act(rt_sb[:, :], rt_sb[:, :], AF.Exp, sem=sRtE, waits=[(sRtE, rte)])
        rtexp = sRtE.n
        rt3 = rt_sb[:, :].rearrange("p (t e) -> p t e", e=8)
        lo3 = r_lo[:, :].rearrange("p (t e) -> p t e", e=8)
        kp3 = r_kp[:, :].rearrange("p (t e) -> p t e", e=8)
        g3 = g_sb[:, :].rearrange("p (t e) -> p t e", e=8)
        m1b3 = r_m1[:, :].rearrange("p (t o) -> p t o", o=1).broadcast_to(
            [128, TT2, 8])
        m2b3 = r_m2[:, :].rearrange("p (t o) -> p t o", o=1).broadcast_to(
            [128, TT2, 8])
        dnb3 = r_dn[:, :].rearrange("p (t o) -> p t o", o=1).broadcast_to(
            [128, TT2, 8])
        vop("tensor_reduce", r_m1[:, :], rt3, AX.X, ALU.max, sem=sTp,
            waits=[(sRtE, rtexp)])
        vop("tensor_tensor", lo3, rt3, m1b3, op=ALU.is_lt, sem=sTp,
            waits=[(sTp, sTp.n)])
        vop("tensor_tensor", lo3, rt3, lo3, op=ALU.mult, sem=sTp,
            waits=[(sTp, sTp.n)])
        vop("tensor_reduce", r_m2[:, :], lo3, AX.X, ALU.max, sem=sTp,
            waits=[(sTp, sTp.n)])
        vop("tensor_tensor", kp3, rt3, m2b3, op=ALU.is_ge, sem=sTp,
            waits=[(sTp, sTp.n)])
        vop("tensor_tensor", r_dn[:, :], r_m1[:, :], r_m2[:, :], op=ALU.add,
            sem=sTp, waits=[(sTp, sTp.n)])
        vop("reciprocal", r_dn[:, :], r_dn[:, :], sem=sTp,
            waits=[(sTp, sTp.n)])
        vop("tensor_tensor", kp3, kp3, dnb3, op=ALU.mult, sem=sTp,
            waits=[(sTp, sTp.n)])
        vop("tensor_tensor", g3, rt3, kp3, op=ALU.mult, sem=sTp,
            waits=[(sTp, sTp.n)])
        gdone = sTp.n

        finale_stores = []
        st_totals = {}
        n_done = 0

        def emit_finale_tile(qt):
            nonlocal n_done
            n = n_done
            pr = n % 3
            wrs = [(sRS, 1)] if n == 0 else []
            if n >= 2:
                wrs.append((sFin, sFin.n - 1))
            dma("sync", rssb[n % 2][:, 0:D2],
                rs_dram[qt * 128:(qt + 1) * 128, :], sRsL[n % 2], waits=wrs)
            rsv = sRsL[n % 2].n
            buf, p, lv = xs.load(xr[qt * 128:(qt + 1) * 128, :], D2)
            wadd = [(sX[p], lv), (sRsL[n % 2], rsv)]
            if n >= 3:
                wadd.append((sSt[pr], 16 * (n // 3)))
            vop("tensor_tensor", outsb[pr][:, 0:D2], buf[:, 0:D2],
                rssb[n % 2][:, 0:D2], op=ALU.add, sem=sFin, waits=wadd)
            xs.set_free(p, [(sFin, sFin.n)])
            fv = sFin.n

            def st(qt=qt, pr=pr, fv=fv):
                dma("gpsimd", out[qt * 128:(qt + 1) * 128, :],
                    outsb[pr][:, 0:D2], sSt[pr],
                    waits=[(sFin, fv), (sSt[pr], sSt[pr].n)])
                st_totals[pr] = sSt[pr].n
            finale_stores.append(st)
            n_done += 1

        def flush_finale_stores(k=None):
            while finale_stores and (k is None or k > 0):
                finale_stores.pop(0)()
                if k is not None:
                    k -= 1

        def emit_finale_tail():
            n = n_done
            pr = n % 3
            dma("sync", rssb[n % 2][:, 0:D2], rs_dram[0:128, :],
                sRsL[n % 2], waits=[(sFin, sFin.n - 1)])
            rsv = sRsL[n % 2].n
            buf, p, lv = xs.load(xr[0:128, :], D2)
            wadd = [(sX[p], lv), (sRsL[n % 2], rsv),
                    (sSt[pr], 16 * (n // 3))]
            vop("tensor_tensor", outsb[pr][:, 0:D2], buf[:, 0:D2],
                rssb[n % 2][:, 0:D2], op=ALU.add, sem=sFin, waits=wadd)
            xs.set_free(p, [(sFin, sFin.n)])
            dma("gpsimd", o8sb[0:8, :], o8r_dram[:, :], sArL,
                waits=[(sAR, 1)])
            vop("tensor_scalar", fad[0:8, :], o8sb[0:8, D2:2 * D2],
                selb[0:8, :], None, op0=ALU.mult,
                waits=[(sArL, sArL.n), (sCst, cl0),
                       (sSt[2], st_totals.get(2, 0))], sem=sVch)
            vop("scalar_tensor_tensor", fad[0:8, :], o8sb[0:8, 0:D2],
                sela[0:8, :], fad[0:8, :], op0=ALU.mult, op1=ALU.add,
                sem=sVch, waits=[(sVch, sVch.n)])
            vop("tensor_tensor", outsb[pr][0:8, 0:D2],
                outsb[pr][0:8, 0:D2], fad[0:8, :], op=ALU.add, sem=sFin,
                waits=[(sVch, sVch.n)])
            fv = sFin.n
            dma("gpsimd", out[0:128, :], outsb[pr][:, 0:D2], sSt[pr],
                waits=[(sFin, fv)])
            for s_ in sSt:
                wt("gpsimd", s_, s_.n)


        def ffn_layer(wsrc, rhs_fn, dst_fn, sMM, sEV, gate_first, nch,
                      conv_gate=()):
            mm0 = sMM.n
            ev0 = sEV.n
            for it in range(NIT):
                wcw = []
                if it >= 2:
                    wcw.append((sMM, mm0 + it - 1))
                elif conv_gate:
                    wcw.append((conv_gate[0], conv_gate[1] - 1 + it))
                wcv = load_conv(
                    lambda off, wdt, it=it: w13b[it % 2][:, off:off + wdt],
                    lambda off, wdt, it=it: wsrc[it][:, off:off + wdt],
                    nch * 128, wcw)
                ps = psA if it % 2 == 0 else psB
                w = [(sWc, wcv)] + (list(gate_first) if it == 0 else [])
                if it >= 2:
                    w.append((sEV, ev0 + it - 1))
                for n in range(NF):
                    for c in range(nch):
                        mm(ps[:, n * FW:(n + 1) * FW],
                           w13b[it % 2][:, c * 128:(c + 1) * 128],
                           rhs_fn(c)[:, n * FW:(n + 1) * FW],
                           c == 0, c == nch - 1,
                           sem=(sMM if (n == NF - 1 and c == nch - 1) else None),
                           waits=(w if (n == 0 and c == 0) else []))
                mmv = sMM.n
                act(dst_fn(it)[:, 0:S2], ps[:, 0:S2], AF.Silu, sem=sEV,
                    waits=[(sMM, mmv)])

        h1mm0 = sH1.n
        ffn_layer(w1, xnF, h1sb, sH1, sH1E, [(sRtE, rte)], NC)
        h1e = sH1E.n
        ffn_layer(w3, h1sb, h2sb, sH2, sH2E, [(sH1E, h1e)], NCI,
                  conv_gate=(sH1, h1mm0 + NIT))
        h2e = sH2E.n

        for qt in range(1, QT):
            emit_finale_tile(qt)

        h3mm = {}
        for er in range(NERH):
            wbuf = RD if er % 2 == 0 else vwb
            w2c = lambda ci, wbuf=wbuf: wbuf[:, ci * EWH:(ci + 1) * EWH]
            w = []
            if er < 2:
                w += [(sHo, ho_all), (sQm, qm_all), (sKm, km_all),
                      (sVm, vm_all)] + pconv_done_gate()
            else:
                w.append((sH3, h3mm[er - 2]))
            dma("gpsimd", wbuf[:, 0:NCI * EWH], w2bf[er],
                sW2L[er % 2], waits=w)
            w2lv = sW2L[er % 2].n
            if finale_stores:
                flush_finale_stores(-(-len(finale_stores) // (NERH - er)))
            for tt in range(TT2):
                ps = psA if tt % 2 == 0 else psB
                w = [(sW2L[er % 2], w2lv)]
                if er == 0 and tt == 0:
                    w.append((sH2E, h2e))
                if tt >= 2 or er > 0:
                    w.append((sH3E, sH3E.n - 1))
                for ci in range(NCI):
                    mm(ps[:, 0:EWH],
                       h2sb(ci)[:, tt * 128:(tt + 1) * 128],
                       w2c(ci), ci == 0, ci == NCI - 1,
                       sem=(sH3 if ci == NCI - 1 else None),
                       waits=(w if ci == 0 else []))
                h3v = sH3.n
                wv = [(sH3, h3v)]
                if er == 0 and tt == 0:
                    wv.append((sH2, sH2.n))
                act(h3sb(tt)[:, er * EWH:(er + 1) * EWH],
                    ps[:, 0:EWH], AF.Copy, sem=sH3E,
                    waits=wv)
            h3mm[er] = sH3.n
        h3e = sH3E.n

        for er in range(NERH):
            ps = psA if er < 4 else psB
            col = (er % 4) * 512
            for tt in range(TT2):
                mm(ps[0:8, col:col + EWH],
                   g_sb[:, tt * 8:(tt + 1) * 8],
                   h3sb(tt)[:, er * EWH:(er + 1) * EWH],
                   tt == 0, tt == TT2 - 1,
                   sem=(sO8 if tt == TT2 - 1 else None),
                   waits=([(sH3E, h3e), (sTp, gdone)]
                          if (er == 0 and tt == 0) else []))
        o8v = sO8.n
        for er in range(NERH):
            ps = psA if er < 4 else psB
            col = (er % 4) * 512
            act(o8sb[0:8, er * EWH:(er + 1) * EWH], ps[0:8, col:col + EWH],
                AF.Copy, sem=sO8E,
                waits=([(sO8, o8v)] if er == 0 else []))
        dma("gpsimd", o8_dram[:, :], o8sb[0:8, :], sO8S,
            waits=[(sO8E, sO8E.n)])
        collective("AllReduce", ALU.add, [[0, 1, 2, 3, 4, 5, 6, 7]],
                   o8_dram.ap().opt(), o8r_dram.ap().opt(), sAR,
                   waits=[(sO8S, sO8S.n)])
        flush_finale_stores()

        emit_finale_tail()

        block = es.enter_context(nc.Block())

        @block.sync
        def _(e):
            for fn in prog["sync"]:
                fn(e)

        @block.gpsimd
        def _(e):
            for fn in prog["gpsimd"]:
                fn(e)

        @block.tensor
        def _(e):
            for fn in prog["tensor"]:
                fn(e)

        @block.vector
        def _(e):
            for fn in prog["vector"]:
                fn(e)

        @block.scalar
        def _(e):
            for fn in prog["scalar"]:
                fn(e)

    return nc


def _block_w(wT_sl, n_out_blk, out_blk):
    Dd = wT_sl.shape[0]
    ncc = Dd // 128
    a = wT_sl.reshape(ncc, 128, n_out_blk, out_blk)
    return np.ascontiguousarray(a.transpose(2, 1, 0, 3).reshape(
        n_out_blk, 128, ncc * out_blk))


def _mask_const():
    m = np.zeros((128, 4 * 512), dtype=np.float32)
    kk = np.arange(128)[:, None]
    qq = np.arange(512)[None, :]
    for d in range(4):
        m[:, d * 512:(d + 1) * 512] = (kk <= (qq - 128 * d))
    return m.astype(bf16)


def make_inputs(x, q_w, k_w, v_w, o_w, gate_w, gate_b, w1_, w3_, w2_, wr_,
                S, D, H, INTER):
    HPC = H // 2
    NC = D // 128
    NIT = INTER // 128
    NCI = INTER // 128
    VG = min(2, HPC)
    NB = HPC // VG
    D2 = D // 2
    NER = max(1, D // 512)
    EW = D // NER
    EWH = min(256, D)
    NERH = D // EWH
    S2 = S // 2

    qwT = np.ascontiguousarray(q_w.T)
    kwT = np.ascontiguousarray(k_w.T)
    vwT = np.ascontiguousarray(v_w.T)
    owT = np.ascontiguousarray(o_w.T)
    w1T = np.ascontiguousarray(w1_.T)
    w3T = np.ascontiguousarray(w3_.T)
    w2T = np.ascontiguousarray(w2_.T)
    wrT = np.ascontiguousarray(wr_.T)
    gwT = np.ascontiguousarray(gate_w.T)
    mask = _mask_const()

    in_maps = []
    for core in range(8):
        b, g = core // 2, core % 2
        hw = HPC * 128
        d = {
            "xT": np.ascontiguousarray(x[b].T),
            "xf": np.ascontiguousarray(x[b, g * S2:(g + 1) * S2, :].T),
            "xr": np.ascontiguousarray(x[b][:, g * D2:(g + 1) * D2]),
            "qw": _block_w(qwT[:, g * hw:(g + 1) * hw], HPC, 128),
            "kw": _block_w(kwT[:, g * hw:(g + 1) * hw], HPC, 128),
            "vw": _block_w(vwT[:, g * hw:(g + 1) * hw], NB, VG * 128)
                  .reshape(NB, 128, NC, VG * 128),
            "gw": gwT,
            "gb": np.ascontiguousarray(gate_b.reshape(128, 1)),
            "ow": np.ascontiguousarray(owT[g * hw:(g + 1) * hw, :]),
            "w1": _block_w(w1T, NIT, 128),
            "w3": _block_w(w3T, NIT, 128),
            "w2": _block_w(w2T, NERH, EWH).reshape(NERH, 128, NCI, EWH),
            "wr": np.ascontiguousarray(
                wrT.reshape(NC, 128, 8).transpose(1, 0, 2)
                .reshape(128, NC * 8)),
            "mk": mask,
            "sel_a": np.full((8, 1), 1.0 if core == 0 else 0.0, np.float32),
            "sel_b": np.full((8, 1), 1.0 if core == 1 else 0.0, np.float32),
        }
        in_maps.append(d)
    return in_maps


def kernel(L0_attn_x, L0_attn_q_w, L0_attn_k_w, L0_attn_v_w, L0_attn_o_w,
           L0_attn_gate_w, L0_attn_gate_b, L0_ffn_w1, L0_ffn_w3, L0_ffn_w2,
           L0_ffn_wr):
    S, D, H, INTER = FULL["S"], FULL["D"], FULL["H"], FULL["INTER"]
    x = np.asarray(L0_attn_x, np.float32)
    in_maps = make_inputs(
        x, np.asarray(L0_attn_q_w, np.float32),
        np.asarray(L0_attn_k_w, np.float32),
        np.asarray(L0_attn_v_w, np.float32),
        np.asarray(L0_attn_o_w, np.float32),
        np.asarray(L0_attn_gate_w, np.float32),
        np.asarray(L0_attn_gate_b, np.float32),
        np.asarray(L0_ffn_w1, np.float32),
        np.asarray(L0_ffn_w3, np.float32),
        np.asarray(L0_ffn_w2, np.float32),
        np.asarray(L0_ffn_wr, np.float32),
        S, D, H, INTER)
    nc = build_program(S, D, H, INTER)
    res = run_bass_kernel_spmd(nc, in_maps, core_ids=list(range(8))).results
    D2 = D // 2
    full = np.empty((4, S, D), np.float32)
    for core in range(8):
        b, g = core // 2, core % 2
        full[b, :, g * D2:(g + 1) * D2] = res[core]["out"]
    return full


# version 24
# speedup vs baseline: 1.0570x; 1.0570x over previous
import math
from contextlib import ExitStack

import numpy as np
import ml_dtypes

import concourse.bass as bass
import concourse.mybir as mybir
from concourse.bass_utils import run_bass_kernel_spmd

F32 = mybir.dt.float32
BF16 = mybir.dt.bfloat16
AF = mybir.ActivationFunctionType
ALU = mybir.AluOpType
AX = mybir.AxisListType

f32 = np.float32
bf16 = ml_dtypes.bfloat16

FULL = dict(S=2048, D=2048, H=16, INTER=2048)


class Sem:
    def __init__(self, nc, es, name):
        self.h = es.enter_context(nc.semaphore(name))
        self.n = 0


def build_program(S, D, H, INTER):
    HPC = H // 2
    NC = D // 128
    NQ = max(1, S // 512)
    QW = S // NQ
    NKT = S // 128
    KPR = NKT // NQ
    S2 = S // 2
    NF = max(1, S2 // 512)
    FW = S2 // NF
    TT2 = S2 // 128
    NIT = INTER // 128
    NCI = INTER // 128
    VG = min(2, HPC)
    NB = HPC // VG
    D2 = D // 2
    NER = max(1, D // 512)
    EW = D // NER
    EWH = min(256, D)
    NERH = D // EWH
    CB = min(1024, S)
    XH = S // CB
    CB2 = min(1024, S2)
    XH2 = S2 // CB2
    NST = max(1, S // 512)
    SW = S // NST
    NST2 = max(1, S2 // 512)
    SW2 = S2 // NST2
    QT = S // 128
    SCALE = float(1.0 / math.sqrt(128.0))
    CSHIFT = -16.0
    EPS = 1e-5
    recipD = 1.0 / D

    nc = bass.Bass("TRN2", target_bir_lowering=False)

    dI = lambda n, sh, dt=F32: nc.dram_tensor(n, sh, dt, kind="ExternalInput").ap()
    xT = dI("xT", [D, S])
    xf = dI("xf", [D, S2])
    xr = dI("xr", [S, D2])
    qw = dI("qw", [HPC, 128, NC * 128])
    kw = dI("kw", [HPC, 128, NC * 128])
    vw = dI("vw", [NB, 128, NC, VG * 128])
    gw = dI("gw", [128, 128])
    gb = dI("gb", [128, 1])
    ow = dI("ow", [HPC * 128, D])
    w1 = dI("w1", [NIT, 128, NC * 128])
    w3 = dI("w3", [NIT, 128, NCI * 128])
    w2 = dI("w2", [NERH, 128, NCI, EWH])
    wr = dI("wr", [128, NC * 8])
    mk = dI("mk", [128, 4 * 512], BF16)
    sel_a = dI("sel_a", [8, 1])
    sel_b = dI("sel_b", [8, 1])

    out = nc.dram_tensor("out", [S, D2], F32, kind="ExternalOutput").ap()

    owbf = nc.dram_tensor("owbf", [NER, 128, HPC * EW], BF16)
    w2bf = nc.dram_tensor("w2bf", [NERH, 128, NCI * EWH], BF16)
    op_dram = nc.dram_tensor("op_dram", [2 * S, D2], BF16)
    rs_dram = nc.dram_tensor("rs_dram", [S, D2], BF16)
    o8_dram = nc.dram_tensor("o8_dram", [8, D], F32)
    o8r_dram = nc.dram_tensor("o8r_dram", [8, D], F32)
    o8g_dram = nc.dram_tensor("o8g_dram", [64, D], F32)

    with ExitStack() as es:
        sbuf = lambda n, sh, dt: es.enter_context(nc.sbuf_tensor(n, sh, dt))

        RA = sbuf("RA", [128, NC * S], BF16)
        xnT = lambda c: RA[:, c * S:(c + 1) * S]
        HALF = NC * S // 2
        xnF = lambda c: RA[:, c * S2:(c + 1) * S2]
        h1sb = lambda i: RA[:, HALF + i * S2: HALF + (i + 1) * S2]
        h3sb = lambda t: RA[:, HALF + t * D: HALF + (t + 1) * D]

        RB = sbuf("RB", [128, HPC * S], BF16)
        hob = lambda hh: RB[:, hh * S:(hh + 1) * S]
        h2sb = lambda i: RB[:, i * S2:(i + 1) * S2]

        NSLOT = 4
        RC = sbuf("RC", [128, NSLOT * 2 * QW], BF16)
        probsT = RC

        RD = sbuf("RD", [128, NKT * VG * 128], BF16)
        vsb = lambda tt: RD[:, tt * VG * 128:(tt + 1) * VG * 128]

        assert HPC >= 2, "need >=2 heads per core for buffer aliasing"
        vwb = sbuf("vwb", [128, NC * VG * 128], BF16)
        qwb = vwb[:, 0:NC * 128]
        kwb = vwb[:, NC * 128:2 * NC * 128]
        w13b = [sbuf("w13b0", [128, NC * 128], BF16),
                sbuf("w13b1", [128, NC * 128], BF16)]
        RDW = NKT * VG * 128

        def w2b(ci):
            off = ci * EW
            if off < RDW:
                return RD[:, off:off + EW]
            return vwb[:, off - RDW:off - RDW + EW]
        gwb = sbuf("gwb", [128, 128], BF16)
        owx = [sbuf("owx0", [128, 512], BF16), sbuf("owx1", [128, 512], BF16)]
        wrb = sbuf("wrb", [128, NC * 8], BF16)

        WS = [sbuf("WS0", [128, 1024], F32), sbuf("WS1", [128, 1024], F32)]
        XS = [sbuf("XS0", [128, CB], F32), sbuf("XS1", [128, CB], F32)]

        qsb = sbuf("qsb", [128, S], BF16)
        ksb = sbuf("ksb", [128, S], BF16)
        tgb = sbuf("tgb", [128, S], BF16)

        scrD = sbuf("scrD", [128, 2 * QW], F32)
        Dsb = scrD[:, 0:QW]
        Dinv = scrD[:, QW:2 * QW]
        nmr2b2 = sbuf("nmr2b2", [128, 2 * S2], F32)
        nm2b_sb = nmr2b2[:, 0:S2]
        r2b_sb = nmr2b2[:, S2:2 * S2]
        outs2 = sbuf("outs2", [128, 2 * D2], F32)
        outsb = None
        OPW = min(4, QT) * EW
        opst = [sbuf("opst0", [128, OPW], BF16),
                sbuf("opst1", [128, OPW], BF16)]
        rssb = [sbuf("rssb0", [128, D2], BF16), sbuf("rssb1", [128, D2], BF16)]

        assert 2 * QW >= D2
        outsb = [outs2[:, 0:D2], outs2[:, D2:2 * D2], scrD[:, 0:D2]]
        m_bf = opst[0][0:1, 0:S]
        stmp = opst[1][0:1, 0:S]
        m2_bf = rssb[0][0:1, 0:S2]
        e2f = outs2[0:1, 0:S]
        e2f2 = outs2[0:1, 0:S2]
        rho_sb = sbuf("rho_sb", [1, QW], F32)

        rt_sb = sbuf("rt_sb", [128, TT2 * 8], F32)
        g_sb = sbuf("g_sb", [128, TT2 * 8], BF16)
        r_m1 = sbuf("r_m1", [128, TT2], F32)
        r_m2 = sbuf("r_m2", [128, TT2], F32)
        r_lo = sbuf("r_lo", [128, TT2 * 8], F32)
        r_kp = sbuf("r_kp", [128, TT2 * 8], F32)
        r_dn = sbuf("r_dn", [128, TT2], F32)

        onesD = sbuf("onesD", [128, 1], BF16)
        ones1 = sbuf("ones1", [1, 128], F32)
        onesC = sbuf("onesC", [128, 1], BF16)
        ones1b = sbuf("ones1b", [1, 128], BF16)
        mask_sb = sbuf("mask_sb", [128, 4 * 512], BF16)
        gbn = sbuf("gbn", [128, 1], F32)
        gbs = sbuf("gbs", [128, 1], F32)
        sela = sbuf("sela", [8, 1], F32)
        selb = sbuf("selb", [8, 1], F32)
        o8sb = nmr2b2[0:8, 0:D]
        fad = scrD[0:8, 0:D2]

        psA = es.enter_context(nc.psum_tensor("psA", [128, 2048], F32))
        psB = es.enter_context(nc.psum_tensor("psB", [128, 2048], F32))

        SM = lambda name: Sem(nc, es, name)
        sX = [SM("sX0"), SM("sX1")]
        sXFr = [SM("sXF0"), SM("sXF1")]
        sW = [SM("sW0"), SM("sW1")]
        sWFr = [SM("sWF0"), SM("sWF1")]
        sInit = SM("sInit")
        sSq = SM("sSq")
        sCv = SM("sCv")
        sStP = SM("sStP")
        sStE = SM("sStE")
        sMath = SM("sMath")
        sBc = SM("sBc")
        sXn = SM("sXn")
        sXn2 = SM("sXn2")
        sM2c = SM("sM2c")
        sWc = SM("sWc")
        sQm = SM("sQm")
        sQe = SM("sQe")
        sKm = SM("sKm")
        sKe = SM("sKe")
        sVm = SM("sVm")
        sVe = SM("sVe")
        sGm = SM("sGm")
        sTg = SM("sTg")
        sSc = SM("sSc")
        sEx = SM("sEx")
        sMk = SM("sMk")
        sAv = SM("sAv")
        sRh = SM("sRh")
        sRhE = SM("sRhE")
        sRhB = SM("sRhB")
        sDm = SM("sDm")
        sDi = SM("sDi")
        sHo = SM("sHo")
        sOp = SM("sOp")
        sOpE = SM("sOpE")
        sOpS = SM("sOpS")
        sRS = SM("sRS")
        sRt = SM("sRt")
        sRtE = SM("sRtE")
        sTp = SM("sTp")
        sH1 = SM("sH1")
        sH1E = SM("sH1E")
        sH2 = SM("sH2")
        sH2E = SM("sH2E")
        sH3 = SM("sH3")
        sH3E = SM("sH3E")
        sO8 = SM("sO8")
        sO8E = SM("sO8E")
        sO8S = SM("sO8S")
        sAR = SM("sAR")
        sFin = SM("sFin")
        sFlb = SM("sFlb")
        sSt = [SM("sSt0"), SM("sSt1"), SM("sSt2")]
        sCst = SM("sCst")
        sWcst = SM("sWcst")
        sRsL = [SM("sRsL0"), SM("sRsL1")]
        sArL = SM("sArL")
        sOpSp = [SM("sOpS0"), SM("sOpS1")]
        sMs = SM("sMs")
        sVch = SM("sVch")
        sPgr = SM("sPgr")
        sPcv = SM("sPcv")
        sPcS = [SM("sPcS0"), SM("sPcS1"), SM("sPcS2"), SM("sPcS3")]
        sOwL = [SM("sOwL0"), SM("sOwL1")]
        sWL = [SM("sWL0"), SM("sWL1")]
        sW2L = [SM("sW2L0"), SM("sW2L1")]
        sAch = SM("sAch")

        prog = {k: [] for k in ("sync", "gpsimd", "tensor", "vector", "scalar")}

        def em(eng, fn):
            prog[eng].append(fn)

        def wt(eng, sem, v):
            if v > 0:
                em(eng, lambda e, s=sem.h, v=v: e.wait_ge(s, v))

        def mm(dst, lhsT, rhs, start, stop, sem=None, waits=(), sem2=None):
            for (s, v) in waits:
                wt("tensor", s, v)

            def f(e, dst=dst, lhsT=lhsT, rhs=rhs, start=start, stop=stop,
                  sem=sem, sem2=sem2):
                i = e.matmul(dst, lhsT, rhs, start=start, stop=stop,
                             skip_group_check=True)
                if sem is not None:
                    i = i.then_inc(sem.h, 1)
                if sem2 is not None:
                    i.then_inc(sem2, 1)
            em("tensor", f)
            if sem is not None:
                sem.n += 1

        def act(out_, in_, func, sem=None, waits=(), bias=None, scl=None,
                sem2=None):
            for (s, v) in waits:
                wt("scalar", s, v)

            def f(e, out_=out_, in_=in_, func=func, sem=sem, bias=bias,
                  scl=scl, sem2=sem2):
                kw_ = {}
                if bias is not None:
                    kw_["bias"] = f32(bias) if isinstance(bias, float) else bias
                if scl is not None:
                    kw_["scale"] = f32(scl)
                i = e.activation(out_, in_, func, **kw_)
                if sem is not None:
                    i = i.then_inc(sem.h, 1)
                if sem2 is not None:
                    i.then_inc(sem2, 1)
            em("scalar", f)
            if sem is not None:
                sem.n += 1

        def vop(name, *args, sem=None, waits=(), sem2=None, **kwargs):
            for (s, v) in waits:
                wt("vector", s, v)

            def f(e, name=name, args=args, sem=sem, kwargs=kwargs, sem2=sem2):
                i = getattr(e, name)(*args, **kwargs)
                if sem is not None:
                    i = i.then_inc(sem.h, 1)
                if sem2 is not None:
                    i.then_inc(sem2, 1)
            em("vector", f)
            if sem is not None:
                sem.n += 1

        def gop(name, *args, sem=None, waits=(), **kwargs):
            for (s, v) in waits:
                wt("gpsimd", s, v)

            def f(e, name=name, args=args, sem=sem, kwargs=kwargs):
                i = getattr(e, name)(*args, **kwargs)
                if sem is not None:
                    i.then_inc(sem.h, 1)
            em("gpsimd", f)
            if sem is not None:
                sem.n += 1

        def dma(eng, out_, in_, sem, waits=(), accum=False):
            for (s, v) in waits:
                wt(eng, s, v)

            def f(e, out_=out_, in_=in_, sem=sem, accum=accum):
                kw_ = dict(accum_op=ALU.add) if accum else {}
                e.dma_start(out=out_, in_=in_, **kw_).then_inc(sem.h, 16)
            em(eng, f)
            sem.n += 16

        def collective(kind, op, groups, in_ap, out_ap, sem, waits=()):
            for (s, v) in waits:
                wt("gpsimd", s, v)

            def f(e, kind=kind, op=op, groups=groups, in_ap=in_ap,
                  out_ap=out_ap, sem=sem):
                e.collective_compute(kind, op, replica_groups=groups,
                                     ins=[in_ap], outs=[out_ap]).then_inc(sem.h)
            em("gpsimd", f)
            sem.n += 1

        def pe_milestone():
            em("tensor", lambda e, s=sMs.h: e.nop().then_inc(s, 1))
            sMs.n += 1
            return sMs.n

        class Stream:
            def __init__(self, queue, bufs, sLoad, sFree):
                self.q, self.bufs = queue, bufs
                self.sL = sLoad
                self.cnt = [0, 0]
                self.pending = [[], []]

            def load(self, src, cols, rows=128):
                p = (self.cnt[0] + self.cnt[1]) % 2
                w = self.pending[p]
                self.pending[p] = []
                dma(self.q, self.bufs[p][0:rows, 0:cols], src, self.sL[p], waits=w)
                self.cnt[p] += 1
                return self.bufs[p], p, self.sL[p].n

            def set_free(self, p, gates):
                self.pending[p] = list(gates)

        xs = Stream("sync", XS, sX, sXFr)
        ws = Stream("gpsimd", WS, sW, sWFr)
        WSC = 1024

        pconv_units = []
        for hhh in range(HPC):
            for er_ in range(NER):
                pconv_units.append(
                    (ow[hhh * 128:(hhh + 1) * 128, er_ * EW:(er_ + 1) * EW],
                     owbf[er_][:, hhh * EW:(hhh + 1) * EW], EW))
        for er in range(NERH):
            tot = NCI * EWH
            for off in range(0, tot, WSC):
                wdt = min(WSC, tot - off)
                pconv_units.append(
                    (w2[er].rearrange("p c e -> p (c e)")[:, off:off + wdt],
                     w2bf[er][:, off:off + wdt], wdt))
        pconv_state = [0]

        def emit_pconv(n):
            for _ in range(n):
                k = pconv_state[0]
                if k >= len(pconv_units):
                    return
                pconv_state[0] = k + 1
                srcap, dstap, wdt = pconv_units[k]
                buf, p, lv = xs.load(srcap, wdt)
                pp = k % 4
                slot = opst[pp // 2][:, (pp % 2) * 1024:(pp % 2) * 1024 + wdt]
                wv = [(sX[p], lv)]
                if k >= 4:
                    wv.append((sPcS[pp], 16 * (k // 4)))
                else:
                    wv.append((sMath, p2_state['nmr2']))
                gop("tensor_copy", slot, buf[:, 0:wdt], sem=sPcv, waits=wv)
                xs.set_free(p, [(sPcv, sPcv.n)])
                dma("sync", dstap, slot, sPcS[pp],
                    waits=[(sPcv, sPcv.n)])

        def pconv_done_gate():
            assert pconv_state[0] == len(pconv_units), \
                f"pconv incomplete {pconv_state[0]}/{len(pconv_units)}"
            return [(s, s.n) for s in sPcS]

        def load_conv(dst_fn, src_fn, total, first_waits):
            for off in range(0, total, WSC):
                wdt = min(WSC, total - off)
                buf, p, lv = ws.load(src_fn(off, wdt), wdt)
                vop("tensor_copy", dst_fn(off, wdt), buf[:, 0:wdt], sem=sWc,
                    waits=[(sW[p], lv)] +
                          (list(first_waits) if off == 0 else []))
                ws.set_free(p, [(sWc, sWc.n)])
            return sWc.n

        XCACHE_OK = NC * CB <= HPC * S
        xcache = lambda c: RB[:, c * CB:(c + 1) * CB]
        xcb_scratch = [qsb[:, 0:CB], ksb[:, 0:CB]]
        xsq_buf = [tgb[:, 0:CB], RD[:, 0:CB]]

        dma("sync", mask_sb[:, :], mk, sCst)
        dma("sync", gbs[:, :], gb, sCst)
        dma("sync", sela[:, :], sel_a, sCst)
        dma("sync", selb[:, :], sel_b, sCst)
        cl0 = sCst.n
        dma("gpsimd", WS[0][:, 0:128], gw, sWcst)
        dma("gpsimd", WS[0][:, 128:128 + NC * 8], wr, sWcst)
        cl1 = sWcst.n
        ws.cnt[0] = 1

        vop("memset", onesD[:, :], recipD)
        vop("memset", ones1[:, :], 1.0)
        vop("memset", onesC[:, :], 1.0)
        vop("memset", ones1b[:, :], 1.0)
        vop("tensor_scalar", gbn[:, :], gbs[:, :], -1.0, None, op0=ALU.mult,
            waits=[(sCst, cl0)])
        vop("tensor_copy", gwb[:, :], WS[0][:, 0:128], waits=[(sWcst, cl1)])
        vop("tensor_copy", wrb[:, :], WS[0][:, 128:128 + NC * 8], sem=sInit)
        ws.set_free(0, [(sInit, sInit.n)])

        vw_flat0 = vw[0].rearrange("p c e -> p (c e)")
        hoisted_vconv = load_conv(
            lambda off, wdt: vwb[:, off:off + wdt],
            lambda off, wdt: vw_flat0[:, off:off + wdt],
            VG * NC * 128, [])

        p1_last = {}

        def pass1_chunk(src_ap, c, bcols, base, CBx, SWx, gate,
                        mt=None, et=None, ebase=None, cache=None,
                        cache_gate=(), sq=None, gp_copy=False):
            mt = psA if mt is None else mt
            et = psB if et is None else et
            ebase = base if ebase is None else ebase
            buf, p, lv = xs.load(src_ap[c * 128:(c + 1) * 128, bcols], CBx)
            xcb = cache if cache is not None else xcb_scratch[p]
            xsq = sq if sq is not None else xsq_buf[p]
            w_prev = ([(sStP, p1_last[p])] if p in p1_last else [])
            cop = gop if gp_copy else vop
            cop("tensor_copy", xcb[:, 0:CBx], buf[:, 0:CBx], sem=sCv,
                waits=[(sX[p], lv)] + w_prev + list(cache_gate))
            act(xsq[:, 0:CBx], buf[:, 0:CBx], AF.Square, sem=sSq,
                waits=[(sX[p], lv)] + w_prev)
            xs.set_free(p, [(sCv, sCv.n), (sSq, sSq.n)])
            cv_v, sq_v = sCv.n, sSq.n
            nrr = CBx // SWx
            for rr in range(nrr):
                first = (c == 0)
                last = (c == NC - 1)
                mm(mt[0:1, base + rr * SWx: base + (rr + 1) * SWx],
                   onesD[:, :], xcb[:, rr * SWx:(rr + 1) * SWx], first, last,
                   waits=(([(sCv, cv_v), (sInit, 1)] + (gate if c == 0 else []))
                          if rr == 0 else []))
                mm(et[0:1, ebase + rr * SWx: ebase + (rr + 1) * SWx],
                   onesD[:, :], xsq[:, rr * SWx:(rr + 1) * SWx], first, last,
                   sem=(sStP if rr == nrr - 1 else None),
                   waits=([(sSq, sq_v)] if rr == 0 else []))
            p1_last[p] = sStP.n

        def math_block(m_ap, e2_ap, st_ap, base, width, evac_gate,
                       mt=None, et=None, ebase=None):
            mt = psA if mt is None else mt
            et = psB if et is None else et
            ebase = base if ebase is None else ebase
            act(m_ap, mt[0:1, base:base + width], AF.Copy, sem=sStE,
                waits=evac_gate)
            act(e2_ap, et[0:1, ebase:ebase + width], AF.Copy, sem=sStE)
            stv = sStE.n
            act(st_ap, m_ap, AF.Square, sem=sSq, waits=[(sStE, stv)])
            sqv = sSq.n
            vop("scalar_tensor_tensor", e2_ap, e2_ap, EPS, st_ap,
                op0=ALU.add, op1=ALU.subtract, sem=sMath,
                waits=[(sSq, sqv), (sStE, stv)])
            mtv = sMath.n
            act(e2_ap, e2_ap, AF.Ln, waits=[(sMath, mtv)], sem=sAch)
            act(e2_ap, e2_ap, AF.Exp, scl=-0.5, sem=sStE,
                waits=[(sAch, sAch.n)])
            rv = sStE.n
            vop("tensor_tensor", m_ap, m_ap, e2_ap, op=ALU.mult,
                waits=[(sStE, rv)], sem=sVch)
            vop("tensor_scalar", m_ap, m_ap, -1.0, None, op0=ALU.mult,
                sem=sMath, waits=[(sVch, sVch.n)])
            return sMath.n

        def bcast_block(m_ap, e2_ap, base, width, nmr_v):
            for r in range(width // 512):
                mm(psB[:, base + r * 512: base + (r + 1) * 512], ones1[0:1, :],
                   e2_ap[0:1, r * 512:(r + 1) * 512], True, True,
                   waits=([(sMath, nmr_v)] if r == 0 else []))
                mm(psA[:, base + r * 512: base + (r + 1) * 512],
                   ones1b[0:1, :],
                   m_ap[0:1, r * 512:(r + 1) * 512], True, True, sem=sBc)
            return sBc.n

        def pass2_chunk(c, bcols, base, CBx, dst_ap, bc_v, tmp):
            if XCACHE_OK:
                vop("tensor_tensor", tmp[:, 0:CBx], xcache(c)[:, 0:CBx],
                    psB[:, base:base + CBx], op=ALU.mult,
                    waits=[(sBc, bc_v)], sem=sVch)
                vop("tensor_tensor", dst_ap, tmp[:, 0:CBx],
                    psA[:, base:base + CBx], op=ALU.add, sem=sXn,
                    waits=[(sVch, sVch.n)])
            else:
                buf, p, lv = xs.load(xT[c * 128:(c + 1) * 128, bcols], CBx)
                vop("tensor_tensor", buf[:, 0:CBx], buf[:, 0:CBx],
                    psB[:, base:base + CBx], op=ALU.mult,
                    waits=[(sX[p], lv), (sBc, bc_v)], sem=sVch)
                vop("tensor_tensor", dst_ap, buf[:, 0:CBx],
                    psA[:, base:base + CBx], op=ALU.add, sem=sXn,
                    waits=[(sVch, sVch.n)])
                xs.set_free(p, [(sXn, sXn.n)])

        xn_after_block = []
        xn_chunk = {}
        for c in range(NC):
            pass1_chunk(xT, c, slice(0, CB), 0, CB, SW, [],
                        cache=(xcache(c) if XCACHE_OK else None))
        for h in range(XH):
            base = (h % 2) * 1024
            bc = slice(h * CB, (h + 1) * CB)
            mb = m_bf[0:1, h * CB:(h + 1) * CB]
            eb = e2f[0:1, h * CB:(h + 1) * CB]
            stb = stmp[0:1, h * CB:(h + 1) * CB]
            nmr_v = math_block(mb, eb, stb, base, CB, [(sStP, sStP.n)])
            bc_v = bcast_block(mb, eb, base, CB, nmr_v)
            nb = (h + 1) % 2 * 1024
            for c in range(NC):
                pass2_chunk(c, bc, base, CB, xnT(c)[:, bc], bc_v,
                            xcb_scratch[c % 2])
                xn_chunk[(h, c)] = sXn.n
                if h + 1 < XH:
                    g = ([(sXn, xn_after_block[h - 1])] if h >= 1 else [])
                    pass1_chunk(xT, c, slice((h + 1) * CB, (h + 2) * CB),
                                nb, CB, SW, g,
                                cache=(xcache(c) if XCACHE_OK else None),
                                cache_gate=[(sXn, sXn.n)])
            xn_after_block.append(sXn.n)
        xn_all = sXn.n
        assert XH2 == 1 and S2 <= 1024
        p2_state = {}

        def emit_p2():
            for c in range(NC):
                pass1_chunk(xf, c, slice(0, CB2), 0, CB2, SW2,
                            [(sXn, xn_all)] if c == 0 else [],
                            mt=psB, et=psB, ebase=1024,
                            sq=tgb[:, (c % 2) * CB2:(c % 2 + 1) * CB2],
                            gp_copy=True)
            p2_state['nmr2'] = math_block(
                m2_bf, e2f2, stmp[0:1, 0:S2], 0, S2,
                [(sStP, sStP.n)], mt=psB, et=psB, ebase=1024)
            p2_state['st2'] = sStE.n

        expv = {}
        gctr = [0]
        grpG = {}
        gmark = {}
        avbank_gate = {0: 0, 1: 0}
        tail_pending = []

        def flush_tail():
            while tail_pending:
                tail_pending.pop(0)()
        prev_head_psA_free = None
        VGW = VG * 128
        per_ps = max(1, 2048 // VGW)

        vconv_next = [None] * (NB + 1)
        vconv_next[0] = hoisted_vconv
        qkconv_next = [None, -1]
        for vb in range(NB):
            if vconv_next[vb] is None:
                vw_flat = vw[vb].rearrange("p c e -> p (c e)")
                vconv = load_conv(
                    lambda off, wdt: vwb[:, off:off + wdt],
                    lambda off, wdt: vw_flat[:, off:off + wdt],
                    NC * VGW,
                    ([(sVm, vb * NKT), (sQm, sQm.n), (sKm, sKm.n)]
                     if vb > 0 else []))
            else:
                vconv = vconv_next[vb]
            sve0 = sVe.n
            TBK = CB // 128
            for tt in range(NKT):
                col = (tt % 2) * 512
                w = []
                if tt == 0:
                    w += [(sWc, vconv)]
                if vb == 0 and tt % TBK == 0:
                    w.append((sXn, xn_after_block[min(tt // TBK, XH - 1)]))
                elif vb > 0 and tt == 0:
                    w.append((sXn, xn_all))
                if tt >= 2:
                    w.append((sVe, sve0 + tt - 1))
                for c in range(NC):
                    mm(psA[:, col:col + VGW],
                       xnT(c)[:, tt * 128:(tt + 1) * 128],
                       vwb[:, c * VGW:(c + 1) * VGW],
                       c == 0, c == NC - 1,
                       sem=(sVm if c == NC - 1 else None),
                       waits=(w if c == 0 else []))
                wv = [(sVm, sVm.n)]
                if vb > 0 and tt == 0:
                    wv.append((sAv, sAv.n))
                act(vsb(tt), psA[:, col:col + VGW], AF.Copy, sem=sVe, waits=wv)

            prev_head_psA_free = (sVe, sVe.n)
            if vb == 0:
                emit_p2()

            for j_in in range(VG):
                hh = vb * VG + j_in
                if qkconv_next[0] is not None and qkconv_next[1] == hh:
                    qconv, kconv = qkconv_next[0]
                    qkconv_next[0] = None
                else:
                    qconv = load_conv(
                        lambda off, wdt: qwb[:, off:off + wdt],
                        lambda off, wdt, hh=hh: qw[hh][:, off:off + wdt],
                        NC * 128,
                        [(sQm, sQm.n)] + [(sVm, (vb + 1) * NKT)])
                    kconv = load_conv(
                        lambda off, wdt: kwb[:, off:off + wdt],
                        lambda off, wdt, hh=hh: kw[hh][:, off:off + wdt],
                        NC * 128,
                        [(sKm, sKm.n)] + [(sVm, (vb + 1) * NKT)])

                for n in range(NQ):
                    w0 = []
                    if n == 0:
                        w0 += [(sWc, qconv), prev_head_psA_free]
                    if hh == 0:
                        w0.append((sXn,
                                   xn_after_block[min(n * QW // CB, XH - 1)]))
                    for c in range(NC):
                        mm(psA[:, n * QW:(n + 1) * QW],
                           qwb[:, c * 128:(c + 1) * 128],
                           xnT(c)[:, n * QW:(n + 1) * QW],
                           c == 0, c == NC - 1,
                           sem=(sQm if c == NC - 1 else None),
                           waits=(w0 if c == 0 else []))
                    act(qsb[:, n * QW:(n + 1) * QW],
                        psA[:, n * QW:(n + 1) * QW], AF.Copy, sem=sQe,
                        waits=[(sQm, sQm.n)])
                qm = sQm.n
                qe = sQe.n

                for n in range(NQ):
                    mm(psA[:, n * QW:(n + 1) * QW],
                       gwb[:, :],
                       qsb[:, n * QW:(n + 1) * QW], True, True,
                       sem=(sGm if n == NQ - 1 else None),
                       waits=([(sQe, qe), (sInit, 1)] if n == 0 else []))
                gm = sGm.n
                flush_tail()
                act(tgb[:, 0:S], psA[:, 0:S], AF.Exp, sem=sTg,
                    waits=[(sGm, gm), (sDm, sDm.n)], bias=gbn[:, :], scl=-1.0)
                tg = sTg.n

                for n in range(NQ):
                    w0 = []
                    if n == 0:
                        w0 += [(sWc, kconv), (sHo, sHo.n), (sVe, sVe.n),
                               (sStE, p2_state['st2'])]
                    if hh == 0:
                        w0.append((sXn,
                                   xn_after_block[min(n * QW // CB, XH - 1)]))
                    for c in range(NC):
                        mm(psB[:, n * QW:(n + 1) * QW],
                           kwb[:, c * 128:(c + 1) * 128],
                           xnT(c)[:, n * QW:(n + 1) * QW],
                           c == 0, c == NC - 1,
                           sem=(sKm if (n == NQ - 1 and c == NC - 1) else None),
                           waits=(w0 if c == 0 else []))
                km = sKm.n
                act(ksb[:, 0:S], psB[:, 0:S], AF.Copy, sem=sKe,
                    waits=[(sKm, km)])
                ke = sKe.n

                spread = []
                if j_in == VG - 1 and vb + 1 < NB:
                    vwf2 = vw[vb + 1].rearrange("p c e -> p (c e)")

                    def vdone(v, vb=vb):
                        vconv_next[vb + 1] = v
                    spread.append((
                        lambda off, wdt: vwb[:, off:off + wdt],
                        lambda off, wdt, vwf2=vwf2: vwf2[:, off:off + wdt],
                        NC * VGW,
                        [(sVm, (vb + 1) * NKT), (sQm, qm), (sKm, km)],
                        vdone))
                elif j_in + 1 < VG:
                    hn = hh + 1
                    qk_acc = {}

                    def qdone(v, qk_acc=qk_acc, hn=hn):
                        qk_acc['q'] = v
                        if 'k' in qk_acc:
                            qkconv_next[0] = (qk_acc['q'], qk_acc['k'])
                            qkconv_next[1] = hn

                    def kdone(v, qk_acc=qk_acc, hn=hn):
                        qk_acc['k'] = v
                        if 'q' in qk_acc:
                            qkconv_next[0] = (qk_acc['q'], qk_acc['k'])
                            qkconv_next[1] = hn
                    spread.append((
                        lambda off, wdt: qwb[:, off:off + wdt],
                        lambda off, wdt, hn=hn: qw[hn][:, off:off + wdt],
                        NC * 128, [(sQm, qm)], qdone))
                    spread.append((
                        lambda off, wdt: kwb[:, off:off + wdt],
                        lambda off, wdt, hn=hn: kw[hn][:, off:off + wdt],
                        NC * 128, [(sKm, km)], kdone))
                spread_pieces = []
                for dstf, srcf, tot, gates, cb_ in spread:
                    for off in range(0, tot, WSC):
                        wdt = min(WSC, tot - off)
                        last = off + wdt >= tot
                        spread_pieces.append(
                            (dstf, srcf, off, wdt, gates if off == 0 else [],
                             cb_ if last else None))

                spread_loaded = []

                def emit_spread_load(n):
                    for _ in range(n):
                        if not spread_pieces or len(spread_loaded) >= 2:
                            return
                        dstf, srcf, off, wdt, gates, cb_ = \
                            spread_pieces.pop(0)
                        buf, p, lv = ws.load(srcf(off, wdt), wdt)
                        spread_loaded.append(
                            (dstf, off, wdt, gates, cb_, buf, p, lv))

                def spread_drain():
                    while spread_pieces or spread_loaded:
                        emit_spread_conv(1)
                        emit_spread_load(1)

                def emit_spread_conv(n):
                    for _ in range(n):
                        if not spread_loaded:
                            return
                        dstf, off, wdt, gates, cb_, buf, p, lv = \
                            spread_loaded.pop(0)
                        vop("tensor_copy", dstf(off, wdt), buf[:, 0:wdt],
                            sem=sWc, waits=[(sW[p], lv)] + list(gates))
                        ws.set_free(p, [(sWc, sWc.n)])
                        if cb_ is not None:
                            cb_(sWc.n)

                npc_tot = len(spread_pieces)
                emit_spread_load((npc_tot + NQ - 1) // NQ)

                for j in range(NQ):
                    nkt_j = KPR * (j + 1)
                    ng = nkt_j // 2
                    av_col = 0 if (j % 2 == 0) else 1536
                    avps = psB[:, av_col:av_col + QW]
                    rhops = psB[0:1, 512:512 + QW]
                    rhobps = psB[:, 1024:1024 + QW]
                    ho_gate = avbank_gate[j % 2]

                    def emit_avrho(g, j=j, j_in=j_in, nkt_j=nkt_j, avps=avps,
                                   rhops=rhops, hh=hh, ho_gate=ho_gate):
                        G = grpG[(hh, j, g)]
                        base = (G % NSLOT) * 2 * QW
                        for kk in range(2):
                            kt = 2 * g + kk
                            diag = kt >= KPR * j
                            if diag:
                                gate, gv = sMk, expv[(hh, j, g, "mk", kk)]
                            else:
                                gate, gv = sEx, expv[(hh, j, g, "ex")]
                            w = [(gate, gv)]
                            if kt == 0 and ho_gate > 0:
                                w.append((sHo, ho_gate))
                            mm(avps,
                               vsb(kt)[:, j_in * 128:(j_in + 1) * 128],
                               probsT[:, base + kk * QW: base + (kk + 1) * QW],
                               kt == 0, kt == nkt_j - 1,
                               sem=(sAv if kt == nkt_j - 1 else None),
                               waits=w)
                            w2_ = [(sRhE, sRhE.n)] if kt == 0 else []
                            last = (kt == nkt_j - 1)
                            mm(rhops, onesC[:, :],
                               probsT[:, base + kk * QW: base + (kk + 1) * QW],
                               kt == 0, last,
                               sem=(sRh if last else
                                    (sPgr if kk == 1 else None)),
                               waits=w2_)
                            if kk == 1:
                                gmark[G] = ((sRh, sRh.n) if last
                                            else (sPgr, sPgr.n))

                    for g in range(ng):
                        gp = g % 2
                        G = gctr[0]
                        gctr[0] += 1
                        grpG[(hh, j, g)] = G
                        base = (G % NSLOT) * 2 * QW
                        w = []
                        if g == 0 and j == 0:
                            w += [(sTg, tg)]
                        if g >= 2:
                            w.append((sEx, expv[(hh, j, g - 2, "ex")]))
                        elif j > 0 and g < 2:
                            w.append((sEx, expv[(hh, j - 1, "last_ex")]))
                        if g == 0 and j == 0:
                            w.append((sKe, ke))
                        for kk in range(2):
                            kt = 2 * g + kk
                            mm(psA[:, gp * 1024 + kk * 512:
                                   gp * 1024 + kk * 512 + QW],
                               ksb[:, kt * 128:(kt + 1) * 128],
                               qsb[:, j * QW:(j + 1) * QW],
                               True, True,
                               sem=(sSc if kk == 1 else None),
                               waits=(w if kk == 0 else []))
                        scv = sSc.n
                        if g == 1:
                            flush_tail()
                        we = [(sSc, scv)]
                        if G >= NSLOT:
                            we.append(gmark[G - NSLOT])
                        act(probsT[:, base: base + 2 * QW],
                            psA[:, gp * 1024: gp * 1024 + 2 * QW],
                            AF.Exp, sem=sEx, waits=we,
                            bias=CSHIFT, scl=SCALE)
                        expv[(hh, j, g, "ex")] = sEx.n
                        if 2 * g + 1 >= KPR * j:
                            d0 = 2 * g - KPR * j
                            for kk2 in range(2):
                                vop("tensor_tensor",
                                    probsT[:, base + kk2 * QW:
                                           base + (kk2 + 1) * QW],
                                    probsT[:, base + kk2 * QW:
                                           base + (kk2 + 1) * QW],
                                    mask_sb[:, (d0 + kk2) * 512:
                                            (d0 + kk2 + 1) * 512],
                                    op=ALU.mult, sem=sMk,
                                    waits=[(sEx, sEx.n)] if kk2 == 0 else [])
                                expv[(hh, j, g, "mk", kk2)] = sMk.n
                        if g >= 1:
                            emit_avrho(g - 1)
                    expv[(hh, j, "last_ex")] = sEx.n
                    emit_avrho(ng - 1)
                    avv = sAv.n
                    rhv = sRh.n
                    act(rho_sb[0:1, 0:QW], rhops, AF.Copy, sem=sRhE,
                        waits=[(sRh, rhv)])
                    rhev = sRhE.n

                    def emit_tail(j=j, hh=hh, rhev=rhev, avv=avv, tg=tg,
                                  avps=avps, rhobps=rhobps):
                        mm(rhobps, ones1[0:1, :], rho_sb[0:1, 0:QW],
                           True, True, sem=sRhB,
                           waits=[(sRhE, rhev), (sDm, sDm.n)])
                        rhbv = sRhB.n
                        vop("scalar_tensor_tensor", Dsb[:, 0:QW],
                            tgb[:, j * QW:(j + 1) * QW], 1.0, rhobps,
                            op0=ALU.add, op1=ALU.mult, sem=sDm,
                            waits=[(sRhB, rhbv), (sTg, tg), (sDi, sDi.n)])
                        dmv = sDm.n
                        act(Dsb[:, 0:QW], Dsb[:, 0:QW], AF.Ln,
                            waits=[(sDm, dmv), (sHo, sHo.n)], sem=sAch)
                        act(Dinv[:, 0:QW], Dsb[:, 0:QW], AF.Exp, scl=-1.0,
                            sem=sDi, waits=[(sAch, sAch.n)])
                        div = sDi.n
                        vop("tensor_tensor", hob(hh)[:, j * QW:(j + 1) * QW],
                            avps, Dinv[:, 0:QW], op=ALU.mult, sem=sHo,
                            waits=[(sDi, div), (sAv, avv)])
                        avbank_gate[j % 2] = sHo.n
                    tail_pending.append(emit_tail)
                    nu = len(pconv_units)
                    rdone = hh * NQ + j + 1
                    emit_pconv((nu * rdone) // (HPC * NQ)
                               - (nu * (rdone - 1)) // (HPC * NQ))
                    if j < NQ - 1:
                        cv_t = (npc_tot * (j + 1) + NQ - 1) // NQ
                        ld_t = min(npc_tot,
                                   (npc_tot * (j + 2) + NQ - 1) // NQ)
                        done_cv = npc_tot - len(spread_pieces) \
                            - len(spread_loaded)
                        emit_spread_conv(cv_t - done_cv)
                        done_ld = npc_tot - len(spread_pieces)
                        emit_spread_load(ld_t - done_ld)
                    else:
                        spread_drain()
                spread_drain()
                prev_head_psA_free = (sEx, expv[(hh, NQ - 1, "last_ex")])
        flush_tail()
        ho_all = sHo.n
        qm_all, km_all, vm_all = sQm.n, sKm.n, sVm.n

        for n in range(NST2):
            mm(psB[:, n * SW2:(n + 1) * SW2], ones1[0:1, :],
               e2f2[0:1, n * SW2:(n + 1) * SW2], True, True,
               waits=([(sMath, p2_state['nmr2']), (sHo, ho_all), (sDm, sDm.n)]
                      if n == 0 else []))
            mm(psA[:, n * SW2:(n + 1) * SW2], ones1b[0:1, :],
               m2_bf[0:1, n * SW2:(n + 1) * SW2], True, True, sem=sBc,
               waits=([(sEx, expv[(HPC - 1, NQ - 1, "last_ex")])]
                      if n == 0 else []))
        bc2 = sBc.n
        act(r2b_sb[:, 0:S2], psB[:, 0:S2], AF.Copy, sem=sM2c,
            waits=[(sBc, bc2)])
        act(nm2b_sb[:, 0:S2], psA[:, 0:S2], AF.Copy, sem=sM2c)
        m2c = sM2c.n
        for c in range(NC):
            for h in range(XH2):
                buf, p, lv = xs.load(
                    xf[c * 128:(c + 1) * 128, h * CB2:(h + 1) * CB2], CB2)
                cols = slice(h * CB2, (h + 1) * CB2)
                vop("tensor_tensor", buf[:, 0:CB2], buf[:, 0:CB2],
                    r2b_sb[:, cols], op=ALU.mult,
                    waits=[(sX[p], lv), (sM2c, m2c)], sem=sVch)
                vop("tensor_tensor", xnF(c)[:, cols], buf[:, 0:CB2],
                    nm2b_sb[:, cols], op=ALU.add, sem=sXn2,
                    waits=[(sVch, sVch.n)] +
                          ([(sQm, qm_all), (sKm, km_all), (sVm, vm_all)]
                           if (c == 0 and h == 0) else []))
                xs.set_free(p, [(sXn2, sXn2.n)])
        xn2_all = sXn2.n

        G8 = min(4, QT)
        QH = QT // G8
        opev_done = {}
        store_done = {}
        oper_done = {}

        ow_segs = {0: [(qsb, S), (ksb, S)],
                   1: [(tgb, S), (rssb[0], D2), (rssb[1], D2)]}

        def seg_place(segs, off):
            for t, w_ in segs:
                if off < w_:
                    return t, off
                off -= w_
            raise AssertionError("owq overflow")

        def segs_ok(segs):
            off = 0
            for hh_ in range(HPC):
                try:
                    t, o = seg_place(segs, hh_ * EW)
                except AssertionError:
                    return False
                if o + EW > dict((id(a), b) for a, b in segs)[id(t)]:
                    return False
            return True

        ow_dbuf = segs_ok(ow_segs[1]) and NER >= 2

        def owq(par, hh):
            t, o = seg_place(ow_segs[par], hh * EW)
            return t[:, o:o + EW]

        op_store_q = []

        def flush_op_stores():
            while op_store_q:
                op_store_q.pop(0)()

        def grp_nstores(er):
            n = 0
            e0 = er * EW
            for r in range(2):
                lo = max(e0, r * D2)
                hi = min(e0 + EW, (r + 1) * D2)
                if lo < hi:
                    n += 1
            return n * G8

        store_plan = {}
        _cnt = [0, 0]
        for _er in range(NER):
            for _qh in range(QH):
                _g = _er * QH + _qh
                _cnt[_g % 2] += 16 * grp_nstores(_er)
                store_plan[_g] = _cnt[_g % 2]

        for er in range(NER):
            par = (er % 2) if ow_dbuf else 0
            wv = []
            if er < (2 if ow_dbuf else 1):
                wv += [(sSc, sSc.n)] + pconv_done_gate()
                if par == 1:
                    wv += [(sDm, sDm.n), (sBc, sBc.n)]
            else:
                wv.append((sOp, oper_done[er - (2 if ow_dbuf else 1)]))
            off = 0
            first_dma = True
            while off < HPC * EW:
                t, o = seg_place(ow_segs[par], off)
                run = EW
                while (off + run < HPC * EW):
                    t2, o2 = seg_place(ow_segs[par], off + run)
                    if t2 is t and o2 == o + run:
                        run += EW
                    else:
                        break
                dma("gpsimd", t[:, o:o + run],
                    owbf[er][:, off:off + run],
                    sOwL[par], waits=(wv if first_dma else []))
                first_dma = False
                off += run
            owlv = sOwL[par].n
            flush_op_stores()
            for qh in range(QH):
                grp = er * QH + qh
                for hhh in range(HPC):
                    for q8 in range(G8):
                        qt = qh * G8 + q8
                        ps = psA if grp % 2 == 0 else psB
                        col = q8 * 512
                        w = []
                        if q8 == 0 and hhh == 0:
                            w.append((sOwL[par], owlv))
                            if grp == 0:
                                w += [(sHo, ho_all), (sM2c, m2c)]
                            elif grp == 1:
                                w.append((sHo, ho_all))
                            else:
                                w.append((sOpE, opev_done[grp - 2]))
                        mm(ps[:, col:col + EW],
                           hob(hhh)[:, qt * 128:(qt + 1) * 128],
                           owq(par, hhh),
                           hhh == 0, hhh == HPC - 1,
                           sem=(sOp if (hhh == HPC - 1 and q8 == G8 - 1)
                                else None),
                           waits=w)
                opv = sOp.n
                ps = psA if grp % 2 == 0 else psB
                wv2 = [(sOp, opv)]
                if grp >= 2:
                    wv2.append((sOpSp[grp % 2], store_plan[grp - 2]))
                act(opst[grp % 2][:, 0:G8 * EW], ps[:, 0:G8 * EW], AF.Copy,
                    sem=sOpE, waits=wv2)
                opev = sOpE.n

                def emit_stores(qh=qh, er=er, grp=grp, opev=opev):
                    first_store = True
                    for q8 in range(G8):
                        qt = qh * G8 + q8
                        e0 = er * EW
                        for r in range(2):
                            lo = max(e0, r * D2)
                            hi = min(e0 + EW, (r + 1) * D2)
                            if lo >= hi:
                                continue
                            dma("gpsimd",
                                op_dram[r * S + qt * 128:
                                        r * S + (qt + 1) * 128,
                                        lo - r * D2:hi - r * D2],
                                opst[grp % 2][:, q8 * EW + lo - e0:
                                              q8 * EW + hi - e0],
                                sOpSp[grp % 2],
                                waits=([(sOpE, opev)] if first_store else []))
                            first_store = False
                    store_done[grp] = sOpSp[grp % 2].n
                op_store_q.append(emit_stores)
                opev_done[grp] = sOpE.n
            oper_done[er] = sOp.n
        flush_op_stores()
        collective("ReduceScatter", ALU.add,
                   [[0, 1], [2, 3], [4, 5], [6, 7]],
                   op_dram.ap().opt(), rs_dram.ap().opt(), sRS,
                   waits=[(sOpSp[0], sOpSp[0].n), (sOpSp[1], sOpSp[1].n)])

        opev_all = sOpE.n
        for tt in range(TT2):
            bank = tt % 8
            ps = psA if bank < 4 else psB
            col = (bank % 4) * 512
            w = [(sXn2, xn2_all)]
            if tt == 0:
                w.append((sOpE, opev_all))
            if tt >= 8:
                w.append((sRtE, sRtE.n - 7))
            for c in range(NC):
                mm(ps[:, col:col + 8],
                   xnF(c)[:, tt * 128:(tt + 1) * 128],
                   wrb[:, c * 8:(c + 1) * 8],
                   c == 0, c == NC - 1,
                   sem=(sRt if c == NC - 1 else None),
                   waits=(w if c == 0 else []))
            rtv = sRt.n
            act(rt_sb[:, tt * 8:(tt + 1) * 8], ps[:, col:col + 8],
                AF.Copy, sem=sRtE, waits=[(sRt, rtv)])
        rte = sRtE.n
        act(rt_sb[:, :], rt_sb[:, :], AF.Exp, sem=sRtE, waits=[(sRtE, rte)])
        rtexp = sRtE.n
        rt3 = rt_sb[:, :].rearrange("p (t e) -> p t e", e=8)
        lo3 = r_lo[:, :].rearrange("p (t e) -> p t e", e=8)
        kp3 = r_kp[:, :].rearrange("p (t e) -> p t e", e=8)
        g3 = g_sb[:, :].rearrange("p (t e) -> p t e", e=8)
        m1b3 = r_m1[:, :].rearrange("p (t o) -> p t o", o=1).broadcast_to(
            [128, TT2, 8])
        m2b3 = r_m2[:, :].rearrange("p (t o) -> p t o", o=1).broadcast_to(
            [128, TT2, 8])
        dnb3 = r_dn[:, :].rearrange("p (t o) -> p t o", o=1).broadcast_to(
            [128, TT2, 8])
        vop("tensor_reduce", r_m1[:, :], rt3, AX.X, ALU.max, sem=sTp,
            waits=[(sRtE, rtexp)])
        vop("tensor_tensor", lo3, rt3, m1b3, op=ALU.is_lt, sem=sTp,
            waits=[(sTp, sTp.n)])
        vop("tensor_tensor", lo3, rt3, lo3, op=ALU.mult, sem=sTp,
            waits=[(sTp, sTp.n)])
        vop("tensor_reduce", r_m2[:, :], lo3, AX.X, ALU.max, sem=sTp,
            waits=[(sTp, sTp.n)])
        vop("tensor_tensor", kp3, rt3, m2b3, op=ALU.is_ge, sem=sTp,
            waits=[(sTp, sTp.n)])
        vop("tensor_tensor", r_dn[:, :], r_m1[:, :], r_m2[:, :], op=ALU.add,
            sem=sTp, waits=[(sTp, sTp.n)])
        vop("reciprocal", r_dn[:, :], r_dn[:, :], sem=sTp,
            waits=[(sTp, sTp.n)])
        vop("tensor_tensor", kp3, kp3, dnb3, op=ALU.mult, sem=sTp,
            waits=[(sTp, sTp.n)])
        vop("tensor_tensor", g3, rt3, kp3, op=ALU.mult, sem=sTp,
            waits=[(sTp, sTp.n)])
        gdone = sTp.n

        finale_stores = []
        st_totals = {}
        n_done = 0

        def emit_finale_tile(qt):
            nonlocal n_done
            n = n_done
            pr = n % 3
            wrs = [(sRS, 1)] if n == 0 else []
            if n >= 2:
                wrs.append((sFin, sFin.n - 1))
            dma("sync", rssb[n % 2][:, 0:D2],
                rs_dram[qt * 128:(qt + 1) * 128, :], sRsL[n % 2], waits=wrs)
            rsv = sRsL[n % 2].n
            buf, p, lv = xs.load(xr[qt * 128:(qt + 1) * 128, :], D2)
            wadd = [(sX[p], lv), (sRsL[n % 2], rsv)]
            if n >= 3:
                wadd.append((sSt[pr], 16 * (n // 3)))
            vop("tensor_tensor", outsb[pr][:, 0:D2], buf[:, 0:D2],
                rssb[n % 2][:, 0:D2], op=ALU.add, sem=sFin, waits=wadd)
            xs.set_free(p, [(sFin, sFin.n)])
            fv = sFin.n

            def st(qt=qt, pr=pr, fv=fv):
                dma("gpsimd", out[qt * 128:(qt + 1) * 128, :],
                    outsb[pr][:, 0:D2], sSt[pr],
                    waits=[(sFin, fv), (sSt[pr], sSt[pr].n)])
                st_totals[pr] = sSt[pr].n
            finale_stores.append(st)
            n_done += 1

        def flush_finale_stores(k=None):
            while finale_stores and (k is None or k > 0):
                finale_stores.pop(0)()
                if k is not None:
                    k -= 1

        def emit_finale_tail():
            n = n_done
            pr = n % 3
            dma("sync", rssb[n % 2][:, 0:D2], rs_dram[0:128, :],
                sRsL[n % 2], waits=[(sFin, sFin.n - 1)])
            rsv = sRsL[n % 2].n
            buf, p, lv = xs.load(xr[0:128, :], D2)
            wadd = [(sX[p], lv), (sRsL[n % 2], rsv),
                    (sSt[pr], 16 * (n // 3))]
            vop("tensor_tensor", outsb[pr][:, 0:D2], buf[:, 0:D2],
                rssb[n % 2][:, 0:D2], op=ALU.add, sem=sFin, waits=wadd)
            xs.set_free(p, [(sFin, sFin.n)])
            dma("gpsimd", o8sb[0:8, :], o8r_dram[:, :], sArL,
                waits=[(sAR, 1)])
            vop("tensor_scalar", fad[0:8, :], o8sb[0:8, D2:2 * D2],
                selb[0:8, :], None, op0=ALU.mult,
                waits=[(sArL, sArL.n), (sCst, cl0),
                       (sSt[2], st_totals.get(2, 0))], sem=sVch)
            vop("scalar_tensor_tensor", fad[0:8, :], o8sb[0:8, 0:D2],
                sela[0:8, :], fad[0:8, :], op0=ALU.mult, op1=ALU.add,
                sem=sVch, waits=[(sVch, sVch.n)])
            vop("tensor_tensor", outsb[pr][0:8, 0:D2],
                outsb[pr][0:8, 0:D2], fad[0:8, :], op=ALU.add, sem=sFin,
                waits=[(sVch, sVch.n)])
            fv = sFin.n
            dma("gpsimd", out[0:128, :], outsb[pr][:, 0:D2], sSt[pr],
                waits=[(sFin, fv)])
            for s_ in sSt:
                wt("gpsimd", s_, s_.n)


        def ffn_layer(wsrc, rhs_fn, dst_fn, sMM, sEV, gate_first, nch,
                      conv_gate=()):
            mm0 = sMM.n
            ev0 = sEV.n
            for it in range(NIT):
                wcw = []
                if it >= 2:
                    wcw.append((sMM, mm0 + it - 1))
                elif conv_gate:
                    wcw.append((conv_gate[0], conv_gate[1] - 1 + it))
                wcv = load_conv(
                    lambda off, wdt, it=it: w13b[it % 2][:, off:off + wdt],
                    lambda off, wdt, it=it: wsrc[it][:, off:off + wdt],
                    nch * 128, wcw)
                ps = psA if it % 2 == 0 else psB
                w = [(sWc, wcv)] + (list(gate_first) if it == 0 else [])
                if it >= 2:
                    w.append((sEV, ev0 + it - 1))
                for n in range(NF):
                    for c in range(nch):
                        mm(ps[:, n * FW:(n + 1) * FW],
                           w13b[it % 2][:, c * 128:(c + 1) * 128],
                           rhs_fn(c)[:, n * FW:(n + 1) * FW],
                           c == 0, c == nch - 1,
                           sem=(sMM if (n == NF - 1 and c == nch - 1) else None),
                           waits=(w if (n == 0 and c == 0) else []))
                mmv = sMM.n
                act(dst_fn(it)[:, 0:S2], ps[:, 0:S2], AF.Silu, sem=sEV,
                    waits=[(sMM, mmv)])

        h1mm0 = sH1.n
        ffn_layer(w1, xnF, h1sb, sH1, sH1E, [(sRtE, rte)], NC)
        h1e = sH1E.n
        ffn_layer(w3, h1sb, h2sb, sH2, sH2E, [(sH1E, h1e)], NCI,
                  conv_gate=(sH1, h1mm0 + NIT))
        h2e = sH2E.n

        for qt in range(1, QT):
            emit_finale_tile(qt)

        h3mm = {}
        for er in range(NERH):
            wbuf = RD if er % 2 == 0 else vwb
            w2c = lambda ci, wbuf=wbuf: wbuf[:, ci * EWH:(ci + 1) * EWH]
            w = []
            if er < 2:
                w += [(sHo, ho_all), (sQm, qm_all), (sKm, km_all),
                      (sVm, vm_all)] + pconv_done_gate()
            else:
                w.append((sH3, h3mm[er - 2]))
            dma("gpsimd", wbuf[:, 0:NCI * EWH], w2bf[er],
                sW2L[er % 2], waits=w)
            w2lv = sW2L[er % 2].n
            if finale_stores:
                flush_finale_stores(-(-len(finale_stores) // (NERH - er)))
            for tt in range(TT2):
                ps = psA if tt % 2 == 0 else psB
                w = [(sW2L[er % 2], w2lv)]
                if er == 0 and tt == 0:
                    w.append((sH2E, h2e))
                if tt >= 2 or er > 0:
                    w.append((sH3E, sH3E.n - 1))
                for ci in range(NCI):
                    mm(ps[:, 0:EWH],
                       h2sb(ci)[:, tt * 128:(tt + 1) * 128],
                       w2c(ci), ci == 0, ci == NCI - 1,
                       sem=(sH3 if ci == NCI - 1 else None),
                       waits=(w if ci == 0 else []))
                h3v = sH3.n
                wv = [(sH3, h3v)]
                if er == 0 and tt == 0:
                    wv.append((sH2, sH2.n))
                act(h3sb(tt)[:, er * EWH:(er + 1) * EWH],
                    ps[:, 0:EWH], AF.Copy, sem=sH3E,
                    waits=wv)
            h3mm[er] = sH3.n
        h3e = sH3E.n

        for er in range(NERH):
            ps = psA if er < 4 else psB
            col = (er % 4) * 512
            for tt in range(TT2):
                mm(ps[0:8, col:col + EWH],
                   g_sb[:, tt * 8:(tt + 1) * 8],
                   h3sb(tt)[:, er * EWH:(er + 1) * EWH],
                   tt == 0, tt == TT2 - 1,
                   sem=(sO8 if tt == TT2 - 1 else None),
                   waits=([(sH3E, h3e), (sTp, gdone)]
                          if (er == 0 and tt == 0) else []))
        o8v = sO8.n
        for er in range(NERH):
            ps = psA if er < 4 else psB
            col = (er % 4) * 512
            act(o8sb[0:8, er * EWH:(er + 1) * EWH], ps[0:8, col:col + EWH],
                AF.Copy, sem=sO8E,
                waits=([(sO8, o8v)] if er == 0 else []))
        dma("gpsimd", o8_dram[:, :], o8sb[0:8, :], sO8S,
            waits=[(sO8E, sO8E.n)])
        collective("AllReduce", ALU.add, [[0, 1, 2, 3, 4, 5, 6, 7]],
                   o8_dram.ap().opt(), o8r_dram.ap().opt(), sAR,
                   waits=[(sO8S, sO8S.n)])
        flush_finale_stores()

        emit_finale_tail()

        block = es.enter_context(nc.Block())

        @block.sync
        def _(e):
            for fn in prog["sync"]:
                fn(e)

        @block.gpsimd
        def _(e):
            for fn in prog["gpsimd"]:
                fn(e)

        @block.tensor
        def _(e):
            for fn in prog["tensor"]:
                fn(e)

        @block.vector
        def _(e):
            for fn in prog["vector"]:
                fn(e)

        @block.scalar
        def _(e):
            for fn in prog["scalar"]:
                fn(e)

    return nc


def _block_w(wT_sl, n_out_blk, out_blk):
    Dd = wT_sl.shape[0]
    ncc = Dd // 128
    a = wT_sl.reshape(ncc, 128, n_out_blk, out_blk)
    return np.ascontiguousarray(a.transpose(2, 1, 0, 3).reshape(
        n_out_blk, 128, ncc * out_blk))


def _mask_const():
    m = np.zeros((128, 4 * 512), dtype=np.float32)
    kk = np.arange(128)[:, None]
    qq = np.arange(512)[None, :]
    for d in range(4):
        m[:, d * 512:(d + 1) * 512] = (kk <= (qq - 128 * d))
    return m.astype(bf16)


def make_inputs(x, q_w, k_w, v_w, o_w, gate_w, gate_b, w1_, w3_, w2_, wr_,
                S, D, H, INTER):
    HPC = H // 2
    NC = D // 128
    NIT = INTER // 128
    NCI = INTER // 128
    VG = min(2, HPC)
    NB = HPC // VG
    D2 = D // 2
    NER = max(1, D // 512)
    EW = D // NER
    EWH = min(256, D)
    NERH = D // EWH
    S2 = S // 2

    qwT = np.ascontiguousarray(q_w.T)
    kwT = np.ascontiguousarray(k_w.T)
    vwT = np.ascontiguousarray(v_w.T)
    owT = np.ascontiguousarray(o_w.T)
    w1T = np.ascontiguousarray(w1_.T)
    w3T = np.ascontiguousarray(w3_.T)
    w2T = np.ascontiguousarray(w2_.T)
    wrT = np.ascontiguousarray(wr_.T)
    gwT = np.ascontiguousarray(gate_w.T)
    mask = _mask_const()

    in_maps = []
    for core in range(8):
        b, g = core // 2, core % 2
        hw = HPC * 128
        d = {
            "xT": np.ascontiguousarray(x[b].T),
            "xf": np.ascontiguousarray(x[b, g * S2:(g + 1) * S2, :].T),
            "xr": np.ascontiguousarray(x[b][:, g * D2:(g + 1) * D2]),
            "qw": _block_w(qwT[:, g * hw:(g + 1) * hw], HPC, 128),
            "kw": _block_w(kwT[:, g * hw:(g + 1) * hw], HPC, 128),
            "vw": _block_w(vwT[:, g * hw:(g + 1) * hw], NB, VG * 128)
                  .reshape(NB, 128, NC, VG * 128),
            "gw": gwT,
            "gb": np.ascontiguousarray(gate_b.reshape(128, 1)),
            "ow": np.ascontiguousarray(owT[g * hw:(g + 1) * hw, :]),
            "w1": _block_w(w1T, NIT, 128),
            "w3": _block_w(w3T, NIT, 128),
            "w2": _block_w(w2T, NERH, EWH).reshape(NERH, 128, NCI, EWH),
            "wr": np.ascontiguousarray(
                wrT.reshape(NC, 128, 8).transpose(1, 0, 2)
                .reshape(128, NC * 8)),
            "mk": mask,
            "sel_a": np.full((8, 1), 1.0 if core == 0 else 0.0, np.float32),
            "sel_b": np.full((8, 1), 1.0 if core == 1 else 0.0, np.float32),
        }
        in_maps.append(d)
    return in_maps


def kernel(L0_attn_x, L0_attn_q_w, L0_attn_k_w, L0_attn_v_w, L0_attn_o_w,
           L0_attn_gate_w, L0_attn_gate_b, L0_ffn_w1, L0_ffn_w3, L0_ffn_w2,
           L0_ffn_wr):
    S, D, H, INTER = FULL["S"], FULL["D"], FULL["H"], FULL["INTER"]
    x = np.asarray(L0_attn_x, np.float32)
    in_maps = make_inputs(
        x, np.asarray(L0_attn_q_w, np.float32),
        np.asarray(L0_attn_k_w, np.float32),
        np.asarray(L0_attn_v_w, np.float32),
        np.asarray(L0_attn_o_w, np.float32),
        np.asarray(L0_attn_gate_w, np.float32),
        np.asarray(L0_attn_gate_b, np.float32),
        np.asarray(L0_ffn_w1, np.float32),
        np.asarray(L0_ffn_w3, np.float32),
        np.asarray(L0_ffn_w2, np.float32),
        np.asarray(L0_ffn_wr, np.float32),
        S, D, H, INTER)
    nc = build_program(S, D, H, INTER)
    res = run_bass_kernel_spmd(nc, in_maps, core_ids=list(range(8))).results
    D2 = D // 2
    full = np.empty((4, S, D), np.float32)
    for core in range(8):
        b, g = core // 2, core % 2
        full[b, :, g * D2:(g + 1) * D2] = res[core]["out"]
    return full


# version 25
# speedup vs baseline: 1.0734x; 1.0155x over previous
import math
from contextlib import ExitStack

import numpy as np
import ml_dtypes

import concourse.bass as bass
import concourse.mybir as mybir
from concourse.bass_utils import run_bass_kernel_spmd

F32 = mybir.dt.float32
BF16 = mybir.dt.bfloat16
AF = mybir.ActivationFunctionType
ALU = mybir.AluOpType
AX = mybir.AxisListType

f32 = np.float32
bf16 = ml_dtypes.bfloat16

FULL = dict(S=2048, D=2048, H=16, INTER=2048)


class Sem:
    def __init__(self, nc, es, name):
        self.h = es.enter_context(nc.semaphore(name))
        self.n = 0


def build_program(S, D, H, INTER):
    HPC = H // 2
    NC = D // 128
    NQ = max(1, S // 512)
    QW = S // NQ
    NKT = S // 128
    KPR = NKT // NQ
    S2 = S // 2
    NF = max(1, S2 // 512)
    FW = S2 // NF
    TT2 = S2 // 128
    NIT = INTER // 128
    NCI = INTER // 128
    VG = min(2, HPC)
    NB = HPC // VG
    D2 = D // 2
    NER = max(1, D // 512)
    EW = D // NER
    EWH = min(256, D)
    NERH = D // EWH
    CB = min(1024, S)
    XH = S // CB
    CB2 = min(1024, S2)
    XH2 = S2 // CB2
    NST = max(1, S // 512)
    SW = S // NST
    NST2 = max(1, S2 // 512)
    SW2 = S2 // NST2
    QT = S // 128
    SCALE = float(1.0 / math.sqrt(128.0))
    CSHIFT = -16.0
    EPS = 1e-5
    recipD = 1.0 / D

    nc = bass.Bass("TRN2", target_bir_lowering=False)

    dI = lambda n, sh, dt=F32: nc.dram_tensor(n, sh, dt, kind="ExternalInput").ap()
    xT = dI("xT", [D, S])
    xf = dI("xf", [D, S2])
    xr = dI("xr", [S, D2])
    qw = dI("qw", [HPC, 128, NC * 128])
    kw = dI("kw", [HPC, 128, NC * 128])
    vw = dI("vw", [NB, 128, NC, VG * 128])
    gw = dI("gw", [128, 128])
    gb = dI("gb", [128, 1])
    ow = dI("ow", [HPC * 128, D])
    w1 = dI("w1", [NIT, 128, NC * 128])
    w3 = dI("w3", [NIT, 128, NCI * 128])
    w2 = dI("w2", [NERH, 128, NCI, EWH])
    wr = dI("wr", [128, NC * 8])
    mk = dI("mk", [128, 4 * 512], BF16)
    sel_a = dI("sel_a", [8, 1])
    sel_b = dI("sel_b", [8, 1])

    out = nc.dram_tensor("out", [S, D2], F32, kind="ExternalOutput").ap()

    owbf = nc.dram_tensor("owbf", [NER, 128, HPC * EW], BF16)
    w2bf = nc.dram_tensor("w2bf", [NERH, 128, NCI * EWH], BF16)
    op_dram = nc.dram_tensor("op_dram", [2 * S, D2], BF16)
    rs_dram = nc.dram_tensor("rs_dram", [S, D2], BF16)
    o8_dram = nc.dram_tensor("o8_dram", [8, D], F32)
    o8r_dram = nc.dram_tensor("o8r_dram", [8, D], F32)
    o8g_dram = nc.dram_tensor("o8g_dram", [64, D], F32)

    with ExitStack() as es:
        sbuf = lambda n, sh, dt: es.enter_context(nc.sbuf_tensor(n, sh, dt))

        RA = sbuf("RA", [128, NC * S], BF16)
        xnT = lambda c: RA[:, c * S:(c + 1) * S]
        HALF = NC * S // 2
        xnF = lambda c: RA[:, c * S2:(c + 1) * S2]
        h1sb = lambda i: RA[:, HALF + i * S2: HALF + (i + 1) * S2]
        h3sb = lambda t: RA[:, HALF + t * D: HALF + (t + 1) * D]

        RB = sbuf("RB", [128, HPC * S], BF16)
        hob = lambda hh: RB[:, hh * S:(hh + 1) * S]
        h2sb = lambda i: RB[:, i * S2:(i + 1) * S2]

        NSLOT = 4
        RC = sbuf("RC", [128, NSLOT * 2 * QW], BF16)
        probsT = RC

        RD = sbuf("RD", [128, NKT * VG * 128], BF16)
        vsb = lambda tt: RD[:, tt * VG * 128:(tt + 1) * VG * 128]

        assert HPC >= 2, "need >=2 heads per core for buffer aliasing"
        vwb = sbuf("vwb", [128, NC * VG * 128], BF16)
        qwb = vwb[:, 0:NC * 128]
        kwb = vwb[:, NC * 128:2 * NC * 128]
        w13b = [sbuf("w13b0", [128, NC * 128], BF16),
                sbuf("w13b1", [128, NC * 128], BF16)]
        RDW = NKT * VG * 128

        def w2b(ci):
            off = ci * EW
            if off < RDW:
                return RD[:, off:off + EW]
            return vwb[:, off - RDW:off - RDW + EW]
        gwb = sbuf("gwb", [128, 128], BF16)
        owx = [sbuf("owx0", [128, 512], BF16), sbuf("owx1", [128, 512], BF16)]
        wrb = sbuf("wrb", [128, NC * 8], BF16)

        WS = [sbuf("WS0", [128, 1024], F32), sbuf("WS1", [128, 1024], F32)]
        XS = [sbuf("XS0", [128, CB], F32), sbuf("XS1", [128, CB], F32)]

        qsb = sbuf("qsb", [128, S], BF16)
        ksb = sbuf("ksb", [128, S], BF16)
        tgb = sbuf("tgb", [128, S], BF16)

        scrD = sbuf("scrD", [128, 2 * QW], F32)
        Dsb = scrD[:, 0:QW]
        Dinv = scrD[:, QW:2 * QW]
        nmr2b2 = sbuf("nmr2b2", [128, 2 * S2], F32)
        nm2b_sb = nmr2b2[:, 0:S2]
        r2b_sb = nmr2b2[:, S2:2 * S2]
        outs2 = sbuf("outs2", [128, 2 * D2], F32)
        outsb = None
        OPW = min(4, QT) * EW
        opst = [sbuf("opst0", [128, OPW], BF16),
                sbuf("opst1", [128, OPW], BF16)]
        rssb = [sbuf("rssb0", [128, D2], BF16), sbuf("rssb1", [128, D2], BF16)]

        assert 2 * QW >= D2
        outsb = [outs2[:, 0:D2], outs2[:, D2:2 * D2], scrD[:, 0:D2]]
        m_bf = opst[0][0:1, 0:S]
        stmp = opst[1][0:1, 0:S]
        m2_bf = rssb[0][0:1, 0:S2]
        e2f = outs2[0:1, 0:S]
        e2f2 = outs2[0:1, 0:S2]
        rho_sb = sbuf("rho_sb", [1, QW], F32)

        rt_sb = sbuf("rt_sb", [128, TT2 * 8], F32)
        g_sb = sbuf("g_sb", [128, TT2 * 8], BF16)
        r_m1 = sbuf("r_m1", [128, TT2], F32)
        r_m2 = sbuf("r_m2", [128, TT2], F32)
        r_lo = sbuf("r_lo", [128, TT2 * 8], F32)
        r_kp = sbuf("r_kp", [128, TT2 * 8], F32)
        r_dn = sbuf("r_dn", [128, TT2], F32)

        onesD = sbuf("onesD", [128, 1], BF16)
        ones1 = sbuf("ones1", [1, 128], F32)
        onesC = sbuf("onesC", [128, 1], BF16)
        ones1b = sbuf("ones1b", [1, 128], BF16)
        mask_sb = sbuf("mask_sb", [128, 4 * 512], BF16)
        gbn = sbuf("gbn", [128, 1], F32)
        gbs = sbuf("gbs", [128, 1], F32)
        sela = sbuf("sela", [8, 1], F32)
        selb = sbuf("selb", [8, 1], F32)
        o8sb = nmr2b2[0:8, 0:D]
        fad = scrD[0:8, 0:D2]

        psA = es.enter_context(nc.psum_tensor("psA", [128, 2048], F32))
        psB = es.enter_context(nc.psum_tensor("psB", [128, 2048], F32))

        SM = lambda name: Sem(nc, es, name)
        sX = [SM("sX0"), SM("sX1")]
        sXFr = [SM("sXF0"), SM("sXF1")]
        sW = [SM("sW0"), SM("sW1")]
        sWFr = [SM("sWF0"), SM("sWF1")]
        sInit = SM("sInit")
        sSq = SM("sSq")
        sCv = SM("sCv")
        sStP = SM("sStP")
        sStE = SM("sStE")
        sMath = SM("sMath")
        sBc = SM("sBc")
        sXn = SM("sXn")
        sXn2 = SM("sXn2")
        sM2c = SM("sM2c")
        sWc = SM("sWc")
        sQm = SM("sQm")
        sQe = SM("sQe")
        sKm = SM("sKm")
        sKe = SM("sKe")
        sVm = SM("sVm")
        sVe = SM("sVe")
        sGm = SM("sGm")
        sTg = SM("sTg")
        sSc = SM("sSc")
        sEx = SM("sEx")
        sMk = SM("sMk")
        sAv = SM("sAv")
        sRh = SM("sRh")
        sRhE = SM("sRhE")
        sRhB = SM("sRhB")
        sDm = SM("sDm")
        sDi = SM("sDi")
        sHo = SM("sHo")
        sOp = SM("sOp")
        sOpE = SM("sOpE")
        sOpS = SM("sOpS")
        sRS = SM("sRS")
        sRt = SM("sRt")
        sRtE = SM("sRtE")
        sTp = SM("sTp")
        sH1 = SM("sH1")
        sH1E = SM("sH1E")
        sH2 = SM("sH2")
        sH2E = SM("sH2E")
        sH3 = SM("sH3")
        sH3E = SM("sH3E")
        sO8 = SM("sO8")
        sO8E = SM("sO8E")
        sO8S = SM("sO8S")
        sAR = SM("sAR")
        sFin = SM("sFin")
        sFlb = SM("sFlb")
        sSt = [SM("sSt0"), SM("sSt1"), SM("sSt2")]
        sCst = SM("sCst")
        sWcst = SM("sWcst")
        sRsL = [SM("sRsL0"), SM("sRsL1")]
        sArL = SM("sArL")
        sOpSp = [SM("sOpS0"), SM("sOpS1")]
        sMs = SM("sMs")
        sVch = SM("sVch")
        sPgr = SM("sPgr")
        sPcv = SM("sPcv")
        sPcS = [SM("sPcS0"), SM("sPcS1"), SM("sPcS2"), SM("sPcS3")]
        sOwL = [SM("sOwL0"), SM("sOwL1")]
        sWL = [SM("sWL0"), SM("sWL1")]
        sW2L = [SM("sW2L0"), SM("sW2L1")]
        sAch = SM("sAch")

        prog = {k: [] for k in ("sync", "gpsimd", "tensor", "vector", "scalar")}

        def em(eng, fn):
            prog[eng].append(fn)

        def wt(eng, sem, v):
            if v > 0:
                em(eng, lambda e, s=sem.h, v=v: e.wait_ge(s, v))

        def mm(dst, lhsT, rhs, start, stop, sem=None, waits=(), sem2=None):
            for (s, v) in waits:
                wt("tensor", s, v)

            def f(e, dst=dst, lhsT=lhsT, rhs=rhs, start=start, stop=stop,
                  sem=sem, sem2=sem2):
                i = e.matmul(dst, lhsT, rhs, start=start, stop=stop,
                             skip_group_check=True)
                if sem is not None:
                    i = i.then_inc(sem.h, 1)
                if sem2 is not None:
                    i.then_inc(sem2, 1)
            em("tensor", f)
            if sem is not None:
                sem.n += 1

        def act(out_, in_, func, sem=None, waits=(), bias=None, scl=None,
                sem2=None):
            for (s, v) in waits:
                wt("scalar", s, v)

            def f(e, out_=out_, in_=in_, func=func, sem=sem, bias=bias,
                  scl=scl, sem2=sem2):
                kw_ = {}
                if bias is not None:
                    kw_["bias"] = f32(bias) if isinstance(bias, float) else bias
                if scl is not None:
                    kw_["scale"] = f32(scl)
                i = e.activation(out_, in_, func, **kw_)
                if sem is not None:
                    i = i.then_inc(sem.h, 1)
                if sem2 is not None:
                    i.then_inc(sem2, 1)
            em("scalar", f)
            if sem is not None:
                sem.n += 1

        def vop(name, *args, sem=None, waits=(), sem2=None, **kwargs):
            for (s, v) in waits:
                wt("vector", s, v)

            def f(e, name=name, args=args, sem=sem, kwargs=kwargs, sem2=sem2):
                i = getattr(e, name)(*args, **kwargs)
                if sem is not None:
                    i = i.then_inc(sem.h, 1)
                if sem2 is not None:
                    i.then_inc(sem2, 1)
            em("vector", f)
            if sem is not None:
                sem.n += 1

        def gop(name, *args, sem=None, waits=(), **kwargs):
            for (s, v) in waits:
                wt("gpsimd", s, v)

            def f(e, name=name, args=args, sem=sem, kwargs=kwargs):
                i = getattr(e, name)(*args, **kwargs)
                if sem is not None:
                    i.then_inc(sem.h, 1)
            em("gpsimd", f)
            if sem is not None:
                sem.n += 1

        def dma(eng, out_, in_, sem, waits=(), accum=False):
            for (s, v) in waits:
                wt(eng, s, v)

            def f(e, out_=out_, in_=in_, sem=sem, accum=accum):
                kw_ = dict(accum_op=ALU.add) if accum else {}
                e.dma_start(out=out_, in_=in_, **kw_).then_inc(sem.h, 16)
            em(eng, f)
            sem.n += 16

        def collective(kind, op, groups, in_ap, out_ap, sem, waits=()):
            for (s, v) in waits:
                wt("gpsimd", s, v)

            def f(e, kind=kind, op=op, groups=groups, in_ap=in_ap,
                  out_ap=out_ap, sem=sem):
                e.collective_compute(kind, op, replica_groups=groups,
                                     ins=[in_ap], outs=[out_ap]).then_inc(sem.h)
            em("gpsimd", f)
            sem.n += 1

        def pe_milestone():
            em("tensor", lambda e, s=sMs.h: e.nop().then_inc(s, 1))
            sMs.n += 1
            return sMs.n

        class Stream:
            def __init__(self, queue, bufs, sLoad, sFree):
                self.q, self.bufs = queue, bufs
                self.sL = sLoad
                self.cnt = [0, 0]
                self.pending = [[], []]

            def load(self, src, cols, rows=128):
                p = (self.cnt[0] + self.cnt[1]) % 2
                w = self.pending[p]
                self.pending[p] = []
                dma(self.q, self.bufs[p][0:rows, 0:cols], src, self.sL[p], waits=w)
                self.cnt[p] += 1
                return self.bufs[p], p, self.sL[p].n

            def set_free(self, p, gates):
                self.pending[p] = list(gates)

        xs = Stream("sync", XS, sX, sXFr)
        ws = Stream("gpsimd", WS, sW, sWFr)
        WSC = 1024

        pconv_units = []
        for hhh in range(HPC):
            for er_ in range(NER):
                pconv_units.append(
                    (ow[hhh * 128:(hhh + 1) * 128, er_ * EW:(er_ + 1) * EW],
                     owbf[er_][:, hhh * EW:(hhh + 1) * EW], EW))
        for er in range(NERH):
            tot = NCI * EWH
            for off in range(0, tot, WSC):
                wdt = min(WSC, tot - off)
                pconv_units.append(
                    (w2[er].rearrange("p c e -> p (c e)")[:, off:off + wdt],
                     w2bf[er][:, off:off + wdt], wdt))
        pconv_state = [0]

        def emit_pconv(n):
            for _ in range(n):
                k = pconv_state[0]
                if k >= len(pconv_units):
                    return
                pconv_state[0] = k + 1
                srcap, dstap, wdt = pconv_units[k]
                buf, p, lv = xs.load(srcap, wdt)
                pp = k % 4
                slot = opst[pp // 2][:, (pp % 2) * 1024:(pp % 2) * 1024 + wdt]
                wv = [(sX[p], lv)]
                if k >= 4:
                    wv.append((sPcS[pp], 16 * (k // 4)))
                else:
                    wv.append((sMath, p2_state['nmr2']))
                gop("tensor_copy", slot, buf[:, 0:wdt], sem=sPcv, waits=wv)
                xs.set_free(p, [(sPcv, sPcv.n)])
                dma("sync", dstap, slot, sPcS[pp],
                    waits=[(sPcv, sPcv.n)])

        def pconv_done_gate():
            assert pconv_state[0] == len(pconv_units), \
                f"pconv incomplete {pconv_state[0]}/{len(pconv_units)}"
            return [(s, s.n) for s in sPcS]

        def load_conv(dst_fn, src_fn, total, first_waits):
            for off in range(0, total, WSC):
                wdt = min(WSC, total - off)
                buf, p, lv = ws.load(src_fn(off, wdt), wdt)
                vop("tensor_copy", dst_fn(off, wdt), buf[:, 0:wdt], sem=sWc,
                    waits=[(sW[p], lv)] +
                          (list(first_waits) if off == 0 else []))
                ws.set_free(p, [(sWc, sWc.n)])
            return sWc.n

        XCACHE_OK = NC * CB <= HPC * S
        xcache = lambda c: RB[:, c * CB:(c + 1) * CB]
        xcb_scratch = [qsb[:, 0:CB], ksb[:, 0:CB]]
        xsq_buf = [tgb[:, 0:CB], RD[:, 0:CB]]

        dma("sync", mask_sb[:, :], mk, sCst)
        dma("sync", gbs[:, :], gb, sCst)
        dma("sync", sela[:, :], sel_a, sCst)
        dma("sync", selb[:, :], sel_b, sCst)
        cl0 = sCst.n
        dma("gpsimd", WS[0][:, 0:128], gw, sWcst)
        dma("gpsimd", WS[0][:, 128:128 + NC * 8], wr, sWcst)
        cl1 = sWcst.n
        ws.cnt[0] = 1

        vop("memset", onesD[:, :], recipD)
        vop("memset", ones1[:, :], 1.0)
        vop("memset", onesC[:, :], 1.0)
        vop("memset", ones1b[:, :], 1.0)
        vop("tensor_scalar", gbn[:, :], gbs[:, :], -1.0, None, op0=ALU.mult,
            waits=[(sCst, cl0)])
        vop("tensor_copy", gwb[:, :], WS[0][:, 0:128], waits=[(sWcst, cl1)])
        vop("tensor_copy", wrb[:, :], WS[0][:, 128:128 + NC * 8], sem=sInit)
        ws.set_free(0, [(sInit, sInit.n)])

        vw_flat0 = vw[0].rearrange("p c e -> p (c e)")
        hoisted_vconv = load_conv(
            lambda off, wdt: vwb[:, off:off + wdt],
            lambda off, wdt: vw_flat0[:, off:off + wdt],
            VG * NC * 128, [])

        p1_last = {}

        def pass1_chunk(src_ap, c, bcols, base, CBx, SWx, gate,
                        mt=None, et=None, ebase=None, cache=None,
                        cache_gate=(), sq=None):
            mt = psA if mt is None else mt
            et = psB if et is None else et
            ebase = base if ebase is None else ebase
            buf, p, lv = xs.load(src_ap[c * 128:(c + 1) * 128, bcols], CBx)
            xcb = cache if cache is not None else xcb_scratch[p]
            xsq = sq if sq is not None else xsq_buf[p]
            w_prev = ([(sStP, p1_last[p])] if p in p1_last else [])
            vop("tensor_copy", xcb[:, 0:CBx], buf[:, 0:CBx], sem=sCv,
                waits=[(sX[p], lv)] + w_prev + list(cache_gate))
            act(xsq[:, 0:CBx], buf[:, 0:CBx], AF.Square, sem=sSq,
                waits=[(sX[p], lv)] + w_prev)
            xs.set_free(p, [(sCv, sCv.n), (sSq, sSq.n)])
            cv_v, sq_v = sCv.n, sSq.n
            nrr = CBx // SWx
            for rr in range(nrr):
                first = (c == 0)
                last = (c == NC - 1)
                mm(mt[0:1, base + rr * SWx: base + (rr + 1) * SWx],
                   onesD[:, :], xcb[:, rr * SWx:(rr + 1) * SWx], first, last,
                   waits=(([(sCv, cv_v), (sInit, 1)] + (gate if c == 0 else []))
                          if rr == 0 else []))
                mm(et[0:1, ebase + rr * SWx: ebase + (rr + 1) * SWx],
                   onesD[:, :], xsq[:, rr * SWx:(rr + 1) * SWx], first, last,
                   sem=(sStP if rr == nrr - 1 else None),
                   waits=([(sSq, sq_v)] if rr == 0 else []))
            p1_last[p] = sStP.n

        def math_block(m_ap, e2_ap, st_ap, base, width, evac_gate,
                       mt=None, et=None, ebase=None):
            mt = psA if mt is None else mt
            et = psB if et is None else et
            ebase = base if ebase is None else ebase
            act(m_ap, mt[0:1, base:base + width], AF.Copy, sem=sStE,
                waits=evac_gate)
            act(e2_ap, et[0:1, ebase:ebase + width], AF.Copy, sem=sStE)
            stv = sStE.n
            act(st_ap, m_ap, AF.Square, sem=sSq, waits=[(sStE, stv)])
            sqv = sSq.n
            vop("scalar_tensor_tensor", e2_ap, e2_ap, EPS, st_ap,
                op0=ALU.add, op1=ALU.subtract, sem=sMath,
                waits=[(sSq, sqv), (sStE, stv)])
            mtv = sMath.n
            act(e2_ap, e2_ap, AF.Ln, waits=[(sMath, mtv)], sem=sAch)
            act(e2_ap, e2_ap, AF.Exp, scl=-0.5, sem=sStE,
                waits=[(sAch, sAch.n)])
            rv = sStE.n
            vop("tensor_tensor", m_ap, m_ap, e2_ap, op=ALU.mult,
                waits=[(sStE, rv)], sem=sVch)
            vop("tensor_scalar", m_ap, m_ap, -1.0, None, op0=ALU.mult,
                sem=sMath, waits=[(sVch, sVch.n)])
            return sMath.n

        def bcast_block(m_ap, e2_ap, base, width, nmr_v):
            for r in range(width // 512):
                mm(psB[:, base + r * 512: base + (r + 1) * 512], ones1[0:1, :],
                   e2_ap[0:1, r * 512:(r + 1) * 512], True, True,
                   waits=([(sMath, nmr_v)] if r == 0 else []))
                mm(psA[:, base + r * 512: base + (r + 1) * 512],
                   ones1b[0:1, :],
                   m_ap[0:1, r * 512:(r + 1) * 512], True, True, sem=sBc)
            return sBc.n

        def pass2_chunk(c, bcols, base, CBx, dst_ap, bc_v, tmp):
            if XCACHE_OK:
                vop("tensor_tensor", tmp[:, 0:CBx], xcache(c)[:, 0:CBx],
                    psB[:, base:base + CBx], op=ALU.mult,
                    waits=[(sBc, bc_v)], sem=sVch)
                vop("tensor_tensor", dst_ap, tmp[:, 0:CBx],
                    psA[:, base:base + CBx], op=ALU.add, sem=sXn,
                    waits=[(sVch, sVch.n)])
            else:
                buf, p, lv = xs.load(xT[c * 128:(c + 1) * 128, bcols], CBx)
                vop("tensor_tensor", buf[:, 0:CBx], buf[:, 0:CBx],
                    psB[:, base:base + CBx], op=ALU.mult,
                    waits=[(sX[p], lv), (sBc, bc_v)], sem=sVch)
                vop("tensor_tensor", dst_ap, buf[:, 0:CBx],
                    psA[:, base:base + CBx], op=ALU.add, sem=sXn,
                    waits=[(sVch, sVch.n)])
                xs.set_free(p, [(sXn, sXn.n)])

        xn_after_block = []
        xn_chunk = {}
        for c in range(NC):
            pass1_chunk(xT, c, slice(0, CB), 0, CB, SW, [],
                        cache=(xcache(c) if XCACHE_OK else None))
        for h in range(XH):
            base = (h % 2) * 1024
            bc = slice(h * CB, (h + 1) * CB)
            mb = m_bf[0:1, h * CB:(h + 1) * CB]
            eb = e2f[0:1, h * CB:(h + 1) * CB]
            stb = stmp[0:1, h * CB:(h + 1) * CB]
            nmr_v = math_block(mb, eb, stb, base, CB, [(sStP, sStP.n)])
            bc_v = bcast_block(mb, eb, base, CB, nmr_v)
            nb = (h + 1) % 2 * 1024
            for c in range(NC):
                pass2_chunk(c, bc, base, CB, xnT(c)[:, bc], bc_v,
                            xcb_scratch[c % 2])
                xn_chunk[(h, c)] = sXn.n
                if h + 1 < XH:
                    g = ([(sXn, xn_after_block[h - 1])] if h >= 1 else [])
                    pass1_chunk(xT, c, slice((h + 1) * CB, (h + 2) * CB),
                                nb, CB, SW, g,
                                cache=(xcache(c) if XCACHE_OK else None),
                                cache_gate=[(sXn, sXn.n)])
            xn_after_block.append(sXn.n)
        xn_all = sXn.n
        assert XH2 == 1 and S2 <= 1024
        p2_state = {}

        def emit_p2():
            for c in range(NC):
                pass1_chunk(xf, c, slice(0, CB2), 0, CB2, SW2,
                            [(sXn, xn_all)] if c == 0 else [],
                            mt=psB, et=psB, ebase=1024,
                            sq=tgb[:, (c % 2) * CB2:(c % 2 + 1) * CB2])
            p2_state['nmr2'] = math_block(
                m2_bf, e2f2, stmp[0:1, 0:S2], 0, S2,
                [(sStP, sStP.n)], mt=psB, et=psB, ebase=1024)
            p2_state['st2'] = sStE.n

        expv = {}
        gctr = [0]
        grpG = {}
        gmark = {}
        avbank_gate = {0: 0, 1: 0}
        tail_pending = []

        def flush_tail():
            while tail_pending:
                tail_pending.pop(0)()
        prev_head_psA_free = None
        VGW = VG * 128
        per_ps = max(1, 2048 // VGW)

        vconv_next = [None] * (NB + 1)
        vconv_next[0] = hoisted_vconv
        qkconv_next = [None, -1]
        for vb in range(NB):
            if vconv_next[vb] is None:
                vw_flat = vw[vb].rearrange("p c e -> p (c e)")
                vconv = load_conv(
                    lambda off, wdt: vwb[:, off:off + wdt],
                    lambda off, wdt: vw_flat[:, off:off + wdt],
                    NC * VGW,
                    ([(sVm, vb * NKT), (sQm, sQm.n), (sKm, sKm.n)]
                     if vb > 0 else []))
            else:
                vconv = vconv_next[vb]
            sve0 = sVe.n
            TBK = CB // 128
            for tt in range(NKT):
                col = (tt % 2) * 512
                w = []
                if tt == 0:
                    w += [(sWc, vconv)]
                if vb == 0 and tt % TBK == 0:
                    w.append((sXn, xn_after_block[min(tt // TBK, XH - 1)]))
                elif vb > 0 and tt == 0:
                    w.append((sXn, xn_all))
                if tt >= 2:
                    w.append((sVe, sve0 + tt - 1))
                for c in range(NC):
                    mm(psA[:, col:col + VGW],
                       xnT(c)[:, tt * 128:(tt + 1) * 128],
                       vwb[:, c * VGW:(c + 1) * VGW],
                       c == 0, c == NC - 1,
                       sem=(sVm if c == NC - 1 else None),
                       waits=(w if c == 0 else []))
                wv = [(sVm, sVm.n)]
                if vb > 0 and tt == 0:
                    wv.append((sAv, sAv.n))
                act(vsb(tt), psA[:, col:col + VGW], AF.Copy, sem=sVe, waits=wv)

            prev_head_psA_free = (sVe, sVe.n)
            if vb == 0:
                hh0 = vb * VG
                qc0 = load_conv(
                    lambda off, wdt: qwb[:, off:off + wdt],
                    lambda off, wdt, hh0=hh0: qw[hh0][:, off:off + wdt],
                    NC * 128,
                    [(sQm, sQm.n)] + [(sVm, (vb + 1) * NKT)])
                kc0 = load_conv(
                    lambda off, wdt: kwb[:, off:off + wdt],
                    lambda off, wdt, hh0=hh0: kw[hh0][:, off:off + wdt],
                    NC * 128,
                    [(sKm, sKm.n)] + [(sVm, (vb + 1) * NKT)])
                qkconv_next[0] = (qc0, kc0)
                qkconv_next[1] = hh0
                emit_p2()

            for j_in in range(VG):
                hh = vb * VG + j_in
                if qkconv_next[0] is not None and qkconv_next[1] == hh:
                    qconv, kconv = qkconv_next[0]
                    qkconv_next[0] = None
                else:
                    qconv = load_conv(
                        lambda off, wdt: qwb[:, off:off + wdt],
                        lambda off, wdt, hh=hh: qw[hh][:, off:off + wdt],
                        NC * 128,
                        [(sQm, sQm.n)] + [(sVm, (vb + 1) * NKT)])
                    kconv = load_conv(
                        lambda off, wdt: kwb[:, off:off + wdt],
                        lambda off, wdt, hh=hh: kw[hh][:, off:off + wdt],
                        NC * 128,
                        [(sKm, sKm.n)] + [(sVm, (vb + 1) * NKT)])

                for n in range(NQ):
                    w0 = []
                    if n == 0:
                        w0 += [(sWc, qconv), prev_head_psA_free]
                    if hh == 0:
                        w0.append((sXn,
                                   xn_after_block[min(n * QW // CB, XH - 1)]))
                    for c in range(NC):
                        mm(psA[:, n * QW:(n + 1) * QW],
                           qwb[:, c * 128:(c + 1) * 128],
                           xnT(c)[:, n * QW:(n + 1) * QW],
                           c == 0, c == NC - 1,
                           sem=(sQm if c == NC - 1 else None),
                           waits=(w0 if c == 0 else []))
                    act(qsb[:, n * QW:(n + 1) * QW],
                        psA[:, n * QW:(n + 1) * QW], AF.Copy, sem=sQe,
                        waits=[(sQm, sQm.n)])
                qm = sQm.n
                qe = sQe.n

                for n in range(NQ):
                    mm(psA[:, n * QW:(n + 1) * QW],
                       gwb[:, :],
                       qsb[:, n * QW:(n + 1) * QW], True, True,
                       sem=(sGm if n == NQ - 1 else None),
                       waits=([(sQe, qe), (sInit, 1)] if n == 0 else []))
                gm = sGm.n
                flush_tail()
                act(tgb[:, 0:S], psA[:, 0:S], AF.Exp, sem=sTg,
                    waits=[(sGm, gm), (sDm, sDm.n)], bias=gbn[:, :], scl=-1.0)
                tg = sTg.n

                for n in range(NQ):
                    w0 = []
                    if n == 0:
                        w0 += [(sWc, kconv), (sHo, sHo.n), (sVe, sVe.n),
                               (sStE, p2_state['st2'])]
                    if hh == 0:
                        w0.append((sXn,
                                   xn_after_block[min(n * QW // CB, XH - 1)]))
                    for c in range(NC):
                        mm(psB[:, n * QW:(n + 1) * QW],
                           kwb[:, c * 128:(c + 1) * 128],
                           xnT(c)[:, n * QW:(n + 1) * QW],
                           c == 0, c == NC - 1,
                           sem=(sKm if (n == NQ - 1 and c == NC - 1) else None),
                           waits=(w0 if c == 0 else []))
                km = sKm.n
                act(ksb[:, 0:S], psB[:, 0:S], AF.Copy, sem=sKe,
                    waits=[(sKm, km)])
                ke = sKe.n

                spread = []
                if j_in == VG - 1 and vb + 1 < NB:
                    vwf2 = vw[vb + 1].rearrange("p c e -> p (c e)")

                    def vdone(v, vb=vb):
                        vconv_next[vb + 1] = v
                    spread.append((
                        lambda off, wdt: vwb[:, off:off + wdt],
                        lambda off, wdt, vwf2=vwf2: vwf2[:, off:off + wdt],
                        NC * VGW,
                        [(sVm, (vb + 1) * NKT), (sQm, qm), (sKm, km)],
                        vdone))
                elif j_in + 1 < VG:
                    hn = hh + 1
                    qk_acc = {}

                    def qdone(v, qk_acc=qk_acc, hn=hn):
                        qk_acc['q'] = v
                        if 'k' in qk_acc:
                            qkconv_next[0] = (qk_acc['q'], qk_acc['k'])
                            qkconv_next[1] = hn

                    def kdone(v, qk_acc=qk_acc, hn=hn):
                        qk_acc['k'] = v
                        if 'q' in qk_acc:
                            qkconv_next[0] = (qk_acc['q'], qk_acc['k'])
                            qkconv_next[1] = hn
                    spread.append((
                        lambda off, wdt: qwb[:, off:off + wdt],
                        lambda off, wdt, hn=hn: qw[hn][:, off:off + wdt],
                        NC * 128, [(sQm, qm)], qdone))
                    spread.append((
                        lambda off, wdt: kwb[:, off:off + wdt],
                        lambda off, wdt, hn=hn: kw[hn][:, off:off + wdt],
                        NC * 128, [(sKm, km)], kdone))
                spread_pieces = []
                for dstf, srcf, tot, gates, cb_ in spread:
                    for off in range(0, tot, WSC):
                        wdt = min(WSC, tot - off)
                        last = off + wdt >= tot
                        spread_pieces.append(
                            (dstf, srcf, off, wdt, gates if off == 0 else [],
                             cb_ if last else None))

                spread_loaded = []

                def emit_spread_load(n):
                    for _ in range(n):
                        if not spread_pieces or len(spread_loaded) >= 2:
                            return
                        dstf, srcf, off, wdt, gates, cb_ = \
                            spread_pieces.pop(0)
                        buf, p, lv = ws.load(srcf(off, wdt), wdt)
                        spread_loaded.append(
                            (dstf, off, wdt, gates, cb_, buf, p, lv))

                def spread_drain():
                    while spread_pieces or spread_loaded:
                        emit_spread_conv(1)
                        emit_spread_load(1)

                def emit_spread_conv(n):
                    for _ in range(n):
                        if not spread_loaded:
                            return
                        dstf, off, wdt, gates, cb_, buf, p, lv = \
                            spread_loaded.pop(0)
                        vop("tensor_copy", dstf(off, wdt), buf[:, 0:wdt],
                            sem=sWc, waits=[(sW[p], lv)] + list(gates))
                        ws.set_free(p, [(sWc, sWc.n)])
                        if cb_ is not None:
                            cb_(sWc.n)

                npc_tot = len(spread_pieces)
                emit_spread_load((npc_tot + NQ - 1) // NQ)

                for j in range(NQ):
                    nkt_j = KPR * (j + 1)
                    ng = nkt_j // 2
                    av_col = 0 if (j % 2 == 0) else 1536
                    avps = psB[:, av_col:av_col + QW]
                    rhops = psB[0:1, 512:512 + QW]
                    rhobps = psB[:, 1024:1024 + QW]
                    ho_gate = avbank_gate[j % 2]

                    def emit_avrho(g, j=j, j_in=j_in, nkt_j=nkt_j, avps=avps,
                                   rhops=rhops, hh=hh, ho_gate=ho_gate):
                        G = grpG[(hh, j, g)]
                        base = (G % NSLOT) * 2 * QW
                        for kk in range(2):
                            kt = 2 * g + kk
                            diag = kt >= KPR * j
                            if diag:
                                gate, gv = sMk, expv[(hh, j, g, "mk", kk)]
                            else:
                                gate, gv = sEx, expv[(hh, j, g, "ex")]
                            w = [(gate, gv)]
                            if kt == 0 and ho_gate > 0:
                                w.append((sHo, ho_gate))
                            mm(avps,
                               vsb(kt)[:, j_in * 128:(j_in + 1) * 128],
                               probsT[:, base + kk * QW: base + (kk + 1) * QW],
                               kt == 0, kt == nkt_j - 1,
                               sem=(sAv if kt == nkt_j - 1 else None),
                               waits=w)
                            w2_ = [(sRhE, sRhE.n)] if kt == 0 else []
                            last = (kt == nkt_j - 1)
                            mm(rhops, onesC[:, :],
                               probsT[:, base + kk * QW: base + (kk + 1) * QW],
                               kt == 0, last,
                               sem=(sRh if last else
                                    (sPgr if kk == 1 else None)),
                               waits=w2_)
                            if kk == 1:
                                gmark[G] = ((sRh, sRh.n) if last
                                            else (sPgr, sPgr.n))

                    for g in range(ng):
                        gp = g % 2
                        G = gctr[0]
                        gctr[0] += 1
                        grpG[(hh, j, g)] = G
                        base = (G % NSLOT) * 2 * QW
                        w = []
                        if g == 0 and j == 0:
                            w += [(sTg, tg)]
                        if g >= 2:
                            w.append((sEx, expv[(hh, j, g - 2, "ex")]))
                        elif j > 0 and g < 2:
                            w.append((sEx, expv[(hh, j - 1, "last_ex")]))
                        if g == 0 and j == 0:
                            w.append((sKe, ke))
                        for kk in range(2):
                            kt = 2 * g + kk
                            mm(psA[:, gp * 1024 + kk * 512:
                                   gp * 1024 + kk * 512 + QW],
                               ksb[:, kt * 128:(kt + 1) * 128],
                               qsb[:, j * QW:(j + 1) * QW],
                               True, True,
                               sem=(sSc if kk == 1 else None),
                               waits=(w if kk == 0 else []))
                        scv = sSc.n
                        if g == 1:
                            flush_tail()
                        we = [(sSc, scv)]
                        if G >= NSLOT:
                            we.append(gmark[G - NSLOT])
                        act(probsT[:, base: base + 2 * QW],
                            psA[:, gp * 1024: gp * 1024 + 2 * QW],
                            AF.Exp, sem=sEx, waits=we,
                            bias=CSHIFT, scl=SCALE)
                        expv[(hh, j, g, "ex")] = sEx.n
                        if 2 * g + 1 >= KPR * j:
                            d0 = 2 * g - KPR * j
                            for kk2 in range(2):
                                vop("tensor_tensor",
                                    probsT[:, base + kk2 * QW:
                                           base + (kk2 + 1) * QW],
                                    probsT[:, base + kk2 * QW:
                                           base + (kk2 + 1) * QW],
                                    mask_sb[:, (d0 + kk2) * 512:
                                            (d0 + kk2 + 1) * 512],
                                    op=ALU.mult, sem=sMk,
                                    waits=[(sEx, sEx.n)] if kk2 == 0 else [])
                                expv[(hh, j, g, "mk", kk2)] = sMk.n
                        if g >= 1:
                            emit_avrho(g - 1)
                    expv[(hh, j, "last_ex")] = sEx.n
                    emit_avrho(ng - 1)
                    avv = sAv.n
                    rhv = sRh.n
                    act(rho_sb[0:1, 0:QW], rhops, AF.Copy, sem=sRhE,
                        waits=[(sRh, rhv)])
                    rhev = sRhE.n

                    def emit_tail(j=j, hh=hh, rhev=rhev, avv=avv, tg=tg,
                                  avps=avps, rhobps=rhobps):
                        mm(rhobps, ones1[0:1, :], rho_sb[0:1, 0:QW],
                           True, True, sem=sRhB,
                           waits=[(sRhE, rhev), (sDm, sDm.n)])
                        rhbv = sRhB.n
                        vop("scalar_tensor_tensor", Dsb[:, 0:QW],
                            tgb[:, j * QW:(j + 1) * QW], 1.0, rhobps,
                            op0=ALU.add, op1=ALU.mult, sem=sDm,
                            waits=[(sRhB, rhbv), (sTg, tg), (sDi, sDi.n)])
                        dmv = sDm.n
                        act(Dsb[:, 0:QW], Dsb[:, 0:QW], AF.Ln,
                            waits=[(sDm, dmv), (sHo, sHo.n)], sem=sAch)
                        act(Dinv[:, 0:QW], Dsb[:, 0:QW], AF.Exp, scl=-1.0,
                            sem=sDi, waits=[(sAch, sAch.n)])
                        div = sDi.n
                        vop("tensor_tensor", hob(hh)[:, j * QW:(j + 1) * QW],
                            avps, Dinv[:, 0:QW], op=ALU.mult, sem=sHo,
                            waits=[(sDi, div), (sAv, avv)])
                        avbank_gate[j % 2] = sHo.n
                    tail_pending.append(emit_tail)
                    nu = len(pconv_units)
                    rdone = hh * NQ + j + 1
                    emit_pconv((nu * rdone) // (HPC * NQ)
                               - (nu * (rdone - 1)) // (HPC * NQ))
                    if j < NQ - 1:
                        cv_t = (npc_tot * (j + 1) + NQ - 1) // NQ
                        ld_t = min(npc_tot,
                                   (npc_tot * (j + 2) + NQ - 1) // NQ)
                        done_cv = npc_tot - len(spread_pieces) \
                            - len(spread_loaded)
                        emit_spread_conv(cv_t - done_cv)
                        done_ld = npc_tot - len(spread_pieces)
                        emit_spread_load(ld_t - done_ld)
                    else:
                        spread_drain()
                spread_drain()
                prev_head_psA_free = (sEx, expv[(hh, NQ - 1, "last_ex")])
        flush_tail()
        ho_all = sHo.n
        qm_all, km_all, vm_all = sQm.n, sKm.n, sVm.n

        for n in range(NST2):
            mm(psB[:, n * SW2:(n + 1) * SW2], ones1[0:1, :],
               e2f2[0:1, n * SW2:(n + 1) * SW2], True, True,
               waits=([(sMath, p2_state['nmr2']), (sHo, ho_all), (sDm, sDm.n)]
                      if n == 0 else []))
            mm(psA[:, n * SW2:(n + 1) * SW2], ones1b[0:1, :],
               m2_bf[0:1, n * SW2:(n + 1) * SW2], True, True, sem=sBc,
               waits=([(sEx, expv[(HPC - 1, NQ - 1, "last_ex")])]
                      if n == 0 else []))
        bc2 = sBc.n
        act(r2b_sb[:, 0:S2], psB[:, 0:S2], AF.Copy, sem=sM2c,
            waits=[(sBc, bc2)])
        act(nm2b_sb[:, 0:S2], psA[:, 0:S2], AF.Copy, sem=sM2c)
        m2c = sM2c.n
        for c in range(NC):
            for h in range(XH2):
                buf, p, lv = xs.load(
                    xf[c * 128:(c + 1) * 128, h * CB2:(h + 1) * CB2], CB2)
                cols = slice(h * CB2, (h + 1) * CB2)
                vop("tensor_tensor", buf[:, 0:CB2], buf[:, 0:CB2],
                    r2b_sb[:, cols], op=ALU.mult,
                    waits=[(sX[p], lv), (sM2c, m2c)], sem=sVch)
                vop("tensor_tensor", xnF(c)[:, cols], buf[:, 0:CB2],
                    nm2b_sb[:, cols], op=ALU.add, sem=sXn2,
                    waits=[(sVch, sVch.n)] +
                          ([(sQm, qm_all), (sKm, km_all), (sVm, vm_all)]
                           if (c == 0 and h == 0) else []))
                xs.set_free(p, [(sXn2, sXn2.n)])
        xn2_all = sXn2.n

        G8 = min(4, QT)
        QH = QT // G8
        opev_done = {}
        store_done = {}
        oper_done = {}

        ow_segs = {0: [(qsb, S), (ksb, S)],
                   1: [(tgb, S), (rssb[0], D2), (rssb[1], D2)]}

        def seg_place(segs, off):
            for t, w_ in segs:
                if off < w_:
                    return t, off
                off -= w_
            raise AssertionError("owq overflow")

        def segs_ok(segs):
            off = 0
            for hh_ in range(HPC):
                try:
                    t, o = seg_place(segs, hh_ * EW)
                except AssertionError:
                    return False
                if o + EW > dict((id(a), b) for a, b in segs)[id(t)]:
                    return False
            return True

        ow_dbuf = segs_ok(ow_segs[1]) and NER >= 2

        def owq(par, hh):
            t, o = seg_place(ow_segs[par], hh * EW)
            return t[:, o:o + EW]

        op_store_q = []

        def flush_op_stores():
            while op_store_q:
                op_store_q.pop(0)()

        def grp_nstores(er):
            n = 0
            e0 = er * EW
            for r in range(2):
                lo = max(e0, r * D2)
                hi = min(e0 + EW, (r + 1) * D2)
                if lo < hi:
                    n += 1
            return n * G8

        store_plan = {}
        _cnt = [0, 0]
        for _er in range(NER):
            for _qh in range(QH):
                _g = _er * QH + _qh
                _cnt[_g % 2] += 16 * grp_nstores(_er)
                store_plan[_g] = _cnt[_g % 2]

        for er in range(NER):
            par = (er % 2) if ow_dbuf else 0
            wv = []
            if er < (2 if ow_dbuf else 1):
                wv += [(sSc, sSc.n)] + pconv_done_gate()
                if par == 1:
                    wv += [(sDm, sDm.n), (sBc, sBc.n)]
            else:
                wv.append((sOp, oper_done[er - (2 if ow_dbuf else 1)]))
            off = 0
            first_dma = True
            while off < HPC * EW:
                t, o = seg_place(ow_segs[par], off)
                run = EW
                while (off + run < HPC * EW):
                    t2, o2 = seg_place(ow_segs[par], off + run)
                    if t2 is t and o2 == o + run:
                        run += EW
                    else:
                        break
                dma("gpsimd", t[:, o:o + run],
                    owbf[er][:, off:off + run],
                    sOwL[par], waits=(wv if first_dma else []))
                first_dma = False
                off += run
            owlv = sOwL[par].n
            flush_op_stores()
            for qh in range(QH):
                grp = er * QH + qh
                for hhh in range(HPC):
                    for q8 in range(G8):
                        qt = qh * G8 + q8
                        ps = psA if grp % 2 == 0 else psB
                        col = q8 * 512
                        w = []
                        if q8 == 0 and hhh == 0:
                            w.append((sOwL[par], owlv))
                            if grp == 0:
                                w += [(sHo, ho_all), (sM2c, m2c)]
                            elif grp == 1:
                                w.append((sHo, ho_all))
                            else:
                                w.append((sOpE, opev_done[grp - 2]))
                        mm(ps[:, col:col + EW],
                           hob(hhh)[:, qt * 128:(qt + 1) * 128],
                           owq(par, hhh),
                           hhh == 0, hhh == HPC - 1,
                           sem=(sOp if (hhh == HPC - 1 and q8 == G8 - 1)
                                else None),
                           waits=w)
                opv = sOp.n
                ps = psA if grp % 2 == 0 else psB
                wv2 = [(sOp, opv)]
                if grp >= 2:
                    wv2.append((sOpSp[grp % 2], store_plan[grp - 2]))
                act(opst[grp % 2][:, 0:G8 * EW], ps[:, 0:G8 * EW], AF.Copy,
                    sem=sOpE, waits=wv2)
                opev = sOpE.n

                def emit_stores(qh=qh, er=er, grp=grp, opev=opev):
                    first_store = True
                    for q8 in range(G8):
                        qt = qh * G8 + q8
                        e0 = er * EW
                        for r in range(2):
                            lo = max(e0, r * D2)
                            hi = min(e0 + EW, (r + 1) * D2)
                            if lo >= hi:
                                continue
                            dma("gpsimd",
                                op_dram[r * S + qt * 128:
                                        r * S + (qt + 1) * 128,
                                        lo - r * D2:hi - r * D2],
                                opst[grp % 2][:, q8 * EW + lo - e0:
                                              q8 * EW + hi - e0],
                                sOpSp[grp % 2],
                                waits=([(sOpE, opev)] if first_store else []))
                            first_store = False
                    store_done[grp] = sOpSp[grp % 2].n
                op_store_q.append(emit_stores)
                opev_done[grp] = sOpE.n
            oper_done[er] = sOp.n
        flush_op_stores()
        collective("ReduceScatter", ALU.add,
                   [[0, 1], [2, 3], [4, 5], [6, 7]],
                   op_dram.ap().opt(), rs_dram.ap().opt(), sRS,
                   waits=[(sOpSp[0], sOpSp[0].n), (sOpSp[1], sOpSp[1].n)])

        opev_all = sOpE.n
        for tt in range(TT2):
            bank = tt % 8
            ps = psA if bank < 4 else psB
            col = (bank % 4) * 512
            w = [(sXn2, xn2_all)]
            if tt == 0:
                w.append((sOpE, opev_all))
            if tt >= 8:
                w.append((sRtE, sRtE.n - 7))
            for c in range(NC):
                mm(ps[:, col:col + 8],
                   xnF(c)[:, tt * 128:(tt + 1) * 128],
                   wrb[:, c * 8:(c + 1) * 8],
                   c == 0, c == NC - 1,
                   sem=(sRt if c == NC - 1 else None),
                   waits=(w if c == 0 else []))
            rtv = sRt.n
            act(rt_sb[:, tt * 8:(tt + 1) * 8], ps[:, col:col + 8],
                AF.Copy, sem=sRtE, waits=[(sRt, rtv)])
        rte = sRtE.n
        act(rt_sb[:, :], rt_sb[:, :], AF.Exp, sem=sRtE, waits=[(sRtE, rte)])
        rtexp = sRtE.n
        rt3 = rt_sb[:, :].rearrange("p (t e) -> p t e", e=8)
        lo3 = r_lo[:, :].rearrange("p (t e) -> p t e", e=8)
        kp3 = r_kp[:, :].rearrange("p (t e) -> p t e", e=8)
        g3 = g_sb[:, :].rearrange("p (t e) -> p t e", e=8)
        m1b3 = r_m1[:, :].rearrange("p (t o) -> p t o", o=1).broadcast_to(
            [128, TT2, 8])
        m2b3 = r_m2[:, :].rearrange("p (t o) -> p t o", o=1).broadcast_to(
            [128, TT2, 8])
        dnb3 = r_dn[:, :].rearrange("p (t o) -> p t o", o=1).broadcast_to(
            [128, TT2, 8])
        vop("tensor_reduce", r_m1[:, :], rt3, AX.X, ALU.max, sem=sTp,
            waits=[(sRtE, rtexp)])
        vop("tensor_tensor", lo3, rt3, m1b3, op=ALU.is_lt, sem=sTp,
            waits=[(sTp, sTp.n)])
        vop("tensor_tensor", lo3, rt3, lo3, op=ALU.mult, sem=sTp,
            waits=[(sTp, sTp.n)])
        vop("tensor_reduce", r_m2[:, :], lo3, AX.X, ALU.max, sem=sTp,
            waits=[(sTp, sTp.n)])
        vop("tensor_tensor", kp3, rt3, m2b3, op=ALU.is_ge, sem=sTp,
            waits=[(sTp, sTp.n)])
        vop("tensor_tensor", r_dn[:, :], r_m1[:, :], r_m2[:, :], op=ALU.add,
            sem=sTp, waits=[(sTp, sTp.n)])
        vop("reciprocal", r_dn[:, :], r_dn[:, :], sem=sTp,
            waits=[(sTp, sTp.n)])
        vop("tensor_tensor", kp3, kp3, dnb3, op=ALU.mult, sem=sTp,
            waits=[(sTp, sTp.n)])
        vop("tensor_tensor", g3, rt3, kp3, op=ALU.mult, sem=sTp,
            waits=[(sTp, sTp.n)])
        gdone = sTp.n

        finale_stores = []
        st_totals = {}
        n_done = 0

        def emit_finale_tile(qt):
            nonlocal n_done
            n = n_done
            pr = n % 3
            wrs = [(sRS, 1)] if n == 0 else []
            if n >= 2:
                wrs.append((sFin, sFin.n - 1))
            dma("sync", rssb[n % 2][:, 0:D2],
                rs_dram[qt * 128:(qt + 1) * 128, :], sRsL[n % 2], waits=wrs)
            rsv = sRsL[n % 2].n
            buf, p, lv = xs.load(xr[qt * 128:(qt + 1) * 128, :], D2)
            wadd = [(sX[p], lv), (sRsL[n % 2], rsv)]
            if n >= 3:
                wadd.append((sSt[pr], 16 * (n // 3)))
            vop("tensor_tensor", outsb[pr][:, 0:D2], buf[:, 0:D2],
                rssb[n % 2][:, 0:D2], op=ALU.add, sem=sFin, waits=wadd)
            xs.set_free(p, [(sFin, sFin.n)])
            fv = sFin.n

            def st(qt=qt, pr=pr, fv=fv):
                dma("gpsimd", out[qt * 128:(qt + 1) * 128, :],
                    outsb[pr][:, 0:D2], sSt[pr],
                    waits=[(sFin, fv), (sSt[pr], sSt[pr].n)])
                st_totals[pr] = sSt[pr].n
            finale_stores.append(st)
            n_done += 1

        def flush_finale_stores(k=None):
            while finale_stores and (k is None or k > 0):
                finale_stores.pop(0)()
                if k is not None:
                    k -= 1

        def emit_finale_tail():
            n = n_done
            pr = n % 3
            dma("sync", rssb[n % 2][:, 0:D2], rs_dram[0:128, :],
                sRsL[n % 2], waits=[(sFin, sFin.n - 1)])
            rsv = sRsL[n % 2].n
            buf, p, lv = xs.load(xr[0:128, :], D2)
            wadd = [(sX[p], lv), (sRsL[n % 2], rsv),
                    (sSt[pr], 16 * (n // 3))]
            vop("tensor_tensor", outsb[pr][:, 0:D2], buf[:, 0:D2],
                rssb[n % 2][:, 0:D2], op=ALU.add, sem=sFin, waits=wadd)
            xs.set_free(p, [(sFin, sFin.n)])
            dma("gpsimd", o8sb[0:8, :], o8r_dram[:, :], sArL,
                waits=[(sAR, 1)])
            vop("tensor_scalar", fad[0:8, :], o8sb[0:8, D2:2 * D2],
                selb[0:8, :], None, op0=ALU.mult,
                waits=[(sArL, sArL.n), (sCst, cl0),
                       (sSt[2], st_totals.get(2, 0))], sem=sVch)
            vop("scalar_tensor_tensor", fad[0:8, :], o8sb[0:8, 0:D2],
                sela[0:8, :], fad[0:8, :], op0=ALU.mult, op1=ALU.add,
                sem=sVch, waits=[(sVch, sVch.n)])
            vop("tensor_tensor", outsb[pr][0:8, 0:D2],
                outsb[pr][0:8, 0:D2], fad[0:8, :], op=ALU.add, sem=sFin,
                waits=[(sVch, sVch.n)])
            fv = sFin.n
            dma("gpsimd", out[0:128, :], outsb[pr][:, 0:D2], sSt[pr],
                waits=[(sFin, fv)])
            for s_ in sSt:
                wt("gpsimd", s_, s_.n)


        def ffn_layer(wsrc, rhs_fn, dst_fn, sMM, sEV, gate_first, nch,
                      conv_gate=()):
            mm0 = sMM.n
            ev0 = sEV.n
            for it in range(NIT):
                wcw = []
                if it >= 2:
                    wcw.append((sMM, mm0 + it - 1))
                elif conv_gate:
                    wcw.append((conv_gate[0], conv_gate[1] - 1 + it))
                wcv = load_conv(
                    lambda off, wdt, it=it: w13b[it % 2][:, off:off + wdt],
                    lambda off, wdt, it=it: wsrc[it][:, off:off + wdt],
                    nch * 128, wcw)
                ps = psA if it % 2 == 0 else psB
                w = [(sWc, wcv)] + (list(gate_first) if it == 0 else [])
                if it >= 2:
                    w.append((sEV, ev0 + it - 1))
                for n in range(NF):
                    for c in range(nch):
                        mm(ps[:, n * FW:(n + 1) * FW],
                           w13b[it % 2][:, c * 128:(c + 1) * 128],
                           rhs_fn(c)[:, n * FW:(n + 1) * FW],
                           c == 0, c == nch - 1,
                           sem=(sMM if (n == NF - 1 and c == nch - 1) else None),
                           waits=(w if (n == 0 and c == 0) else []))
                mmv = sMM.n
                act(dst_fn(it)[:, 0:S2], ps[:, 0:S2], AF.Silu, sem=sEV,
                    waits=[(sMM, mmv)])

        h1mm0 = sH1.n
        ffn_layer(w1, xnF, h1sb, sH1, sH1E, [(sRtE, rte)], NC)
        h1e = sH1E.n
        ffn_layer(w3, h1sb, h2sb, sH2, sH2E, [(sH1E, h1e)], NCI,
                  conv_gate=(sH1, h1mm0 + NIT))
        h2e = sH2E.n

        for qt in range(1, QT):
            emit_finale_tile(qt)

        h3mm = {}
        for er in range(NERH):
            wbuf = RD if er % 2 == 0 else vwb
            w2c = lambda ci, wbuf=wbuf: wbuf[:, ci * EWH:(ci + 1) * EWH]
            w = []
            if er < 2:
                w += [(sHo, ho_all), (sQm, qm_all), (sKm, km_all),
                      (sVm, vm_all)] + pconv_done_gate()
            else:
                w.append((sH3, h3mm[er - 2]))
            dma("gpsimd", wbuf[:, 0:NCI * EWH], w2bf[er],
                sW2L[er % 2], waits=w)
            w2lv = sW2L[er % 2].n
            if finale_stores:
                flush_finale_stores(-(-len(finale_stores) // (NERH - er)))
            for tt in range(TT2):
                ps = psA if tt % 2 == 0 else psB
                w = [(sW2L[er % 2], w2lv)]
                if er == 0 and tt == 0:
                    w.append((sH2E, h2e))
                if tt >= 2 or er > 0:
                    w.append((sH3E, sH3E.n - 1))
                for ci in range(NCI):
                    mm(ps[:, 0:EWH],
                       h2sb(ci)[:, tt * 128:(tt + 1) * 128],
                       w2c(ci), ci == 0, ci == NCI - 1,
                       sem=(sH3 if ci == NCI - 1 else None),
                       waits=(w if ci == 0 else []))
                h3v = sH3.n
                wv = [(sH3, h3v)]
                if er == 0 and tt == 0:
                    wv.append((sH2, sH2.n))
                act(h3sb(tt)[:, er * EWH:(er + 1) * EWH],
                    ps[:, 0:EWH], AF.Copy, sem=sH3E,
                    waits=wv)
            h3mm[er] = sH3.n
        h3e = sH3E.n

        for er in range(NERH):
            ps = psA if er < 4 else psB
            col = (er % 4) * 512
            for tt in range(TT2):
                mm(ps[0:8, col:col + EWH],
                   g_sb[:, tt * 8:(tt + 1) * 8],
                   h3sb(tt)[:, er * EWH:(er + 1) * EWH],
                   tt == 0, tt == TT2 - 1,
                   sem=(sO8 if tt == TT2 - 1 else None),
                   waits=([(sH3E, h3e), (sTp, gdone)]
                          if (er == 0 and tt == 0) else []))
        o8v = sO8.n
        for er in range(NERH):
            ps = psA if er < 4 else psB
            col = (er % 4) * 512
            act(o8sb[0:8, er * EWH:(er + 1) * EWH], ps[0:8, col:col + EWH],
                AF.Copy, sem=sO8E,
                waits=([(sO8, o8v)] if er == 0 else []))
        dma("gpsimd", o8_dram[:, :], o8sb[0:8, :], sO8S,
            waits=[(sO8E, sO8E.n)])
        collective("AllReduce", ALU.add, [[0, 1, 2, 3, 4, 5, 6, 7]],
                   o8_dram.ap().opt(), o8r_dram.ap().opt(), sAR,
                   waits=[(sO8S, sO8S.n)])
        flush_finale_stores()

        emit_finale_tail()

        block = es.enter_context(nc.Block())

        @block.sync
        def _(e):
            for fn in prog["sync"]:
                fn(e)

        @block.gpsimd
        def _(e):
            for fn in prog["gpsimd"]:
                fn(e)

        @block.tensor
        def _(e):
            for fn in prog["tensor"]:
                fn(e)

        @block.vector
        def _(e):
            for fn in prog["vector"]:
                fn(e)

        @block.scalar
        def _(e):
            for fn in prog["scalar"]:
                fn(e)

    return nc


def _block_w(wT_sl, n_out_blk, out_blk):
    Dd = wT_sl.shape[0]
    ncc = Dd // 128
    a = wT_sl.reshape(ncc, 128, n_out_blk, out_blk)
    return np.ascontiguousarray(a.transpose(2, 1, 0, 3).reshape(
        n_out_blk, 128, ncc * out_blk))


def _mask_const():
    m = np.zeros((128, 4 * 512), dtype=np.float32)
    kk = np.arange(128)[:, None]
    qq = np.arange(512)[None, :]
    for d in range(4):
        m[:, d * 512:(d + 1) * 512] = (kk <= (qq - 128 * d))
    return m.astype(bf16)


def make_inputs(x, q_w, k_w, v_w, o_w, gate_w, gate_b, w1_, w3_, w2_, wr_,
                S, D, H, INTER):
    HPC = H // 2
    NC = D // 128
    NIT = INTER // 128
    NCI = INTER // 128
    VG = min(2, HPC)
    NB = HPC // VG
    D2 = D // 2
    NER = max(1, D // 512)
    EW = D // NER
    EWH = min(256, D)
    NERH = D // EWH
    S2 = S // 2

    qwT = np.ascontiguousarray(q_w.T)
    kwT = np.ascontiguousarray(k_w.T)
    vwT = np.ascontiguousarray(v_w.T)
    owT = np.ascontiguousarray(o_w.T)
    w1T = np.ascontiguousarray(w1_.T)
    w3T = np.ascontiguousarray(w3_.T)
    w2T = np.ascontiguousarray(w2_.T)
    wrT = np.ascontiguousarray(wr_.T)
    gwT = np.ascontiguousarray(gate_w.T)
    mask = _mask_const()

    in_maps = []
    for core in range(8):
        b, g = core // 2, core % 2
        hw = HPC * 128
        d = {
            "xT": np.ascontiguousarray(x[b].T),
            "xf": np.ascontiguousarray(x[b, g * S2:(g + 1) * S2, :].T),
            "xr": np.ascontiguousarray(x[b][:, g * D2:(g + 1) * D2]),
            "qw": _block_w(qwT[:, g * hw:(g + 1) * hw], HPC, 128),
            "kw": _block_w(kwT[:, g * hw:(g + 1) * hw], HPC, 128),
            "vw": _block_w(vwT[:, g * hw:(g + 1) * hw], NB, VG * 128)
                  .reshape(NB, 128, NC, VG * 128),
            "gw": gwT,
            "gb": np.ascontiguousarray(gate_b.reshape(128, 1)),
            "ow": np.ascontiguousarray(owT[g * hw:(g + 1) * hw, :]),
            "w1": _block_w(w1T, NIT, 128),
            "w3": _block_w(w3T, NIT, 128),
            "w2": _block_w(w2T, NERH, EWH).reshape(NERH, 128, NCI, EWH),
            "wr": np.ascontiguousarray(
                wrT.reshape(NC, 128, 8).transpose(1, 0, 2)
                .reshape(128, NC * 8)),
            "mk": mask,
            "sel_a": np.full((8, 1), 1.0 if core == 0 else 0.0, np.float32),
            "sel_b": np.full((8, 1), 1.0 if core == 1 else 0.0, np.float32),
        }
        in_maps.append(d)
    return in_maps


def kernel(L0_attn_x, L0_attn_q_w, L0_attn_k_w, L0_attn_v_w, L0_attn_o_w,
           L0_attn_gate_w, L0_attn_gate_b, L0_ffn_w1, L0_ffn_w3, L0_ffn_w2,
           L0_ffn_wr):
    S, D, H, INTER = FULL["S"], FULL["D"], FULL["H"], FULL["INTER"]
    x = np.asarray(L0_attn_x, np.float32)
    in_maps = make_inputs(
        x, np.asarray(L0_attn_q_w, np.float32),
        np.asarray(L0_attn_k_w, np.float32),
        np.asarray(L0_attn_v_w, np.float32),
        np.asarray(L0_attn_o_w, np.float32),
        np.asarray(L0_attn_gate_w, np.float32),
        np.asarray(L0_attn_gate_b, np.float32),
        np.asarray(L0_ffn_w1, np.float32),
        np.asarray(L0_ffn_w3, np.float32),
        np.asarray(L0_ffn_w2, np.float32),
        np.asarray(L0_ffn_wr, np.float32),
        S, D, H, INTER)
    nc = build_program(S, D, H, INTER)
    res = run_bass_kernel_spmd(nc, in_maps, core_ids=list(range(8))).results
    D2 = D // 2
    full = np.empty((4, S, D), np.float32)
    for core in range(8):
        b, g = core // 2, core % 2
        full[b, :, g * D2:(g + 1) * D2] = res[core]["out"]
    return full


# version 26 (final)
# speedup vs baseline: 1.0864x; 1.0121x over previous
import math
from contextlib import ExitStack

import numpy as np
import ml_dtypes

import concourse.bass as bass
import concourse.mybir as mybir
from concourse.bass_utils import run_bass_kernel_spmd

F32 = mybir.dt.float32
BF16 = mybir.dt.bfloat16
AF = mybir.ActivationFunctionType
ALU = mybir.AluOpType
AX = mybir.AxisListType

f32 = np.float32
bf16 = ml_dtypes.bfloat16

FULL = dict(S=2048, D=2048, H=16, INTER=2048)


class Sem:
    def __init__(self, nc, es, name):
        self.h = es.enter_context(nc.semaphore(name))
        self.n = 0


def build_program(S, D, H, INTER):
    HPC = H // 2
    NC = D // 128
    NQ = max(1, S // 512)
    QW = S // NQ
    NKT = S // 128
    KPR = NKT // NQ
    S2 = S // 2
    NF = max(1, S2 // 512)
    FW = S2 // NF
    TT2 = S2 // 128
    NIT = INTER // 128
    NCI = INTER // 128
    VG = min(2, HPC)
    NB = HPC // VG
    D2 = D // 2
    NER = max(1, D // 512)
    EW = D // NER
    EWH = min(256, D)
    NERH = D // EWH
    CB = min(1024, S)
    XH = S // CB
    CB2 = min(1024, S2)
    XH2 = S2 // CB2
    NST = max(1, S // 512)
    SW = S // NST
    NST2 = max(1, S2 // 512)
    SW2 = S2 // NST2
    QT = S // 128
    SCALE = float(1.0 / math.sqrt(128.0))
    CSHIFT = -16.0
    EPS = 1e-5
    recipD = 1.0 / D

    nc = bass.Bass("TRN2", target_bir_lowering=False)

    dI = lambda n, sh, dt=F32: nc.dram_tensor(n, sh, dt, kind="ExternalInput").ap()
    xT = dI("xT", [D, S])
    xf = dI("xf", [D, S2])
    xr = dI("xr", [S, D2])
    qw = dI("qw", [HPC, 128, NC * 128])
    kw = dI("kw", [HPC, 128, NC * 128])
    vw = dI("vw", [NB, 128, NC, VG * 128])
    gw = dI("gw", [128, 128])
    gb = dI("gb", [128, 1])
    ow = dI("ow", [HPC * 128, D])
    w1 = dI("w1", [NIT, 128, NC * 128])
    w3 = dI("w3", [NIT, 128, NCI * 128])
    w2 = dI("w2", [NERH, 128, NCI, EWH])
    wr = dI("wr", [128, NC * 8])
    mk = dI("mk", [128, 4 * 512], BF16)
    sel_a = dI("sel_a", [8, 1])
    sel_b = dI("sel_b", [8, 1])

    out = nc.dram_tensor("out", [S, D2], F32, kind="ExternalOutput").ap()

    owbf = nc.dram_tensor("owbf", [NER, 128, HPC * EW], BF16)
    w2bf = nc.dram_tensor("w2bf", [NERH, 128, NCI * EWH], BF16)
    op_dram = nc.dram_tensor("op_dram", [2 * S, D2], BF16)
    rs_dram = nc.dram_tensor("rs_dram", [S, D2], BF16)
    o8_dram = nc.dram_tensor("o8_dram", [8, D], F32)
    o8r_dram = nc.dram_tensor("o8r_dram", [8, D], F32)
    o8g_dram = nc.dram_tensor("o8g_dram", [64, D], F32)

    with ExitStack() as es:
        sbuf = lambda n, sh, dt: es.enter_context(nc.sbuf_tensor(n, sh, dt))

        RA = sbuf("RA", [128, NC * S], BF16)
        xnT = lambda c: RA[:, c * S:(c + 1) * S]
        HALF = NC * S // 2
        xnF = lambda c: RA[:, c * S2:(c + 1) * S2]
        h1sb = lambda i: RA[:, HALF + i * S2: HALF + (i + 1) * S2]
        h3sb = lambda t: RA[:, HALF + t * D: HALF + (t + 1) * D]

        RB = sbuf("RB", [128, HPC * S], BF16)
        hob = lambda hh: RB[:, hh * S:(hh + 1) * S]
        h2sb = lambda i: RB[:, i * S2:(i + 1) * S2]

        NSLOT = 4
        RC = sbuf("RC", [128, NSLOT * 2 * QW], BF16)
        probsT = RC

        RD = sbuf("RD", [128, NKT * VG * 128], BF16)
        vsb = lambda tt: RD[:, tt * VG * 128:(tt + 1) * VG * 128]

        assert HPC >= 2, "need >=2 heads per core for buffer aliasing"
        vwb = sbuf("vwb", [128, NC * VG * 128], BF16)
        qwb = vwb[:, 0:NC * 128]
        kwb = vwb[:, NC * 128:2 * NC * 128]
        w13b = [sbuf("w13b0", [128, NC * 128], BF16),
                sbuf("w13b1", [128, NC * 128], BF16)]
        RDW = NKT * VG * 128

        def w2b(ci):
            off = ci * EW
            if off < RDW:
                return RD[:, off:off + EW]
            return vwb[:, off - RDW:off - RDW + EW]
        gwb = sbuf("gwb", [128, 128], BF16)
        owx = [sbuf("owx0", [128, 512], BF16), sbuf("owx1", [128, 512], BF16)]
        wrb = sbuf("wrb", [128, NC * 8], BF16)

        WS = [sbuf("WS0", [128, 1024], F32), sbuf("WS1", [128, 1024], F32)]
        XS = [sbuf("XS0", [128, CB], F32), sbuf("XS1", [128, CB], F32)]

        qsb = sbuf("qsb", [128, S], BF16)
        ksb = sbuf("ksb", [128, S], BF16)
        tgb = sbuf("tgb", [128, S], BF16)

        scrD = sbuf("scrD", [128, 2 * QW], F32)
        Dsb = scrD[:, 0:QW]
        Dinv = scrD[:, QW:2 * QW]
        nmr2b2 = sbuf("nmr2b2", [128, 2 * S2], F32)
        nm2b_sb = nmr2b2[:, 0:S2]
        r2b_sb = nmr2b2[:, S2:2 * S2]
        outs2 = sbuf("outs2", [128, 2 * D2], F32)
        outsb = None
        OPW = min(4, QT) * EW
        opst = [sbuf("opst0", [128, OPW], BF16),
                sbuf("opst1", [128, OPW], BF16)]
        rssb = [sbuf("rssb0", [128, D2], BF16), sbuf("rssb1", [128, D2], BF16)]

        assert 2 * QW >= D2
        outsb = [outs2[:, 0:D2], outs2[:, D2:2 * D2], scrD[:, 0:D2]]
        m_bf = opst[0][0:1, 0:S]
        stmp = opst[1][0:1, 0:S]
        m2_bf = rssb[0][0:1, 0:S2]
        e2f = outs2[0:1, 0:S]
        e2f2 = outs2[0:1, 0:S2]
        rho_sb = sbuf("rho_sb", [1, QW], F32)

        rt_sb = sbuf("rt_sb", [128, TT2 * 8], F32)
        g_sb = sbuf("g_sb", [128, TT2 * 8], BF16)
        r_m1 = sbuf("r_m1", [128, TT2], F32)
        r_m2 = sbuf("r_m2", [128, TT2], F32)
        r_lo = sbuf("r_lo", [128, TT2 * 8], F32)
        r_kp = sbuf("r_kp", [128, TT2 * 8], F32)
        r_dn = sbuf("r_dn", [128, TT2], F32)

        onesD = sbuf("onesD", [128, 1], BF16)
        ones1 = sbuf("ones1", [1, 128], F32)
        onesC = sbuf("onesC", [128, 1], BF16)
        ones1b = sbuf("ones1b", [1, 128], BF16)
        mask_sb = sbuf("mask_sb", [128, 4 * 512], BF16)
        gbn = sbuf("gbn", [128, 1], F32)
        gbs = sbuf("gbs", [128, 1], F32)
        sela = sbuf("sela", [8, 1], F32)
        selb = sbuf("selb", [8, 1], F32)
        o8sb = nmr2b2[0:8, 0:D]
        fad = scrD[0:8, 0:D2]

        psA = es.enter_context(nc.psum_tensor("psA", [128, 2048], F32))
        psB = es.enter_context(nc.psum_tensor("psB", [128, 2048], F32))

        SM = lambda name: Sem(nc, es, name)
        sX = [SM("sX0"), SM("sX1")]
        sXFr = [SM("sXF0"), SM("sXF1")]
        sW = [SM("sW0"), SM("sW1")]
        sWFr = [SM("sWF0"), SM("sWF1")]
        sInit = SM("sInit")
        sSq = SM("sSq")
        sCv = SM("sCv")
        sStP = SM("sStP")
        sStE = SM("sStE")
        sMath = SM("sMath")
        sBc = SM("sBc")
        sXn = SM("sXn")
        sXn2 = SM("sXn2")
        sM2c = SM("sM2c")
        sWc = SM("sWc")
        sQm = SM("sQm")
        sQe = SM("sQe")
        sKm = SM("sKm")
        sKe = SM("sKe")
        sVm = SM("sVm")
        sVe = SM("sVe")
        sGm = SM("sGm")
        sTg = SM("sTg")
        sSc = SM("sSc")
        sEx = SM("sEx")
        sMk = SM("sMk")
        sAv = SM("sAv")
        sRh = SM("sRh")
        sRhE = SM("sRhE")
        sRhB = SM("sRhB")
        sDm = SM("sDm")
        sDi = SM("sDi")
        sHo = SM("sHo")
        sOp = SM("sOp")
        sOpE = SM("sOpE")
        sOpS = SM("sOpS")
        sRS = SM("sRS")
        sRt = SM("sRt")
        sRtE = SM("sRtE")
        sTp = SM("sTp")
        sH1 = SM("sH1")
        sH1E = SM("sH1E")
        sH2 = SM("sH2")
        sH2E = SM("sH2E")
        sH3 = SM("sH3")
        sH3E = SM("sH3E")
        sO8 = SM("sO8")
        sO8E = SM("sO8E")
        sO8S = SM("sO8S")
        sAR = SM("sAR")
        sFin = SM("sFin")
        sFlb = SM("sFlb")
        sSt = [SM("sSt0"), SM("sSt1"), SM("sSt2")]
        sCst = SM("sCst")
        sWcst = SM("sWcst")
        sRsL = [SM("sRsL0"), SM("sRsL1")]
        sArL = SM("sArL")
        sOpSp = [SM("sOpS0"), SM("sOpS1")]
        sMs = SM("sMs")
        sVch = SM("sVch")
        sPgr = SM("sPgr")
        sPcv = SM("sPcv")
        sPcS = [SM("sPcS0"), SM("sPcS1"), SM("sPcS2"), SM("sPcS3")]
        sOwL = [SM("sOwL0"), SM("sOwL1")]
        sWL = [SM("sWL0"), SM("sWL1")]
        sW2L = [SM("sW2L0"), SM("sW2L1")]
        sAch = SM("sAch")

        prog = {k: [] for k in ("sync", "gpsimd", "tensor", "vector", "scalar")}

        def em(eng, fn):
            prog[eng].append(fn)

        def wt(eng, sem, v):
            if v > 0:
                em(eng, lambda e, s=sem.h, v=v: e.wait_ge(s, v))

        def mm(dst, lhsT, rhs, start, stop, sem=None, waits=(), sem2=None):
            for (s, v) in waits:
                wt("tensor", s, v)

            def f(e, dst=dst, lhsT=lhsT, rhs=rhs, start=start, stop=stop,
                  sem=sem, sem2=sem2):
                i = e.matmul(dst, lhsT, rhs, start=start, stop=stop,
                             skip_group_check=True)
                if sem is not None:
                    i = i.then_inc(sem.h, 1)
                if sem2 is not None:
                    i.then_inc(sem2, 1)
            em("tensor", f)
            if sem is not None:
                sem.n += 1

        def act(out_, in_, func, sem=None, waits=(), bias=None, scl=None,
                sem2=None):
            for (s, v) in waits:
                wt("scalar", s, v)

            def f(e, out_=out_, in_=in_, func=func, sem=sem, bias=bias,
                  scl=scl, sem2=sem2):
                kw_ = {}
                if bias is not None:
                    kw_["bias"] = f32(bias) if isinstance(bias, float) else bias
                if scl is not None:
                    kw_["scale"] = f32(scl)
                i = e.activation(out_, in_, func, **kw_)
                if sem is not None:
                    i = i.then_inc(sem.h, 1)
                if sem2 is not None:
                    i.then_inc(sem2, 1)
            em("scalar", f)
            if sem is not None:
                sem.n += 1

        def vop(name, *args, sem=None, waits=(), sem2=None, **kwargs):
            for (s, v) in waits:
                wt("vector", s, v)

            def f(e, name=name, args=args, sem=sem, kwargs=kwargs, sem2=sem2):
                i = getattr(e, name)(*args, **kwargs)
                if sem is not None:
                    i = i.then_inc(sem.h, 1)
                if sem2 is not None:
                    i.then_inc(sem2, 1)
            em("vector", f)
            if sem is not None:
                sem.n += 1

        def gop(name, *args, sem=None, waits=(), **kwargs):
            for (s, v) in waits:
                wt("gpsimd", s, v)

            def f(e, name=name, args=args, sem=sem, kwargs=kwargs):
                i = getattr(e, name)(*args, **kwargs)
                if sem is not None:
                    i.then_inc(sem.h, 1)
            em("gpsimd", f)
            if sem is not None:
                sem.n += 1

        def dma(eng, out_, in_, sem, waits=(), accum=False):
            for (s, v) in waits:
                wt(eng, s, v)

            def f(e, out_=out_, in_=in_, sem=sem, accum=accum):
                kw_ = dict(accum_op=ALU.add) if accum else {}
                e.dma_start(out=out_, in_=in_, **kw_).then_inc(sem.h, 16)
            em(eng, f)
            sem.n += 16

        def collective(kind, op, groups, in_ap, out_ap, sem, waits=()):
            for (s, v) in waits:
                wt("gpsimd", s, v)

            def f(e, kind=kind, op=op, groups=groups, in_ap=in_ap,
                  out_ap=out_ap, sem=sem):
                e.collective_compute(kind, op, replica_groups=groups,
                                     ins=[in_ap], outs=[out_ap]).then_inc(sem.h)
            em("gpsimd", f)
            sem.n += 1

        def pe_milestone():
            em("tensor", lambda e, s=sMs.h: e.nop().then_inc(s, 1))
            sMs.n += 1
            return sMs.n

        class Stream:
            def __init__(self, queue, bufs, sLoad, sFree):
                self.q, self.bufs = queue, bufs
                self.sL = sLoad
                self.cnt = [0, 0]
                self.pending = [[], []]

            def load(self, src, cols, rows=128):
                p = (self.cnt[0] + self.cnt[1]) % 2
                w = self.pending[p]
                self.pending[p] = []
                dma(self.q, self.bufs[p][0:rows, 0:cols], src, self.sL[p], waits=w)
                self.cnt[p] += 1
                return self.bufs[p], p, self.sL[p].n

            def set_free(self, p, gates):
                self.pending[p] = list(gates)

        xs = Stream("sync", XS, sX, sXFr)
        ws = Stream("gpsimd", WS, sW, sWFr)
        WSC = 1024

        pconv_units = []
        for hhh in range(HPC):
            for er_ in range(NER):
                pconv_units.append(
                    (ow[hhh * 128:(hhh + 1) * 128, er_ * EW:(er_ + 1) * EW],
                     owbf[er_][:, hhh * EW:(hhh + 1) * EW], EW))
        for er in range(NERH):
            tot = NCI * EWH
            for off in range(0, tot, WSC):
                wdt = min(WSC, tot - off)
                pconv_units.append(
                    (w2[er].rearrange("p c e -> p (c e)")[:, off:off + wdt],
                     w2bf[er][:, off:off + wdt], wdt))
        pconv_state = [0]

        def emit_pconv(n):
            for _ in range(n):
                k = pconv_state[0]
                if k >= len(pconv_units):
                    return
                pconv_state[0] = k + 1
                srcap, dstap, wdt = pconv_units[k]
                buf, p, lv = xs.load(srcap, wdt)
                pp = k % 4
                slot = opst[pp // 2][:, (pp % 2) * 1024:(pp % 2) * 1024 + wdt]
                wv = [(sX[p], lv)]
                if k >= 4:
                    wv.append((sPcS[pp], 16 * (k // 4)))
                else:
                    wv.append((sMath, p2_state['nmr2']))
                gop("tensor_copy", slot, buf[:, 0:wdt], sem=sPcv, waits=wv)
                xs.set_free(p, [(sPcv, sPcv.n)])
                dma("sync", dstap, slot, sPcS[pp],
                    waits=[(sPcv, sPcv.n)])

        def pconv_done_gate():
            assert pconv_state[0] == len(pconv_units), \
                f"pconv incomplete {pconv_state[0]}/{len(pconv_units)}"
            return [(s, s.n) for s in sPcS]

        def load_conv(dst_fn, src_fn, total, first_waits):
            for off in range(0, total, WSC):
                wdt = min(WSC, total - off)
                buf, p, lv = ws.load(src_fn(off, wdt), wdt)
                vop("tensor_copy", dst_fn(off, wdt), buf[:, 0:wdt], sem=sWc,
                    waits=[(sW[p], lv)] +
                          (list(first_waits) if off == 0 else []))
                ws.set_free(p, [(sWc, sWc.n)])
            return sWc.n

        XCACHE_OK = NC * CB <= HPC * S
        xcache = lambda c: RB[:, c * CB:(c + 1) * CB]
        xcb_scratch = [qsb[:, 0:CB], ksb[:, 0:CB]]
        xsq_buf = [tgb[:, 0:CB], RD[:, 0:CB]]

        dma("sync", mask_sb[:, :], mk, sCst)
        dma("sync", gbs[:, :], gb, sCst)
        dma("sync", sela[:, :], sel_a, sCst)
        dma("sync", selb[:, :], sel_b, sCst)
        cl0 = sCst.n
        dma("gpsimd", WS[0][:, 0:128], gw, sWcst)
        dma("gpsimd", WS[0][:, 128:128 + NC * 8], wr, sWcst)
        cl1 = sWcst.n
        ws.cnt[0] = 1

        vop("memset", onesD[:, :], recipD)
        vop("memset", ones1[:, :], 1.0)
        vop("memset", onesC[:, :], 1.0)
        vop("memset", ones1b[:, :], 1.0)
        vop("tensor_scalar", gbn[:, :], gbs[:, :], -1.0, None, op0=ALU.mult,
            waits=[(sCst, cl0)])
        vop("tensor_copy", gwb[:, :], WS[0][:, 0:128], waits=[(sWcst, cl1)])
        vop("tensor_copy", wrb[:, :], WS[0][:, 128:128 + NC * 8], sem=sInit)
        ws.set_free(0, [(sInit, sInit.n)])

        vw_flat0 = vw[0].rearrange("p c e -> p (c e)")
        hoisted_vconv = load_conv(
            lambda off, wdt: vwb[:, off:off + wdt],
            lambda off, wdt: vw_flat0[:, off:off + wdt],
            VG * NC * 128, [])

        p1_last = {}

        def pass1_chunk(src_ap, c, bcols, base, CBx, SWx, gate,
                        mt=None, et=None, ebase=None, cache=None,
                        cache_gate=(), sq=None):
            mt = psA if mt is None else mt
            et = psB if et is None else et
            ebase = base if ebase is None else ebase
            buf, p, lv = xs.load(src_ap[c * 128:(c + 1) * 128, bcols], CBx)
            xcb = cache if cache is not None else xcb_scratch[p]
            xsq = sq if sq is not None else xsq_buf[p]
            w_prev = ([(sStP, p1_last[p])] if p in p1_last else [])
            vop("tensor_copy", xcb[:, 0:CBx], buf[:, 0:CBx], sem=sCv,
                waits=[(sX[p], lv)] + w_prev + list(cache_gate))
            act(xsq[:, 0:CBx], buf[:, 0:CBx], AF.Square, sem=sSq,
                waits=[(sX[p], lv)] + w_prev)
            xs.set_free(p, [(sCv, sCv.n), (sSq, sSq.n)])
            cv_v, sq_v = sCv.n, sSq.n
            nrr = CBx // SWx
            for rr in range(nrr):
                first = (c == 0)
                last = (c == NC - 1)
                mm(mt[0:1, base + rr * SWx: base + (rr + 1) * SWx],
                   onesD[:, :], xcb[:, rr * SWx:(rr + 1) * SWx], first, last,
                   waits=(([(sCv, cv_v), (sInit, 1)] + (gate if c == 0 else []))
                          if rr == 0 else []))
                mm(et[0:1, ebase + rr * SWx: ebase + (rr + 1) * SWx],
                   onesD[:, :], xsq[:, rr * SWx:(rr + 1) * SWx], first, last,
                   sem=(sStP if rr == nrr - 1 else None),
                   waits=([(sSq, sq_v)] if rr == 0 else []))
            p1_last[p] = sStP.n

        def math_block(m_ap, e2_ap, st_ap, base, width, evac_gate,
                       mt=None, et=None, ebase=None):
            mt = psA if mt is None else mt
            et = psB if et is None else et
            ebase = base if ebase is None else ebase
            act(m_ap, mt[0:1, base:base + width], AF.Copy, sem=sStE,
                waits=evac_gate)
            act(e2_ap, et[0:1, ebase:ebase + width], AF.Copy, sem=sStE)
            stv = sStE.n
            act(st_ap, m_ap, AF.Square, sem=sSq, waits=[(sStE, stv)])
            sqv = sSq.n
            vop("scalar_tensor_tensor", e2_ap, e2_ap, EPS, st_ap,
                op0=ALU.add, op1=ALU.subtract, sem=sMath,
                waits=[(sSq, sqv), (sStE, stv)])
            mtv = sMath.n
            act(e2_ap, e2_ap, AF.Ln, waits=[(sMath, mtv)], sem=sAch)
            act(e2_ap, e2_ap, AF.Exp, scl=-0.5, sem=sStE,
                waits=[(sAch, sAch.n)])
            rv = sStE.n
            vop("tensor_tensor", m_ap, m_ap, e2_ap, op=ALU.mult,
                waits=[(sStE, rv)], sem=sVch)
            vop("tensor_scalar", m_ap, m_ap, -1.0, None, op0=ALU.mult,
                sem=sMath, waits=[(sVch, sVch.n)])
            return sMath.n

        def bcast_block(m_ap, e2_ap, base, width, nmr_v):
            for r in range(width // 512):
                mm(psB[:, base + r * 512: base + (r + 1) * 512], ones1[0:1, :],
                   e2_ap[0:1, r * 512:(r + 1) * 512], True, True,
                   waits=([(sMath, nmr_v)] if r == 0 else []))
                mm(psA[:, base + r * 512: base + (r + 1) * 512],
                   ones1b[0:1, :],
                   m_ap[0:1, r * 512:(r + 1) * 512], True, True, sem=sBc)
            return sBc.n

        def pass2_chunk(c, bcols, base, CBx, dst_ap, bc_v, tmp):
            if XCACHE_OK:
                vop("tensor_tensor", tmp[:, 0:CBx], xcache(c)[:, 0:CBx],
                    psB[:, base:base + CBx], op=ALU.mult,
                    waits=[(sBc, bc_v)], sem=sVch)
                vop("tensor_tensor", dst_ap, tmp[:, 0:CBx],
                    psA[:, base:base + CBx], op=ALU.add, sem=sXn,
                    waits=[(sVch, sVch.n)])
            else:
                buf, p, lv = xs.load(xT[c * 128:(c + 1) * 128, bcols], CBx)
                vop("tensor_tensor", buf[:, 0:CBx], buf[:, 0:CBx],
                    psB[:, base:base + CBx], op=ALU.mult,
                    waits=[(sX[p], lv), (sBc, bc_v)], sem=sVch)
                vop("tensor_tensor", dst_ap, buf[:, 0:CBx],
                    psA[:, base:base + CBx], op=ALU.add, sem=sXn,
                    waits=[(sVch, sVch.n)])
                xs.set_free(p, [(sXn, sXn.n)])

        xn_after_block = []
        xn_chunk = {}
        for c in range(NC):
            pass1_chunk(xT, c, slice(0, CB), 0, CB, SW, [],
                        cache=(xcache(c) if XCACHE_OK else None))
        for h in range(XH):
            base = (h % 2) * 1024
            bc = slice(h * CB, (h + 1) * CB)
            mb = m_bf[0:1, h * CB:(h + 1) * CB]
            eb = e2f[0:1, h * CB:(h + 1) * CB]
            stb = stmp[0:1, h * CB:(h + 1) * CB]
            nmr_v = math_block(mb, eb, stb, base, CB, [(sStP, sStP.n)])
            bc_v = bcast_block(mb, eb, base, CB, nmr_v)
            nb = (h + 1) % 2 * 1024
            for c in range(NC):
                pass2_chunk(c, bc, base, CB, xnT(c)[:, bc], bc_v,
                            xcb_scratch[c % 2])
                xn_chunk[(h, c)] = sXn.n
                if h + 1 < XH:
                    g = ([(sXn, xn_after_block[h - 1])] if h >= 1 else [])
                    pass1_chunk(xT, c, slice((h + 1) * CB, (h + 2) * CB),
                                nb, CB, SW, g,
                                cache=(xcache(c) if XCACHE_OK else None),
                                cache_gate=[(sXn, sXn.n)])
            xn_after_block.append(sXn.n)
        xn_all = sXn.n
        assert XH2 == 1 and S2 <= 1024
        p2_state = {}

        def emit_p2():
            for c in range(NC):
                pass1_chunk(xf, c, slice(0, CB2), 0, CB2, SW2,
                            [(sXn, xn_all)] if c == 0 else [],
                            mt=psB, et=psB, ebase=1024,
                            sq=tgb[:, (c % 2) * CB2:(c % 2 + 1) * CB2])
            p2_state['nmr2'] = math_block(
                m2_bf, e2f2, stmp[0:1, 0:S2], 0, S2,
                [(sStP, sStP.n)], mt=psB, et=psB, ebase=1024)
            p2_state['st2'] = sStE.n

        expv = {}
        gctr = [0]
        grpG = {}
        gmark = {}
        avbank_gate = {0: 0, 1: 0}
        tail_pending = []

        def flush_tail():
            while tail_pending:
                tail_pending.pop(0)()
        prev_head_psA_free = None
        VGW = VG * 128
        per_ps = max(1, 2048 // VGW)

        vconv_next = [None] * (NB + 1)
        vconv_next[0] = hoisted_vconv
        qkconv_next = [None, -1]
        for vb in range(NB):
            if vconv_next[vb] is None:
                vw_flat = vw[vb].rearrange("p c e -> p (c e)")
                vconv = load_conv(
                    lambda off, wdt: vwb[:, off:off + wdt],
                    lambda off, wdt: vw_flat[:, off:off + wdt],
                    NC * VGW,
                    ([(sVm, vb * NKT), (sQm, sQm.n), (sKm, sKm.n)]
                     if vb > 0 else []))
            else:
                vconv = vconv_next[vb]
            sve0 = sVe.n
            TBK = CB // 128
            for tt in range(NKT):
                col = (tt % 2) * 512
                w = []
                if tt == 0:
                    w += [(sWc, vconv)]
                if vb == 0 and tt % TBK == 0:
                    w.append((sXn, xn_after_block[min(tt // TBK, XH - 1)]))
                elif vb > 0 and tt == 0:
                    w.append((sXn, xn_all))
                if tt >= 2:
                    w.append((sVe, sve0 + tt - 1))
                for c in range(NC):
                    mm(psA[:, col:col + VGW],
                       xnT(c)[:, tt * 128:(tt + 1) * 128],
                       vwb[:, c * VGW:(c + 1) * VGW],
                       c == 0, c == NC - 1,
                       sem=(sVm if c == NC - 1 else None),
                       waits=(w if c == 0 else []))
                wv = [(sVm, sVm.n)]
                if vb > 0 and tt == 0:
                    wv.append((sAv, sAv.n))
                act(vsb(tt), psA[:, col:col + VGW], AF.Copy, sem=sVe, waits=wv)

            prev_head_psA_free = (sVe, sVe.n)
            if vb == 0:
                hh0 = vb * VG
                qc0 = load_conv(
                    lambda off, wdt: qwb[:, off:off + wdt],
                    lambda off, wdt, hh0=hh0: qw[hh0][:, off:off + wdt],
                    NC * 128,
                    [(sQm, sQm.n)] + [(sVm, (vb + 1) * NKT)])
                kc0 = load_conv(
                    lambda off, wdt: kwb[:, off:off + wdt],
                    lambda off, wdt, hh0=hh0: kw[hh0][:, off:off + wdt],
                    NC * 128,
                    [(sKm, sKm.n)] + [(sVm, (vb + 1) * NKT)])
                qkconv_next[0] = (qc0, kc0)
                qkconv_next[1] = hh0
                emit_p2()

            for j_in in range(VG):
                hh = vb * VG + j_in
                if qkconv_next[0] is not None and qkconv_next[1] == hh:
                    qconv, kconv = qkconv_next[0]
                    qkconv_next[0] = None
                else:
                    qconv = load_conv(
                        lambda off, wdt: qwb[:, off:off + wdt],
                        lambda off, wdt, hh=hh: qw[hh][:, off:off + wdt],
                        NC * 128,
                        [(sQm, sQm.n)] + [(sVm, (vb + 1) * NKT)])
                    kconv = load_conv(
                        lambda off, wdt: kwb[:, off:off + wdt],
                        lambda off, wdt, hh=hh: kw[hh][:, off:off + wdt],
                        NC * 128,
                        [(sKm, sKm.n)] + [(sVm, (vb + 1) * NKT)])

                for n in range(NQ):
                    w0 = []
                    if n == 0:
                        w0 += [(sWc, qconv), prev_head_psA_free]
                    if hh == 0:
                        w0.append((sXn,
                                   xn_after_block[min(n * QW // CB, XH - 1)]))
                    for c in range(NC):
                        mm(psA[:, n * QW:(n + 1) * QW],
                           qwb[:, c * 128:(c + 1) * 128],
                           xnT(c)[:, n * QW:(n + 1) * QW],
                           c == 0, c == NC - 1,
                           sem=(sQm if c == NC - 1 else None),
                           waits=(w0 if c == 0 else []))
                    act(qsb[:, n * QW:(n + 1) * QW],
                        psA[:, n * QW:(n + 1) * QW], AF.Copy, sem=sQe,
                        waits=[(sQm, sQm.n)])
                qm = sQm.n
                qe = sQe.n

                for n in range(NQ):
                    mm(psA[:, n * QW:(n + 1) * QW],
                       gwb[:, :],
                       qsb[:, n * QW:(n + 1) * QW], True, True,
                       sem=(sGm if n == NQ - 1 else None),
                       waits=([(sQe, qe), (sInit, 1)] if n == 0 else []))
                gm = sGm.n
                flush_tail()
                act(tgb[:, 0:S], psA[:, 0:S], AF.Exp, sem=sTg,
                    waits=[(sGm, gm), (sDm, sDm.n)], bias=gbn[:, :], scl=-1.0)
                tg = sTg.n

                for n in range(NQ):
                    w0 = []
                    if n == 0:
                        w0 += [(sWc, kconv), (sVe, sVe.n),
                               (sStE, p2_state['st2']),
                               (sHo, sHo.n - (1 if NQ >= 2 else 0)),
                               (sRhE, sRhE.n)]
                    if n == 2:
                        w0.append((sDm, sDm.n))
                    if n == 3:
                        w0.append((sHo, sHo.n))
                    if hh == 0:
                        w0.append((sXn,
                                   xn_after_block[min(n * QW // CB, XH - 1)]))
                    for c in range(NC):
                        mm(psB[:, n * QW:(n + 1) * QW],
                           kwb[:, c * 128:(c + 1) * 128],
                           xnT(c)[:, n * QW:(n + 1) * QW],
                           c == 0, c == NC - 1,
                           sem=(sKm if (n == NQ - 1 and c == NC - 1) else None),
                           waits=(w0 if c == 0 else []))
                km = sKm.n
                act(ksb[:, 0:S], psB[:, 0:S], AF.Copy, sem=sKe,
                    waits=[(sKm, km)])
                ke = sKe.n

                spread = []
                if j_in == VG - 1 and vb + 1 < NB:
                    vwf2 = vw[vb + 1].rearrange("p c e -> p (c e)")

                    def vdone(v, vb=vb):
                        vconv_next[vb + 1] = v
                    spread.append((
                        lambda off, wdt: vwb[:, off:off + wdt],
                        lambda off, wdt, vwf2=vwf2: vwf2[:, off:off + wdt],
                        NC * VGW,
                        [(sVm, (vb + 1) * NKT), (sQm, qm), (sKm, km)],
                        vdone))
                elif j_in + 1 < VG:
                    hn = hh + 1
                    qk_acc = {}

                    def qdone(v, qk_acc=qk_acc, hn=hn):
                        qk_acc['q'] = v
                        if 'k' in qk_acc:
                            qkconv_next[0] = (qk_acc['q'], qk_acc['k'])
                            qkconv_next[1] = hn

                    def kdone(v, qk_acc=qk_acc, hn=hn):
                        qk_acc['k'] = v
                        if 'q' in qk_acc:
                            qkconv_next[0] = (qk_acc['q'], qk_acc['k'])
                            qkconv_next[1] = hn
                    spread.append((
                        lambda off, wdt: qwb[:, off:off + wdt],
                        lambda off, wdt, hn=hn: qw[hn][:, off:off + wdt],
                        NC * 128, [(sQm, qm)], qdone))
                    spread.append((
                        lambda off, wdt: kwb[:, off:off + wdt],
                        lambda off, wdt, hn=hn: kw[hn][:, off:off + wdt],
                        NC * 128, [(sKm, km)], kdone))
                spread_pieces = []
                for dstf, srcf, tot, gates, cb_ in spread:
                    for off in range(0, tot, WSC):
                        wdt = min(WSC, tot - off)
                        last = off + wdt >= tot
                        spread_pieces.append(
                            (dstf, srcf, off, wdt, gates if off == 0 else [],
                             cb_ if last else None))

                spread_loaded = []

                def emit_spread_load(n):
                    for _ in range(n):
                        if not spread_pieces or len(spread_loaded) >= 2:
                            return
                        dstf, srcf, off, wdt, gates, cb_ = \
                            spread_pieces.pop(0)
                        buf, p, lv = ws.load(srcf(off, wdt), wdt)
                        spread_loaded.append(
                            (dstf, off, wdt, gates, cb_, buf, p, lv))

                def spread_drain():
                    while spread_pieces or spread_loaded:
                        emit_spread_conv(1)
                        emit_spread_load(1)

                def emit_spread_conv(n):
                    for _ in range(n):
                        if not spread_loaded:
                            return
                        dstf, off, wdt, gates, cb_, buf, p, lv = \
                            spread_loaded.pop(0)
                        vop("tensor_copy", dstf(off, wdt), buf[:, 0:wdt],
                            sem=sWc, waits=[(sW[p], lv)] + list(gates))
                        ws.set_free(p, [(sWc, sWc.n)])
                        if cb_ is not None:
                            cb_(sWc.n)

                npc_tot = len(spread_pieces)
                emit_spread_load((npc_tot + NQ - 1) // NQ)

                for j in range(NQ):
                    nkt_j = KPR * (j + 1)
                    ng = nkt_j // 2
                    av_col = 0 if (j % 2 == 0) else 1536
                    avps = psB[:, av_col:av_col + QW]
                    rhops = psB[0:1, 512:512 + QW]
                    rhobps = psB[:, 1024:1024 + QW]
                    ho_gate = avbank_gate[j % 2]

                    def emit_avrho(g, j=j, j_in=j_in, nkt_j=nkt_j, avps=avps,
                                   rhops=rhops, hh=hh, ho_gate=ho_gate):
                        G = grpG[(hh, j, g)]
                        base = (G % NSLOT) * 2 * QW
                        for kk in range(2):
                            kt = 2 * g + kk
                            diag = kt >= KPR * j
                            if diag:
                                gate, gv = sMk, expv[(hh, j, g, "mk", kk)]
                            else:
                                gate, gv = sEx, expv[(hh, j, g, "ex")]
                            w = [(gate, gv)]
                            if kt == 0 and ho_gate > 0:
                                w.append((sHo, ho_gate))
                            mm(avps,
                               vsb(kt)[:, j_in * 128:(j_in + 1) * 128],
                               probsT[:, base + kk * QW: base + (kk + 1) * QW],
                               kt == 0, kt == nkt_j - 1,
                               sem=(sAv if kt == nkt_j - 1 else None),
                               waits=w)
                            w2_ = [(sRhE, sRhE.n)] if kt == 0 else []
                            last = (kt == nkt_j - 1)
                            mm(rhops, onesC[:, :],
                               probsT[:, base + kk * QW: base + (kk + 1) * QW],
                               kt == 0, last,
                               sem=(sRh if last else
                                    (sPgr if kk == 1 else None)),
                               waits=w2_)
                            if kk == 1:
                                gmark[G] = ((sRh, sRh.n) if last
                                            else (sPgr, sPgr.n))

                    for g in range(ng):
                        gp = g % 2
                        G = gctr[0]
                        gctr[0] += 1
                        grpG[(hh, j, g)] = G
                        base = (G % NSLOT) * 2 * QW
                        w = []
                        if g == 0 and j == 0:
                            w += [(sTg, tg)]
                        if g >= 2:
                            w.append((sEx, expv[(hh, j, g - 2, "ex")]))
                        elif j > 0 and g < 2:
                            w.append((sEx, expv[(hh, j - 1, "last_ex")]))
                        if g == 0 and j == 0:
                            w.append((sKe, ke))
                        for kk in range(2):
                            kt = 2 * g + kk
                            mm(psA[:, gp * 1024 + kk * 512:
                                   gp * 1024 + kk * 512 + QW],
                               ksb[:, kt * 128:(kt + 1) * 128],
                               qsb[:, j * QW:(j + 1) * QW],
                               True, True,
                               sem=(sSc if kk == 1 else None),
                               waits=(w if kk == 0 else []))
                        scv = sSc.n
                        if g == 1:
                            flush_tail()
                        we = [(sSc, scv)]
                        if G >= NSLOT:
                            we.append(gmark[G - NSLOT])
                        act(probsT[:, base: base + 2 * QW],
                            psA[:, gp * 1024: gp * 1024 + 2 * QW],
                            AF.Exp, sem=sEx, waits=we,
                            bias=CSHIFT, scl=SCALE)
                        expv[(hh, j, g, "ex")] = sEx.n
                        if 2 * g + 1 >= KPR * j:
                            d0 = 2 * g - KPR * j
                            for kk2 in range(2):
                                vop("tensor_tensor",
                                    probsT[:, base + kk2 * QW:
                                           base + (kk2 + 1) * QW],
                                    probsT[:, base + kk2 * QW:
                                           base + (kk2 + 1) * QW],
                                    mask_sb[:, (d0 + kk2) * 512:
                                            (d0 + kk2 + 1) * 512],
                                    op=ALU.mult, sem=sMk,
                                    waits=[(sEx, sEx.n)] if kk2 == 0 else [])
                                expv[(hh, j, g, "mk", kk2)] = sMk.n
                        if g >= 1:
                            emit_avrho(g - 1)
                    expv[(hh, j, "last_ex")] = sEx.n
                    emit_avrho(ng - 1)
                    avv = sAv.n
                    rhv = sRh.n
                    act(rho_sb[0:1, 0:QW], rhops, AF.Copy, sem=sRhE,
                        waits=[(sRh, rhv)])
                    rhev = sRhE.n

                    def emit_tail(j=j, hh=hh, rhev=rhev, avv=avv, tg=tg,
                                  avps=avps, rhobps=rhobps):
                        mm(rhobps, ones1[0:1, :], rho_sb[0:1, 0:QW],
                           True, True, sem=sRhB,
                           waits=[(sRhE, rhev), (sDm, sDm.n)])
                        rhbv = sRhB.n
                        vop("scalar_tensor_tensor", Dsb[:, 0:QW],
                            tgb[:, j * QW:(j + 1) * QW], 1.0, rhobps,
                            op0=ALU.add, op1=ALU.mult, sem=sDm,
                            waits=[(sRhB, rhbv), (sTg, tg), (sDi, sDi.n)])
                        dmv = sDm.n
                        act(Dsb[:, 0:QW], Dsb[:, 0:QW], AF.Ln,
                            waits=[(sDm, dmv), (sHo, sHo.n)], sem=sAch)
                        act(Dinv[:, 0:QW], Dsb[:, 0:QW], AF.Exp, scl=-1.0,
                            sem=sDi, waits=[(sAch, sAch.n)])
                        div = sDi.n
                        vop("tensor_tensor", hob(hh)[:, j * QW:(j + 1) * QW],
                            avps, Dinv[:, 0:QW], op=ALU.mult, sem=sHo,
                            waits=[(sDi, div), (sAv, avv)])
                        avbank_gate[j % 2] = sHo.n
                    tail_pending.append(emit_tail)
                    nu = len(pconv_units)
                    rdone = hh * NQ + j + 1
                    emit_pconv((nu * rdone) // (HPC * NQ)
                               - (nu * (rdone - 1)) // (HPC * NQ))
                    if j < NQ - 1:
                        cv_t = (npc_tot * (j + 1) + NQ - 1) // NQ
                        ld_t = min(npc_tot,
                                   (npc_tot * (j + 2) + NQ - 1) // NQ)
                        done_cv = npc_tot - len(spread_pieces) \
                            - len(spread_loaded)
                        emit_spread_conv(cv_t - done_cv)
                        done_ld = npc_tot - len(spread_pieces)
                        emit_spread_load(ld_t - done_ld)
                    else:
                        spread_drain()
                spread_drain()
                prev_head_psA_free = (sEx, expv[(hh, NQ - 1, "last_ex")])
        flush_tail()
        ho_all = sHo.n
        qm_all, km_all, vm_all = sQm.n, sKm.n, sVm.n

        for n in range(NST2):
            mm(psB[:, n * SW2:(n + 1) * SW2], ones1[0:1, :],
               e2f2[0:1, n * SW2:(n + 1) * SW2], True, True,
               waits=([(sMath, p2_state['nmr2']), (sHo, ho_all), (sDm, sDm.n)]
                      if n == 0 else []))
            mm(psA[:, n * SW2:(n + 1) * SW2], ones1b[0:1, :],
               m2_bf[0:1, n * SW2:(n + 1) * SW2], True, True, sem=sBc,
               waits=([(sEx, expv[(HPC - 1, NQ - 1, "last_ex")])]
                      if n == 0 else []))
        bc2 = sBc.n
        act(r2b_sb[:, 0:S2], psB[:, 0:S2], AF.Copy, sem=sM2c,
            waits=[(sBc, bc2)])
        act(nm2b_sb[:, 0:S2], psA[:, 0:S2], AF.Copy, sem=sM2c)
        m2c = sM2c.n
        for c in range(NC):
            for h in range(XH2):
                buf, p, lv = xs.load(
                    xf[c * 128:(c + 1) * 128, h * CB2:(h + 1) * CB2], CB2)
                cols = slice(h * CB2, (h + 1) * CB2)
                vop("tensor_tensor", buf[:, 0:CB2], buf[:, 0:CB2],
                    r2b_sb[:, cols], op=ALU.mult,
                    waits=[(sX[p], lv), (sM2c, m2c)], sem=sVch)
                vop("tensor_tensor", xnF(c)[:, cols], buf[:, 0:CB2],
                    nm2b_sb[:, cols], op=ALU.add, sem=sXn2,
                    waits=[(sVch, sVch.n)] +
                          ([(sQm, qm_all), (sKm, km_all), (sVm, vm_all)]
                           if (c == 0 and h == 0) else []))
                xs.set_free(p, [(sXn2, sXn2.n)])
        xn2_all = sXn2.n

        G8 = min(4, QT)
        QH = QT // G8
        opev_done = {}
        store_done = {}
        oper_done = {}

        ow_segs = {0: [(qsb, S), (ksb, S)],
                   1: [(tgb, S), (rssb[0], D2), (rssb[1], D2)]}

        def seg_place(segs, off):
            for t, w_ in segs:
                if off < w_:
                    return t, off
                off -= w_
            raise AssertionError("owq overflow")

        def segs_ok(segs):
            off = 0
            for hh_ in range(HPC):
                try:
                    t, o = seg_place(segs, hh_ * EW)
                except AssertionError:
                    return False
                if o + EW > dict((id(a), b) for a, b in segs)[id(t)]:
                    return False
            return True

        ow_dbuf = segs_ok(ow_segs[1]) and NER >= 2

        def owq(par, hh):
            t, o = seg_place(ow_segs[par], hh * EW)
            return t[:, o:o + EW]

        op_store_q = []

        def flush_op_stores():
            while op_store_q:
                op_store_q.pop(0)()

        def grp_nstores(er):
            n = 0
            e0 = er * EW
            for r in range(2):
                lo = max(e0, r * D2)
                hi = min(e0 + EW, (r + 1) * D2)
                if lo < hi:
                    n += 1
            return n * G8

        store_plan = {}
        _cnt = [0, 0]
        for _er in range(NER):
            for _qh in range(QH):
                _g = _er * QH + _qh
                _cnt[_g % 2] += 16 * grp_nstores(_er)
                store_plan[_g] = _cnt[_g % 2]

        for er in range(NER):
            par = (er % 2) if ow_dbuf else 0
            wv = []
            if er < (2 if ow_dbuf else 1):
                wv += [(sSc, sSc.n)] + pconv_done_gate()
                if par == 1:
                    wv += [(sDm, sDm.n), (sBc, sBc.n)]
            else:
                wv.append((sOp, oper_done[er - (2 if ow_dbuf else 1)]))
            off = 0
            first_dma = True
            while off < HPC * EW:
                t, o = seg_place(ow_segs[par], off)
                run = EW
                while (off + run < HPC * EW):
                    t2, o2 = seg_place(ow_segs[par], off + run)
                    if t2 is t and o2 == o + run:
                        run += EW
                    else:
                        break
                dma("gpsimd", t[:, o:o + run],
                    owbf[er][:, off:off + run],
                    sOwL[par], waits=(wv if first_dma else []))
                first_dma = False
                off += run
            owlv = sOwL[par].n
            flush_op_stores()
            for qh in range(QH):
                grp = er * QH + qh
                for hhh in range(HPC):
                    for q8 in range(G8):
                        qt = qh * G8 + q8
                        ps = psA if grp % 2 == 0 else psB
                        col = q8 * 512
                        w = []
                        if q8 == 0 and hhh == 0:
                            w.append((sOwL[par], owlv))
                            if grp == 0:
                                w += [(sHo, ho_all), (sM2c, m2c)]
                            elif grp == 1:
                                w.append((sHo, ho_all))
                            else:
                                w.append((sOpE, opev_done[grp - 2]))
                        mm(ps[:, col:col + EW],
                           hob(hhh)[:, qt * 128:(qt + 1) * 128],
                           owq(par, hhh),
                           hhh == 0, hhh == HPC - 1,
                           sem=(sOp if (hhh == HPC - 1 and q8 == G8 - 1)
                                else None),
                           waits=w)
                opv = sOp.n
                ps = psA if grp % 2 == 0 else psB
                wv2 = [(sOp, opv)]
                if grp >= 2:
                    wv2.append((sOpSp[grp % 2], store_plan[grp - 2]))
                act(opst[grp % 2][:, 0:G8 * EW], ps[:, 0:G8 * EW], AF.Copy,
                    sem=sOpE, waits=wv2)
                opev = sOpE.n

                def emit_stores(qh=qh, er=er, grp=grp, opev=opev):
                    first_store = True
                    for q8 in range(G8):
                        qt = qh * G8 + q8
                        e0 = er * EW
                        for r in range(2):
                            lo = max(e0, r * D2)
                            hi = min(e0 + EW, (r + 1) * D2)
                            if lo >= hi:
                                continue
                            dma("gpsimd",
                                op_dram[r * S + qt * 128:
                                        r * S + (qt + 1) * 128,
                                        lo - r * D2:hi - r * D2],
                                opst[grp % 2][:, q8 * EW + lo - e0:
                                              q8 * EW + hi - e0],
                                sOpSp[grp % 2],
                                waits=([(sOpE, opev)] if first_store else []))
                            first_store = False
                    store_done[grp] = sOpSp[grp % 2].n
                op_store_q.append(emit_stores)
                opev_done[grp] = sOpE.n
            oper_done[er] = sOp.n
        flush_op_stores()
        collective("ReduceScatter", ALU.add,
                   [[0, 1], [2, 3], [4, 5], [6, 7]],
                   op_dram.ap().opt(), rs_dram.ap().opt(), sRS,
                   waits=[(sOpSp[0], sOpSp[0].n), (sOpSp[1], sOpSp[1].n)])

        opev_all = sOpE.n
        for tt in range(TT2):
            bank = tt % 8
            ps = psA if bank < 4 else psB
            col = (bank % 4) * 512
            w = [(sXn2, xn2_all)]
            if tt == 0:
                w.append((sOpE, opev_all))
            if tt >= 8:
                w.append((sRtE, sRtE.n - 7))
            for c in range(NC):
                mm(ps[:, col:col + 8],
                   xnF(c)[:, tt * 128:(tt + 1) * 128],
                   wrb[:, c * 8:(c + 1) * 8],
                   c == 0, c == NC - 1,
                   sem=(sRt if c == NC - 1 else None),
                   waits=(w if c == 0 else []))
            rtv = sRt.n
            act(rt_sb[:, tt * 8:(tt + 1) * 8], ps[:, col:col + 8],
                AF.Copy, sem=sRtE, waits=[(sRt, rtv)])
        rte = sRtE.n
        act(rt_sb[:, :], rt_sb[:, :], AF.Exp, sem=sRtE, waits=[(sRtE, rte)])
        rtexp = sRtE.n
        rt3 = rt_sb[:, :].rearrange("p (t e) -> p t e", e=8)
        lo3 = r_lo[:, :].rearrange("p (t e) -> p t e", e=8)
        kp3 = r_kp[:, :].rearrange("p (t e) -> p t e", e=8)
        g3 = g_sb[:, :].rearrange("p (t e) -> p t e", e=8)
        m1b3 = r_m1[:, :].rearrange("p (t o) -> p t o", o=1).broadcast_to(
            [128, TT2, 8])
        m2b3 = r_m2[:, :].rearrange("p (t o) -> p t o", o=1).broadcast_to(
            [128, TT2, 8])
        dnb3 = r_dn[:, :].rearrange("p (t o) -> p t o", o=1).broadcast_to(
            [128, TT2, 8])
        vop("tensor_reduce", r_m1[:, :], rt3, AX.X, ALU.max, sem=sTp,
            waits=[(sRtE, rtexp)])
        vop("tensor_tensor", lo3, rt3, m1b3, op=ALU.is_lt, sem=sTp,
            waits=[(sTp, sTp.n)])
        vop("tensor_tensor", lo3, rt3, lo3, op=ALU.mult, sem=sTp,
            waits=[(sTp, sTp.n)])
        vop("tensor_reduce", r_m2[:, :], lo3, AX.X, ALU.max, sem=sTp,
            waits=[(sTp, sTp.n)])
        vop("tensor_tensor", kp3, rt3, m2b3, op=ALU.is_ge, sem=sTp,
            waits=[(sTp, sTp.n)])
        vop("tensor_tensor", r_dn[:, :], r_m1[:, :], r_m2[:, :], op=ALU.add,
            sem=sTp, waits=[(sTp, sTp.n)])
        vop("reciprocal", r_dn[:, :], r_dn[:, :], sem=sTp,
            waits=[(sTp, sTp.n)])
        vop("tensor_tensor", kp3, kp3, dnb3, op=ALU.mult, sem=sTp,
            waits=[(sTp, sTp.n)])
        vop("tensor_tensor", g3, rt3, kp3, op=ALU.mult, sem=sTp,
            waits=[(sTp, sTp.n)])
        gdone = sTp.n

        finale_stores = []
        st_totals = {}
        n_done = 0

        def emit_finale_tile(qt):
            nonlocal n_done
            n = n_done
            pr = n % 3
            wrs = [(sRS, 1)] if n == 0 else []
            if n >= 2:
                wrs.append((sFin, sFin.n - 1))
            dma("sync", rssb[n % 2][:, 0:D2],
                rs_dram[qt * 128:(qt + 1) * 128, :], sRsL[n % 2], waits=wrs)
            rsv = sRsL[n % 2].n
            buf, p, lv = xs.load(xr[qt * 128:(qt + 1) * 128, :], D2)
            wadd = [(sX[p], lv), (sRsL[n % 2], rsv)]
            if n >= 3:
                wadd.append((sSt[pr], 16 * (n // 3)))
            vop("tensor_tensor", outsb[pr][:, 0:D2], buf[:, 0:D2],
                rssb[n % 2][:, 0:D2], op=ALU.add, sem=sFin, waits=wadd)
            xs.set_free(p, [(sFin, sFin.n)])
            fv = sFin.n

            def st(qt=qt, pr=pr, fv=fv):
                dma("gpsimd", out[qt * 128:(qt + 1) * 128, :],
                    outsb[pr][:, 0:D2], sSt[pr],
                    waits=[(sFin, fv), (sSt[pr], sSt[pr].n)])
                st_totals[pr] = sSt[pr].n
            finale_stores.append(st)
            n_done += 1

        def flush_finale_stores(k=None):
            while finale_stores and (k is None or k > 0):
                finale_stores.pop(0)()
                if k is not None:
                    k -= 1

        def emit_finale_tail():
            n = n_done
            pr = n % 3
            dma("sync", rssb[n % 2][:, 0:D2], rs_dram[0:128, :],
                sRsL[n % 2], waits=[(sFin, sFin.n - 1)])
            rsv = sRsL[n % 2].n
            buf, p, lv = xs.load(xr[0:128, :], D2)
            wadd = [(sX[p], lv), (sRsL[n % 2], rsv),
                    (sSt[pr], 16 * (n // 3))]
            vop("tensor_tensor", outsb[pr][:, 0:D2], buf[:, 0:D2],
                rssb[n % 2][:, 0:D2], op=ALU.add, sem=sFin, waits=wadd)
            xs.set_free(p, [(sFin, sFin.n)])
            dma("gpsimd", o8sb[0:8, :], o8r_dram[:, :], sArL,
                waits=[(sAR, 1)])
            vop("tensor_scalar", fad[0:8, :], o8sb[0:8, D2:2 * D2],
                selb[0:8, :], None, op0=ALU.mult,
                waits=[(sArL, sArL.n), (sCst, cl0),
                       (sSt[2], st_totals.get(2, 0))], sem=sVch)
            vop("scalar_tensor_tensor", fad[0:8, :], o8sb[0:8, 0:D2],
                sela[0:8, :], fad[0:8, :], op0=ALU.mult, op1=ALU.add,
                sem=sVch, waits=[(sVch, sVch.n)])
            vop("tensor_tensor", outsb[pr][0:8, 0:D2],
                outsb[pr][0:8, 0:D2], fad[0:8, :], op=ALU.add, sem=sFin,
                waits=[(sVch, sVch.n)])
            fv = sFin.n
            dma("gpsimd", out[0:128, :], outsb[pr][:, 0:D2], sSt[pr],
                waits=[(sFin, fv)])
            for s_ in sSt:
                wt("gpsimd", s_, s_.n)


        def ffn_layer(wsrc, rhs_fn, dst_fn, sMM, sEV, gate_first, nch,
                      conv_gate=()):
            mm0 = sMM.n
            ev0 = sEV.n
            for it in range(NIT):
                wcw = []
                if it >= 2:
                    wcw.append((sMM, mm0 + it - 1))
                elif conv_gate:
                    wcw.append((conv_gate[0], conv_gate[1] - 1 + it))
                wcv = load_conv(
                    lambda off, wdt, it=it: w13b[it % 2][:, off:off + wdt],
                    lambda off, wdt, it=it: wsrc[it][:, off:off + wdt],
                    nch * 128, wcw)
                ps = psA if it % 2 == 0 else psB
                w = [(sWc, wcv)] + (list(gate_first) if it == 0 else [])
                if it >= 2:
                    w.append((sEV, ev0 + it - 1))
                for n in range(NF):
                    for c in range(nch):
                        mm(ps[:, n * FW:(n + 1) * FW],
                           w13b[it % 2][:, c * 128:(c + 1) * 128],
                           rhs_fn(c)[:, n * FW:(n + 1) * FW],
                           c == 0, c == nch - 1,
                           sem=(sMM if (n == NF - 1 and c == nch - 1) else None),
                           waits=(w if (n == 0 and c == 0) else []))
                mmv = sMM.n
                act(dst_fn(it)[:, 0:S2], ps[:, 0:S2], AF.Silu, sem=sEV,
                    waits=[(sMM, mmv)])

        h1mm0 = sH1.n
        ffn_layer(w1, xnF, h1sb, sH1, sH1E, [(sRtE, rte)], NC)
        h1e = sH1E.n
        ffn_layer(w3, h1sb, h2sb, sH2, sH2E, [(sH1E, h1e)], NCI,
                  conv_gate=(sH1, h1mm0 + NIT))
        h2e = sH2E.n

        for qt in range(1, QT):
            emit_finale_tile(qt)

        h3mm = {}
        for er in range(NERH):
            wbuf = RD if er % 2 == 0 else vwb
            w2c = lambda ci, wbuf=wbuf: wbuf[:, ci * EWH:(ci + 1) * EWH]
            w = []
            if er < 2:
                w += [(sHo, ho_all), (sQm, qm_all), (sKm, km_all),
                      (sVm, vm_all)] + pconv_done_gate()
            else:
                w.append((sH3, h3mm[er - 2]))
            dma("gpsimd", wbuf[:, 0:NCI * EWH], w2bf[er],
                sW2L[er % 2], waits=w)
            w2lv = sW2L[er % 2].n
            if finale_stores:
                flush_finale_stores(-(-len(finale_stores) // (NERH - er)))
            for tt in range(TT2):
                ps = psA if tt % 2 == 0 else psB
                w = [(sW2L[er % 2], w2lv)]
                if er == 0 and tt == 0:
                    w.append((sH2E, h2e))
                if tt >= 2 or er > 0:
                    w.append((sH3E, sH3E.n - 1))
                for ci in range(NCI):
                    mm(ps[:, 0:EWH],
                       h2sb(ci)[:, tt * 128:(tt + 1) * 128],
                       w2c(ci), ci == 0, ci == NCI - 1,
                       sem=(sH3 if ci == NCI - 1 else None),
                       waits=(w if ci == 0 else []))
                h3v = sH3.n
                wv = [(sH3, h3v)]
                if er == 0 and tt == 0:
                    wv.append((sH2, sH2.n))
                act(h3sb(tt)[:, er * EWH:(er + 1) * EWH],
                    ps[:, 0:EWH], AF.Copy, sem=sH3E,
                    waits=wv)
            h3mm[er] = sH3.n
        h3e = sH3E.n

        for er in range(NERH):
            ps = psA if er < 4 else psB
            col = (er % 4) * 512
            for tt in range(TT2):
                mm(ps[0:8, col:col + EWH],
                   g_sb[:, tt * 8:(tt + 1) * 8],
                   h3sb(tt)[:, er * EWH:(er + 1) * EWH],
                   tt == 0, tt == TT2 - 1,
                   sem=(sO8 if tt == TT2 - 1 else None),
                   waits=([(sH3E, h3e), (sTp, gdone)]
                          if (er == 0 and tt == 0) else []))
        o8v = sO8.n
        for er in range(NERH):
            ps = psA if er < 4 else psB
            col = (er % 4) * 512
            act(o8sb[0:8, er * EWH:(er + 1) * EWH], ps[0:8, col:col + EWH],
                AF.Copy, sem=sO8E,
                waits=([(sO8, o8v)] if er == 0 else []))
        dma("gpsimd", o8_dram[:, :], o8sb[0:8, :], sO8S,
            waits=[(sO8E, sO8E.n)])
        collective("AllReduce", ALU.add, [[0, 1, 2, 3, 4, 5, 6, 7]],
                   o8_dram.ap().opt(), o8r_dram.ap().opt(), sAR,
                   waits=[(sO8S, sO8S.n)])
        flush_finale_stores()

        emit_finale_tail()

        block = es.enter_context(nc.Block())

        @block.sync
        def _(e):
            for fn in prog["sync"]:
                fn(e)

        @block.gpsimd
        def _(e):
            for fn in prog["gpsimd"]:
                fn(e)

        @block.tensor
        def _(e):
            for fn in prog["tensor"]:
                fn(e)

        @block.vector
        def _(e):
            for fn in prog["vector"]:
                fn(e)

        @block.scalar
        def _(e):
            for fn in prog["scalar"]:
                fn(e)

    return nc


def _block_w(wT_sl, n_out_blk, out_blk):
    Dd = wT_sl.shape[0]
    ncc = Dd // 128
    a = wT_sl.reshape(ncc, 128, n_out_blk, out_blk)
    return np.ascontiguousarray(a.transpose(2, 1, 0, 3).reshape(
        n_out_blk, 128, ncc * out_blk))


def _mask_const():
    m = np.zeros((128, 4 * 512), dtype=np.float32)
    kk = np.arange(128)[:, None]
    qq = np.arange(512)[None, :]
    for d in range(4):
        m[:, d * 512:(d + 1) * 512] = (kk <= (qq - 128 * d))
    return m.astype(bf16)


def make_inputs(x, q_w, k_w, v_w, o_w, gate_w, gate_b, w1_, w3_, w2_, wr_,
                S, D, H, INTER):
    HPC = H // 2
    NC = D // 128
    NIT = INTER // 128
    NCI = INTER // 128
    VG = min(2, HPC)
    NB = HPC // VG
    D2 = D // 2
    NER = max(1, D // 512)
    EW = D // NER
    EWH = min(256, D)
    NERH = D // EWH
    S2 = S // 2

    qwT = np.ascontiguousarray(q_w.T)
    kwT = np.ascontiguousarray(k_w.T)
    vwT = np.ascontiguousarray(v_w.T)
    owT = np.ascontiguousarray(o_w.T)
    w1T = np.ascontiguousarray(w1_.T)
    w3T = np.ascontiguousarray(w3_.T)
    w2T = np.ascontiguousarray(w2_.T)
    wrT = np.ascontiguousarray(wr_.T)
    gwT = np.ascontiguousarray(gate_w.T)
    mask = _mask_const()

    in_maps = []
    for core in range(8):
        b, g = core // 2, core % 2
        hw = HPC * 128
        d = {
            "xT": np.ascontiguousarray(x[b].T),
            "xf": np.ascontiguousarray(x[b, g * S2:(g + 1) * S2, :].T),
            "xr": np.ascontiguousarray(x[b][:, g * D2:(g + 1) * D2]),
            "qw": _block_w(qwT[:, g * hw:(g + 1) * hw], HPC, 128),
            "kw": _block_w(kwT[:, g * hw:(g + 1) * hw], HPC, 128),
            "vw": _block_w(vwT[:, g * hw:(g + 1) * hw], NB, VG * 128)
                  .reshape(NB, 128, NC, VG * 128),
            "gw": gwT,
            "gb": np.ascontiguousarray(gate_b.reshape(128, 1)),
            "ow": np.ascontiguousarray(owT[g * hw:(g + 1) * hw, :]),
            "w1": _block_w(w1T, NIT, 128),
            "w3": _block_w(w3T, NIT, 128),
            "w2": _block_w(w2T, NERH, EWH).reshape(NERH, 128, NCI, EWH),
            "wr": np.ascontiguousarray(
                wrT.reshape(NC, 128, 8).transpose(1, 0, 2)
                .reshape(128, NC * 8)),
            "mk": mask,
            "sel_a": np.full((8, 1), 1.0 if core == 0 else 0.0, np.float32),
            "sel_b": np.full((8, 1), 1.0 if core == 1 else 0.0, np.float32),
        }
        in_maps.append(d)
    return in_maps


def kernel(L0_attn_x, L0_attn_q_w, L0_attn_k_w, L0_attn_v_w, L0_attn_o_w,
           L0_attn_gate_w, L0_attn_gate_b, L0_ffn_w1, L0_ffn_w3, L0_ffn_w2,
           L0_ffn_wr):
    S, D, H, INTER = FULL["S"], FULL["D"], FULL["H"], FULL["INTER"]
    x = np.asarray(L0_attn_x, np.float32)
    in_maps = make_inputs(
        x, np.asarray(L0_attn_q_w, np.float32),
        np.asarray(L0_attn_k_w, np.float32),
        np.asarray(L0_attn_v_w, np.float32),
        np.asarray(L0_attn_o_w, np.float32),
        np.asarray(L0_attn_gate_w, np.float32),
        np.asarray(L0_attn_gate_b, np.float32),
        np.asarray(L0_ffn_w1, np.float32),
        np.asarray(L0_ffn_w3, np.float32),
        np.asarray(L0_ffn_w2, np.float32),
        np.asarray(L0_ffn_wr, np.float32),
        S, D, H, INTER)
    nc = build_program(S, D, H, INTER)
    res = run_bass_kernel_spmd(nc, in_maps, core_ids=list(range(8))).results
    D2 = D // 2
    full = np.empty((4, S, D), np.float32)
    for core in range(8):
        b, g = core // 2, core % 2
        full[b, :, g * D2:(g + 1) * D2] = res[core]["out"]
    return full
